# version 04
# baseline (speedup 1.0000x reference)
import concourse.bass as bass
import concourse.mybir as mybir

ENGINES = ("tensor", "vector", "scalar", "gpsimd", "sync")


def _tname(x):
    if isinstance(x, str):
        return x
    t = getattr(x, "tensor", None)
    if t is not None:
        return t.name
    return x.name


class _Op:
    __slots__ = ("idx", "engine", "emit", "reads", "writes", "dma", "deps",
                 "needs_inc", "rank", "label", "private", "switch_n")

    def __init__(self, idx, engine, emit, reads, writes, dma, label,
                 private=False, switch_n=0):
        self.idx = idx
        self.engine = engine
        self.emit = emit
        self.reads = reads
        self.writes = writes
        self.dma = dma
        self.deps = set()
        self.needs_inc = False
        self.rank = None
        self.label = label
        self.private = private
        self.switch_n = switch_n


class Prog:
    def __init__(self, nc):
        self.nc = nc
        self.ops = []
        self.state = {}
        self.alias = {}
        self.psum_names = set()

    def register_psum(self, tensors):
        for t in tensors:
            self.psum_names.add(_tname(t))

    def add_alias(self, virt, base):
        self.alias[_tname(virt)] = _tname(base)

    def _norm(self, u):
        if isinstance(u, tuple):
            x, tag = u
        else:
            x, tag = u, None
        n = _tname(x)
        n = self.alias.get(n, n)
        return (n, tag)

    def _recs(self, name):
        return self.state.setdefault(name, {})

    def _dep_read(self, op, name, tag):
        recs = self._recs(name)
        if tag is None:
            for t, rec in recs.items():
                if rec["w"] is not None:
                    op.deps.add(rec["w"])
        else:
            for t in (tag, None):
                rec = recs.get(t)
                if rec and rec["w"] is not None:
                    op.deps.add(rec["w"])

    def _dep_write(self, op, name, tag):
        recs = self._recs(name)
        if tag is None:
            for t, rec in recs.items():
                if rec["w"] is not None:
                    op.deps.add(rec["w"])
                op.deps.update(rec["r"])
        else:
            for t in (tag, None):
                rec = recs.get(t)
                if rec:
                    if rec["w"] is not None:
                        op.deps.add(rec["w"])
                    op.deps.update(rec["r"])

    def _mark_read(self, idx, name, tag):
        rec = self._recs(name).setdefault(tag, {"w": None, "r": set()})
        rec["r"].add(idx)

    def _mark_write(self, idx, name, tag):
        recs = self._recs(name)
        if tag is None:
            recs.clear()
        recs[tag] = {"w": idx, "r": set()}

    def op(self, engine, emit, reads=(), writes=(), dma=False, label="",
           private=False, switch_n=0):
        idx = len(self.ops)
        nreads = [self._norm(u) for u in reads]
        nwrites = [self._norm(u) for u in writes]
        promoted = [u for u in nreads if u[0] in self.psum_names]
        if promoted:
            nreads = [u for u in nreads if u[0] not in self.psum_names]
            nwrites = nwrites + promoted
        o = _Op(idx, engine, emit, nreads, nwrites, dma, label,
                private=private, switch_n=switch_n)
        for n, t in o.reads:
            self._dep_read(o, n, t)
        for n, t in o.writes:
            self._dep_write(o, n, t)
        o.deps.discard(idx)
        if engine == "tensor":
            o.deps = {d for d in o.deps
                      if not (self.ops[d].engine == "tensor" and not self.ops[d].dma)}
        for n, t in o.reads:
            self._mark_read(idx, n, t)
        for n, t in o.writes:
            self._mark_write(idx, n, t)
        self.ops.append(o)
        return idx

    def barrier_unit(self, name):
        return (name, None)

    NSEM = 8

    def finalize_and_emit(self):
        nc = self.nc
        R = self.NSEM
        for o in self.ops:
            for d in o.deps:
                self.ops[d].needs_inc = True
        eng_count = {e: 0 for e in ENGINES}
        dma_count = {e: 0 for e in ENGINES}
        self._psems = {}
        for o in self.ops:
            if o.private:
                o.rank = None
                o.needs_inc = True
            elif o.switch_n:
                slots = []
                for _ in range(o.switch_n):
                    j = dma_count[o.engine]
                    dma_count[o.engine] += 1
                    slots.append((j % R, j // R + 1))
                o.rank = slots
                o.needs_inc = True
            elif o.dma:
                j = dma_count[o.engine]
                dma_count[o.engine] += 1
                o.rank = (j % R, j // R + 1)
                o.needs_inc = True
            elif o.needs_inc:
                eng_count[o.engine] += 1
                o.rank = eng_count[o.engine]
        self._esem = {}
        self._qsem = {}
        import contextlib
        self._stack = contextlib.ExitStack()
        for o in self.ops:
            if o.private:
                self._psems[o.idx] = self._stack.enter_context(
                    nc.semaphore(f"sp_{o.idx}"))
        for e in ENGINES:
            if eng_count[e]:
                self._esem[e] = self._stack.enter_context(nc.semaphore(f"se_{e}"))
            if dma_count[e]:
                self._qsem[e] = [
                    self._stack.enter_context(nc.semaphore(f"sq_{e}{i}"))
                    for i in range(min(R, dma_count[e]))
                ]
        ops_by_engine = {e: [o for o in self.ops if o.engine == e] for e in ENGINES}

        block_cm = nc.Block()
        block = block_cm.__enter__()

        def emit_stream(eng_name, eng):
            hwm = {}
            def do_wait(key, val):
                if hwm.get(key, 0) >= val:
                    return
                kind = key[0]
                if kind == "e":
                    sem = self._esem[key[1]]
                elif kind == "p":
                    sem = self._psems[key[1]]
                else:
                    sem = self._qsem[key[1]][key[2]]
                eng.wait_ge(sem, val)
                hwm[key] = val
            for o in ops_by_engine[eng_name]:
                waits = {}
                for d in o.deps:
                    dop = self.ops[d]
                    if (dop.engine == eng_name and not dop.dma and not dop.private
                            and not dop.switch_n and eng_name == "tensor"):
                        continue
                    if dop.private:
                        kvs = [(("p", d), 1)]
                    elif dop.switch_n:
                        kvs = [(("q", dop.engine, s), 16 * k) for s, k in dop.rank]
                    elif dop.dma:
                        slot, k = dop.rank
                        kvs = [(("q", dop.engine, slot), 16 * k)]
                    else:
                        kvs = [(("e", dop.engine), dop.rank)]
                    for key, val in kvs:
                        if val > waits.get(key, 0):
                            waits[key] = val
                if o.dma and not o.switch_n and not o.private:
                    slot, k = o.rank
                    if k > 1:
                        key = ("q", o.engine, slot)
                        v = 16 * (k - 1)
                        if v > waits.get(key, 0):
                            waits[key] = v
                if o.switch_n:
                    for slot, k in o.rank:
                        if k > 1:
                            key = ("q", o.engine, slot)
                            v = 16 * (k - 1)
                            if v > waits.get(key, 0):
                                waits[key] = v
                for key in sorted(waits):
                    do_wait(key, waits[key])
                if o.private:
                    inst = o.emit(eng)
                    inst.then_inc(self._psems[o.idx], 1)
                elif o.switch_n:
                    slots = list(o.rank)
                    qs = self._qsem[o.engine]
                    if o.label.endswith("_nosw"):
                        it = iter(slots)
                        def inc(inst, _it=it, _qs=qs):
                            s, k = next(_it)
                            inst.then_inc(_qs[s], 16)
                        o.emit(eng, 0, inc)
                    else:
                        for arm in eng.switch_core_id(n=512):
                            me = int(arm.logical) % 8
                            if int(arm.logical) < 8:
                                it = iter(slots)
                                def inc(inst, _it=it, _qs=qs):
                                    s, k = next(_it)
                                    inst.then_inc(_qs[s], 16)
                                o.emit(eng, me, inc)
                elif o.dma:
                    slot, k = o.rank
                    inst = o.emit(eng)
                    inst.then_inc(self._qsem[o.engine][slot], 16)
                else:
                    inst = o.emit(eng)
                    if o.needs_inc:
                        inst.then_inc(self._esem[o.engine], 1)

        for e in ENGINES:
            if not ops_by_engine[e]:
                continue
            dec = getattr(block, e)
            def body(eng, _e=e):
                emit_stream(_e, eng)
            dec(body)
        block_cm.__exit__(None, None, None)

    def final_wait(self, engine="sync"):
        pass


import numpy as np
import ml_dtypes
import concourse.mybir as mybir
import os
DMAT = os.environ.get("DMAT", "1") == "1"

F32 = mybir.dt.float32
BF16 = mybir.dt.bfloat16
AF = mybir.ActivationFunctionType
ALU = mybir.AluOpType
bfd = ml_dtypes.bfloat16

D = 128
HID = 4096
DC = HID // 128
NQ = 16
NKV = 4
GRP = 4
C = 128
SCALE = float(1.0 / np.sqrt(np.float32(D)))
NEG = -1e9


def host_constants():
    tri01 = np.triu(np.ones((C, C), np.float32))
    tt = np.arange(C)[None, :]
    cons = {}
    cons["triUs"] = np.ascontiguousarray(tri01 * 0.0625)
    cons["oness"] = np.full((C, C), 0.0625, np.float32)
    cons["sixcol"] = np.full((C, 1), 0.0625, np.float32)
    cons["trimask"] = np.ascontiguousarray(np.tile(tri01, (1, 4)))
    kkA = np.arange(64)[:, None]
    mA_full = np.full((C, C), NEG, np.float32)
    mA_full[64:128, :] = np.where(kkA >= tt, 0.0, NEG)
    cons["maskA"] = np.ascontiguousarray(np.tile(mA_full, (1, 4)))
    kkB = np.arange(C)[:, None]
    mB = np.where((kkB <= tt) & (kkB >= tt - 64), 0.0, NEG).astype(np.float32)
    cons["maskB"] = np.ascontiguousarray(np.tile(mB, (1, 4)))
    M = np.zeros((C, C), np.float32)
    for p in range(64):
        M[p + 64, p] = -1.0
        M[p, p + 64] = 1.0
    cons["mrot"] = M
    cons["idn"] = np.eye(C, dtype=np.float32)
    cons["twocol"] = np.full((C, 1), 2.0, np.float32)
    cons["epscol"] = np.full((C, 1), 1e-5, np.float32)
    return cons


def rope_tables_fm(T):
    half = np.arange(D // 2, dtype=np.float32)
    inv = 1.0 / 10000.0 ** (2.0 * half / D)
    fr = np.arange(T, dtype=np.float32)[:, None] * inv[None, :]
    emb = np.concatenate([fr, fr], axis=-1)
    return (np.ascontiguousarray(np.cos(emb).T.astype(np.float32)),
            np.ascontiguousarray(np.sin(emb).T.astype(np.float32)))


def build_phase_a(nc, pr, T=2048, SCH=4, KVG=4, stage=9, fused=False, coll_hook=None):
    NCH = T // C
    NSC = NCH // SCH
    SCT = SCH * C
    half = NCH // 2
    io = {}

    def din(name, shape, dt=F32, kind="ExternalInput"):
        h = nc.dram_tensor(name, shape, dt, kind=kind)
        io[name] = h
        return h.ap()

    x = din("x", [T, HID])
    wq = din("wq", [HID, NQ * D])
    wkv = din("wkv", [HID, 1024])
    cosf = din("cosf", [D, T])
    sinf = din("sinf", [D, T])
    triUs_d = din("triUs", [C, C])
    oness_d = din("oness", [C, C])
    sixcol_d = din("sixcol", [C, 1])
    trimask_d = din("trimask", [C, 512])
    maskA_d = din("maskA", [C, 512])
    maskB_d = din("maskB", [C, 512])
    mrot_d = din("mrot", [C, C])
    idn_d = din("idn", [C, C])
    twocol_d = din("twocol", [C, 1])
    epscol_d = din("epscol", [C, 1])
    if fused:
        o_send = nc.dram_tensor("o_send", [NCH * C, NQ * C], BF16, kind="Internal")
        io["o_send"] = o_send
        o_lo = o_hi = o_send.ap()
    else:
        o_lo = din("o_lo", [half * C, NQ * C], BF16, kind="ExternalOutput")
        o_hi = din("o_hi", [half * C, NQ * C], BF16, kind="ExternalOutput")

    A = lambda name, shape, dt: nc.alloc_sbuf_tensor(name, shape, dt)
    c_triUs = A("c_triUs", [C, C], F32)
    c_oness = A("c_oness", [C, C], F32)
    c_sixcol = A("c_sixcol", [C, 1], F32)
    c_trimask = A("c_trimask", [C, 512], F32)
    c_maskA = A("c_maskA", [C, 512], F32)
    c_maskB = A("c_maskB", [C, 512], F32)
    c_mrot = A("c_mrot", [C, C], BF16)
    c_idn = A("c_idn", [C, C], BF16)
    c_two = A("c_two", [C, 1], BF16)
    c_eps = A("c_eps", [C, 1], F32)
    c_cos = A("c_cos", [D, T], BF16)
    c_sin = A("c_sin", [D, T], BF16)
    for t_, src in [(c_triUs, triUs_d), (c_oness, oness_d), (c_sixcol, sixcol_d),
                    (c_trimask, trimask_d), (c_maskA, maskA_d), (c_maskB, maskB_d),
                    (c_eps, epscol_d)]:
        pr.op("sync", lambda e, t_=t_, s=src: e.dma_start(out=t_[:], in_=s[:, :]),
              writes=[t_], dma=True)
    for t_, src in [(c_mrot, mrot_d), (c_idn, idn_d), (c_two, twocol_d),
                    (c_cos, cosf), (c_sin, sinf)]:
        pr.op("gpsimd", lambda e, t_=t_, s=src: e.dma_start(out=t_[:], in_=s[:, :]),
              writes=[t_], dma=True)

    x_tm = [A("x_tm0", [C, HID], F32)]
    xn_tm = [A(f"xn_tm{i}", [C, HID], BF16) for i in range(2)]
    xn_fm = A("xn_fm", [C, DC * SCT], BF16)
    wqb = [A(f"wqb{i}", [C, DC * D], BF16) for i in range(2)]
    wkvb = [A(f"wkvb{i}", [C, KVG * 1024], BF16) for i in range(2)]
    k_tm = A("k_tm", [C, SCH * 512], F32)
    expq = A("expq", [C, NQ * SCT], BF16)
    rq_ar = A("rq_ar", [C, NQ * SCT], BF16)
    q_tmp = [A(f"q_tmp{i}", [C, SCT], BF16) for i in range(2)]
    rt1 = [A("rt1_0", [C, SCT], BF16)]
    NS = SCH + 1
    v_ring = A("v_ring", [C, NKV * NS * C], BF16)
    rk_ring = A("rk_ring", [C, NKV * 2 * C], BF16)
    S_f32 = A("S_f32", [C, NKV * C], F32)
    S_bf = A("S_bf", [C, NKV * C], BF16)
    stats = A("stats", [C, 48], F32)
    mv = A("mv", [C, 2], F32)
    rstd = A("rstd", [C, 1], F32)
    nmean = A("nmean", [C, 1], F32)
    gs0 = A("gs0", [C, 512], F32)
    gs1 = A("gs1", [C, 512], F32)
    gtm = A("gtm", [C, 512], F32)
    expk = A("expk", [C, 512], F32)
    ksum = A("ksum", [C, NKV], F32)
    rk_col = A("rk_col", [C, NKV], F32)
    enG = A("enG", [C, 512], F32)
    eBC = A("eBC", [C, 512], BF16)
    lamc = A("lamc", [C, NKV], F32)
    eGf = A("eGf", [C, 512], BF16)
    kt1 = A("kt1", [C, 512], F32)
    ktm = A("ktm", [C, 512], BF16)
    khm = A("khm", [C, 512], BF16)
    kb16 = A("kb16", [C, 512], BF16)
    kfm = A("kfm", [C, NKV * C], BF16)
    ktfm = A("ktfm", [C, NKV * C], BF16)
    rk1 = A("rk1", [C, 512], BF16)
    rk2 = A("rk2", [C, 512], BF16)
    qt_ar = A("qt_ar", [C, NQ * C], BF16)
    rq_col = A("rq_col", [C, NQ], F32)
    ptm = A("ptm", [C, 512], BF16)
    wtmp = A("wtmp", [C, 512], F32)
    expsA = A("expsA", [C, 512], BF16)
    expsB = A("expsB", [C, 512], BF16)
    rw_col = A("rw_col", [C, NQ], F32)
    t_y = A("t_y", [C, C], F32)
    o_tm = A("o_tm", [C, NQ * C], BF16)
    ostg = [A(f"ostg{i}", [C, NQ * C], BF16) for i in range(2)]

    PS = [nc.alloc_psum_tensor(f"bank{i}", [C, 512], F32) for i in range(8)]
    pr.register_psum(PS)

    def mm(out_ap, bank, lhsT, rhs, reads, start, stop, label="", tp=None):
        pr.op("tensor",
              lambda e: e.matmul(out_ap, lhsT, rhs, start=start, stop=stop,
                                 tile_position=tp, skip_group_check=True),
              reads=reads, writes=[bank], label=label)

    pr.op("vector", lambda e: e.memset(o_tm[:], 0.0), writes=[(o_tm, None)])
    pr.op("vector", lambda e: e.memset(S_f32[:], 0.0), writes=[S_f32])
    pr.op("vector", lambda e: e.memset(S_bf[:], 0.0), writes=[S_bf])

    for sc in range(NSC):
        ts0 = sc * SCT
        for cl in range(SCH):
            c = sc * SCH + cl
            xb = x_tm[0]
            nb = xn_tm[c % 2]
            for xh in range(2):
                pr.op("sync", lambda e, xb=xb, c=c, xh=xh: e.dma_start(
                    out=xb[:, xh * 2048:(xh + 1) * 2048],
                    in_=x[c * C:(c + 1) * C, xh * 2048:(xh + 1) * 2048]),
                      writes=[(xb, xh)], dma=True)
                for sl4 in range(4):
                    sl8 = xh * 4 + sl4
                    pr.op("vector", lambda e, xb=xb, sl8=sl8, xh=xh: e.bn_stats(
                        stats[:, sl8 * 6:(sl8 + 1) * 6],
                        xb[:, sl8 * 512:(sl8 + 1) * 512]),
                          reads=[(xb, xh)], writes=[(stats, sl8)])
            pr.op("vector", lambda e: e.bn_aggr(mv[:], stats[:]),
                  reads=[(stats, None)], writes=[mv])
            pr.op("scalar", lambda e: e.activation(rstd[:], mv[:, 1:2], AF.Ln, bias=c_eps[:, 0:1]),
                  reads=[mv, c_eps], writes=[(rstd, "ln")])
            pr.op("scalar", lambda e: e.activation(rstd[:], rstd[:], AF.Exp, scale=-0.5),
                  reads=[(rstd, "ln")], writes=[rstd])
            pr.op("vector", lambda e: e.tensor_scalar(nmean[:], mv[:, 0:1], -1.0, None,
                                                      ALU.mult),
                  reads=[mv], writes=[nmean])
            for xh in range(2):
                pr.op("vector", lambda e, xb=xb, nb=nb, xh=xh: e.tensor_scalar(
                    nb[:, xh * 2048:(xh + 1) * 2048], xb[:, xh * 2048:(xh + 1) * 2048],
                    nmean[:, 0:1], rstd[:, 0:1], ALU.add, ALU.mult),
                      reads=[(xb, xh), nmean, rstd], writes=[(nb, xh)])
            for dc in range(DC):
                if DMAT:
                    q = "sync"
                    pr.op(q, lambda e, nb=nb, dc=dc, cl=cl: e.dma_start(
                        out=xn_fm[:, dc * SCT + cl * C: dc * SCT + (cl + 1) * C],
                        in_=nb[:, dc * C:(dc + 1) * C], transpose=True),
                          reads=[(nb, dc // 16)], writes=[(xn_fm, (dc, cl))],
                          dma=True, label="xnT")
                else:
                    bk = PS[6 + (dc % 2)]
                    pv = bk[:, 0:64].bitcast(BF16)
                    pr.op("tensor", lambda e, nb=nb, dc=dc, pv=pv: e.transpose(
                        pv, nb[:, dc * C:(dc + 1) * C], c_idn[:]),
                          reads=[(nb, dc // 16), c_idn], writes=[bk], label="xnT")
                    pr.op("scalar", lambda e, dc=dc, cl=cl, pv=pv: e.copy(
                        xn_fm[:, dc * SCT + cl * C: dc * SCT + (cl + 1) * C], pv),
                          reads=[bk], writes=[(xn_fm, (dc, cl))])
        if stage < 2:
            continue
        for gg in range(DC // KVG):
            wb = wkvb[gg % 2]
            pr.op("gpsimd", lambda e, wb=wb, gg=gg: e.dma_start(
                out=wb[:, :].rearrange("p (a n) -> p a n", a=KVG),
                in_=wkv[gg * KVG * C:(gg + 1) * KVG * C, :].rearrange(
                    "(a p) n -> p a n", p=C)),
                  writes=[wb], dma=True)
            for di in range(KVG):
                dc = gg * KVG + di
                for t in range(SCH):
                    lhs = xn_fm[:, dc * SCT + t * C: dc * SCT + (t + 1) * C]
                    mm(PS[t][:, :], PS[t], lhs, wb[:, di * 1024: di * 1024 + 512],
                       [(xn_fm, (dc, t)), wb], dc == 0, dc == DC - 1, "kmm")
                    mm(PS[4 + t][:, :], PS[4 + t], lhs,
                       wb[:, di * 1024 + 512: (di + 1) * 1024],
                       [(xn_fm, (dc, t)), wb], dc == 0, dc == DC - 1, "vmm")
        for t in range(SCH):
            c = sc * SCH + t
            pr.op("vector", lambda e, t=t: e.tensor_copy(
                k_tm[:, t * 512:(t + 1) * 512], PS[t][:, :]),
                  reads=[PS[t]], writes=[(k_tm, t)])
            dst = v_ring[:, :].rearrange("p (g s) -> p g s", g=NKV)[
                :, :, (c % NS) * C:(c % NS + 1) * C]
            pr.op("vector", lambda e, t=t, dst=dst: e.tensor_copy(
                dst, PS[4 + t][:, :].rearrange("p (g s) -> p g s", g=NKV)),
                  reads=[PS[4 + t]], writes=[(v_ring, c % NS)])
        for h in range(NQ if stage >= 3 else 0):
            wb = wqb[h % 2]
            for qg in range(8):
                pr.op("gpsimd", lambda e, wb=wb, h=h, qg=qg: e.dma_start(
                    out=wb[:, qg * 4 * D:(qg + 1) * 4 * D].rearrange(
                        "p (a n) -> p a n", a=4),
                    in_=wq[qg * 4 * C:(qg + 1) * 4 * C,
                           h * D:(h + 1) * D].rearrange("(a p) n -> p a n", p=C)),
                      writes=[(wb, qg)], dma=True)
            import os
            qsub = int(os.environ.get("QSUB", "9"))
            psq = PS[h % 2]
            for dc in range(DC if qsub >= 1 else 0):
                mm(psq[:, 0:SCT], psq, wb[:, dc * D:(dc + 1) * D],
                   xn_fm[:, dc * SCT:(dc + 1) * SCT],
                   [wb, (xn_fm, None)], dc == 0, dc == DC - 1, "qmm")
            if qsub < 2:
                continue
            pr.op("scalar", lambda e, h=h, psq=psq: e.activation(
                expq[:, h * SCT:(h + 1) * SCT], psq[:, 0:SCT], AF.Exp),
                  reads=[psq], writes=[(expq, h)])
            qt = q_tmp[h % 2]
            if qsub < 3:
                continue
            pr.op("vector", lambda e, qt=qt, psq=psq: e.tensor_copy(qt[:], psq[:, 0:SCT]),
                  reads=[psq], writes=[qt])
            if qsub < 4:
                continue
            rb = PS[2 + h % 2]
            rbv = rb[:, 0:SCT]
            mm(rbv, rb, c_mrot[:], qt[:], [c_mrot, qt], True, True, "rotq")
            if qsub < 5:
                continue
            t1 = rt1[0]
            pr.op("vector", lambda e, qt=qt, t1=t1, ts0=ts0: e.tensor_tensor(
                t1[:], qt[:], c_cos[:, ts0:ts0 + SCT], ALU.mult),
                  reads=[qt, c_cos], writes=[t1])
            pr.op("vector", lambda e, h=h, rbv=rbv, rb=rb, ts0=ts0: e.tensor_tensor(
                rq_ar[:, h * SCT:(h + 1) * SCT], rbv, c_sin[:, ts0:ts0 + SCT],
                ALU.mult),
                  reads=[rb, c_sin], writes=[(rq_ar, (h, 0))], label="rq_sin")
            pr.op("vector", lambda e, h=h, t1=t1: e.tensor_tensor(
                rq_ar[:, h * SCT:(h + 1) * SCT], rq_ar[:, h * SCT:(h + 1) * SCT],
                t1[:], ALU.add),
                  reads=[(rq_ar, (h, 0)), t1], writes=[(rq_ar, h)], label="rq_add")
        for cl in range(SCH if stage >= 4 else 0):
            c = sc * SCH + cl
            ks = k_tm[:, cl * 512:(cl + 1) * 512]
            kread = (k_tm, cl)
            cur = c % 2
            prv = (c - 1) % 2
            vcur = c % NS
            vprv = (c - 1) % NS
            pr.op("scalar", lambda e, ks=ks: e.activation(gs0[:], ks, AF.Abs),
                  reads=[kread], writes=[gs0])
            pr.op("scalar", lambda e: e.activation(gs1[:], gs0[:], AF.Exp, scale=-1.0),
                  reads=[gs0], writes=[gs1])
            pr.op("scalar", lambda e: e.activation(gs1[:], gs1[:], AF.Ln, bias=1.0),
                  reads=[(gs1, None)], writes=[(gs1, "l")])
            pr.op("vector", lambda e, ks=ks: e.tensor_scalar(
                gs0[:], ks, 0.0, None, ALU.min), reads=[kread], writes=[(gs0, "m")])
            pr.op("vector", lambda e: e.tensor_tensor(
                gtm[:], gs0[:], gs1[:], ALU.subtract),
                  reads=[(gs0, "m"), (gs1, "l")], writes=[gtm])
            for g in range(NKV):
                pr.op("scalar", lambda e, g=g, ks=ks: e.activation(
                    expk[:, g * C:(g + 1) * C], ks[:, g * C:(g + 1) * C], AF.Exp,
                    accum_out=ksum[:, g:g + 1]),
                      reads=[kread], writes=[(expk, g), (ksum, g)])
            pr.op("vector", lambda e: e.reciprocal(rk_col[:], ksum[:]),
                  reads=[(ksum, None)], writes=[rk_col])
            mm(PS[0][:, :], PS[0], c_triUs[:], gtm[:], [c_triUs, gtm], True, True, "Gtm")
            mm(PS[1][:, :], PS[1], c_oness[:], gtm[:], [c_oness, gtm], True, True, "GBC")
            for g in range(NKV):
                mm(PS[2][:, g:g + 1], PS[2], gtm[:, g * C:(g + 1) * C], c_sixcol[:],
                   [gtm, c_sixcol], True, True, "lam")
                mm(PS[3][:, g * C:(g + 1) * C], PS[3], gtm[:, g * C:(g + 1) * C],
                   c_triUs[:], [gtm, c_triUs], g == 0, g == NKV - 1, "Gfm")
            pr.op("scalar", lambda e: e.activation(enG[:], PS[0][:, :], AF.Exp, scale=-1.0),
                  reads=[PS[0]], writes=[enG])
            pr.op("scalar", lambda e: e.activation(eBC[:], PS[1][:, :], AF.Exp),
                  reads=[PS[1]], writes=[eBC])
            pr.op("scalar", lambda e: e.activation(lamc[:], PS[2][:, 0:NKV], AF.Exp),
                  reads=[PS[2]], writes=[lamc])
            pr.op("scalar", lambda e: e.activation(eGf[:], PS[3][:, :], AF.Exp),
                  reads=[PS[3]], writes=[eGf])
            if stage < 5:
                continue
            pr.op("vector", lambda e: e.tensor_tensor(kt1[:], expk[:], enG[:], ALU.mult),
                  reads=[(expk, None), enG], writes=[kt1])
            for g in range(NKV):
                pr.op("vector", lambda e, g=g: e.tensor_scalar(
                    ktm[:, g * C:(g + 1) * C], kt1[:, g * C:(g + 1) * C],
                    rk_col[:, g:g + 1], None, ALU.mult),
                      reads=[kt1, rk_col], writes=[(ktm, g)])
            pr.op("vector", lambda e: e.tensor_tensor(khm[:], ktm[:], eBC[:], ALU.mult),
                  reads=[(ktm, None), eBC], writes=[khm])
            pr.op("vector", lambda e, ks=ks: e.tensor_copy(kb16[:], ks),
                  reads=[kread], writes=[kb16])
            for g in range(NKV):
                bk = PS[6 + g % 2]
                pv = bk[:, 0:64].bitcast(BF16)
                pr.op("tensor", lambda e, g=g, pv=pv: e.transpose(
                    pv, kb16[:, g * C:(g + 1) * C], c_idn[:]),
                      reads=[kb16, c_idn], writes=[bk], label="kT")
                pr.op("scalar", lambda e, g=g, pv=pv: e.copy(
                    kfm[:, g * C:(g + 1) * C], pv), reads=[bk], writes=[(kfm, g)])
                bk2 = PS[6 + (g + 1) % 2]
                pv2 = bk2[:, 64:128].bitcast(BF16)
                pr.op("tensor", lambda e, g=g, pv2=pv2: e.transpose(
                    pv2, ktm[:, g * C:(g + 1) * C], c_idn[:]),
                      reads=[(ktm, g), c_idn], writes=[bk2], label="KtT")
                pr.op("scalar", lambda e, g=g, pv2=pv2: e.copy(
                    ktfm[:, g * C:(g + 1) * C], pv2), reads=[bk2], writes=[(ktfm, g)])
            rb = PS[4]
            rbv = rb[:, 0:512]
            mm(rbv, rb, c_mrot[:], kfm[:, :], [c_mrot, (kfm, None)], True, True, "rotk")
            cos3 = c_cos[:, c * C:(c + 1) * C].unsqueeze(1).broadcast_to((C, NKV, C))
            sin3 = c_sin[:, c * C:(c + 1) * C].unsqueeze(1).broadcast_to((C, NKV, C))
            pr.op("vector", lambda e, cos3=cos3: e.tensor_tensor(
                rk1[:, :].rearrange("p (g s) -> p g s", g=NKV),
                kfm[:, :].rearrange("p (g s) -> p g s", g=NKV), cos3, ALU.mult),
                  reads=[(kfm, None), c_cos], writes=[rk1])
            pr.op("vector", lambda e, sin3=sin3, rbv=rbv, rb=rb: e.tensor_tensor(
                rk2[:, :].rearrange("p (g s) -> p g s", g=NKV),
                rbv.rearrange("p (g s) -> p g s", g=NKV), sin3, ALU.mult),
                  reads=[rb, c_sin], writes=[rk2])
            rks = rk_ring[:, :].rearrange("p (g s) -> p g s", g=NKV)[
                :, :, cur * C:(cur + 1) * C]
            pr.op("vector", lambda e, rks=rks: e.tensor_tensor(
                rks, rk1[:, :].rearrange("p (g s) -> p g s", g=NKV),
                rk2[:, :].rearrange("p (g s) -> p g s", g=NKV), ALU.add),
                  reads=[rk1, rk2], writes=[(rk_ring, cur)])
            if stage < 6:
                continue
            for h in range(NQ):
                g = h // GRP
                pr.op("vector", lambda e, h=h, g=g, cl=cl: e.tensor_tensor(
                    qt_ar[:, h * C:(h + 1) * C],
                    expq[:, h * SCT + cl * C: h * SCT + (cl + 1) * C],
                    eGf[:, g * C:(g + 1) * C], ALU.mult),
                      reads=[(expq, h), eGf], writes=[(qt_ar, h)])
                mm(PS[5][:, h:h + 1], PS[5],
                   expq[:, h * SCT + cl * C: h * SCT + (cl + 1) * C], c_two[:],
                   [(expq, h), c_two], True, True, "qsum")
            pr.op("vector", lambda e: e.reciprocal(rq_col[:], PS[5][:, 0:NQ]),
                  reads=[PS[5]], writes=[rq_col])
            for g in range(NKV if stage >= 7 else 0):
                vsl = v_ring[:, g * NS * C + vcur * C: g * NS * C + (vcur + 1) * C]
                vrc = (v_ring, vcur)
                mm(PS[4][:, :], PS[4], ktfm[:, g * C:(g + 1) * C],
                   qt_ar[:, g * GRP * C:(g + 1) * GRP * C],
                   [(ktfm, g), (qt_ar, None)], True, True, "PT")
                pr.op("vector", lambda e: e.tensor_tensor(
                    ptm[:], PS[4][:, :], c_trimask[:], ALU.mult),
                      reads=[PS[4], c_trimask], writes=[ptm])
                for j in range(GRP):
                    h = g * GRP + j
                    mm(PS[5][:, j * C:(j + 1) * C], PS[5], qt_ar[:, h * C:(h + 1) * C],
                       S_bf[:, g * C:(g + 1) * C],
                       [(qt_ar, h), (S_bf, g)], True, False, "o_int")
                    mm(PS[5][:, j * C:(j + 1) * C], PS[5], ptm[:, j * C:(j + 1) * C],
                       vsl, [ptm, vrc], False, True, "o_intra")
                mm(PS[3][:, 0:C], PS[3], khm[:, g * C:(g + 1) * C], vsl,
                   [khm, vrc], True, True, "dS")
                pr.op("vector", lambda e, g=g: e.scalar_tensor_tensor(
                    S_f32[:, g * C:(g + 1) * C], S_f32[:, g * C:(g + 1) * C],
                    lamc[:, g:g + 1], PS[3][:, 0:C], ALU.mult, ALU.add),
                      reads=[(S_f32, g), lamc, PS[3]], writes=[(S_f32, g)])
                pr.op("scalar", lambda e, g=g: e.copy(
                    S_bf[:, g * C:(g + 1) * C], S_f32[:, g * C:(g + 1) * C]),
                      reads=[(S_f32, g)], writes=[(S_bf, g)])
                if stage < 8:
                    continue
                rqv = rq_ar[:, :].rearrange("p (h s) -> p h s", h=NQ)[
                    :, g * GRP:(g + 1) * GRP, cl * C:(cl + 1) * C]
                rkB = rk_ring[:, g * 2 * C + cur * C: g * 2 * C + (cur + 1) * C]
                mm(PS[7][:, :], PS[7], rkB, rqv, [(rk_ring, cur), (rq_ar, None)],
                   True, True, "scB")
                pr.op("vector", lambda e: e.tensor_tensor(
                    wtmp[:], PS[7][:, :], c_maskB[:], ALU.add),
                      reads=[PS[7], c_maskB], writes=[wtmp])
                pr.op("scalar", lambda e: e.activation(expsB[:], wtmp[:], AF.Exp,
                                                       scale=SCALE),
                      reads=[wtmp], writes=[expsB])
                if c > 0:
                    rkA = rk_ring[:, g * 2 * C + prv * C + 64:
                                  g * 2 * C + prv * C + 128]
                    mm(PS[6][64:128, :], PS[6], rkA, rqv,
                       [(rk_ring, prv), (rq_ar, None)], True, True, "scA", tp=(0, 64))
                    pr.op("vector", lambda e: e.tensor_tensor(
                        wtmp[64:128, :], PS[6][64:128, :], c_maskA[64:128, :], ALU.add),
                          reads=[PS[6], c_maskA], writes=[(wtmp, "A")])
                    pr.op("scalar", lambda e: e.activation(
                        expsA[64:128, :], wtmp[64:128, :], AF.Exp, scale=SCALE),
                          reads=[(wtmp, "A")], writes=[expsA])
                for j in range(GRP):
                    h = g * GRP + j
                    if c > 0:
                        mm(PS[2][:, 16 + h:17 + h], PS[2],
                           expsA[64:128, j * C:(j + 1) * C], c_two[64:128, :],
                           [expsA, c_two], True, False, "wsA", tp=(64, 0))
                    mm(PS[2][:, 16 + h:17 + h], PS[2], expsB[:, j * C:(j + 1) * C],
                       c_two[:], [expsB, c_two], c == 0, True, "wsB")
                    if c > 0:
                        vprev = v_ring[64:128, g * NS * C + vprv * C:
                                       g * NS * C + vprv * C + C]
                        mm(PS[0][:, j * C:(j + 1) * C], PS[0],
                           expsA[64:128, j * C:(j + 1) * C], vprev,
                           [expsA, (v_ring, vprv)], True, False, "yA", tp=(64, 0))
                    mm(PS[0][:, j * C:(j + 1) * C], PS[0],
                       expsB[:, j * C:(j + 1) * C], vsl,
                       [expsB, vrc], c == 0, True, "yB")
                pr.op("vector", lambda e, g=g: e.reciprocal(
                    rw_col[:, g * GRP:(g + 1) * GRP],
                    PS[2][:, 16 + g * GRP:16 + (g + 1) * GRP]),
                      reads=[PS[2]], writes=[(rw_col, g)])
                for j in range(GRP):
                    h = g * GRP + j
                    pr.op("scalar", lambda e, h=h, j=j: e.activation(
                        t_y[:], PS[0][:, j * C:(j + 1) * C], AF.Copy,
                        scale=rw_col[:, h:h + 1]),
                          reads=[PS[0], (rw_col, g)], writes=[t_y])
                    pr.op("vector", lambda e, h=h, j=j: e.scalar_tensor_tensor(
                        o_tm[:, h * C:(h + 1) * C], PS[5][:, j * C:(j + 1) * C],
                        rq_col[:, h:h + 1], t_y[:], ALU.mult, ALU.add),
                          reads=[PS[5], rq_col, t_y], writes=[(o_tm, h)])
            stg = ostg[c % 2]
            for h in range(NQ):
                if DMAT:
                    q = "sync"
                    pr.op(q, lambda e, h=h, stg=stg: e.dma_start(
                        out=stg[:, h * C:(h + 1) * C], in_=o_tm[:, h * C:(h + 1) * C],
                        transpose=True),
                          reads=[(o_tm, h)], writes=[(stg, h)], dma=True, label="oT")
                else:
                    bk = PS[6 + h % 2]
                    pv = bk[:, 0:64].bitcast(BF16)
                    pr.op("tensor", lambda e, h=h, pv=pv: e.transpose(
                        pv, o_tm[:, h * C:(h + 1) * C], c_idn[:]),
                          reads=[(o_tm, h), c_idn], writes=[bk], label="oT")
                    pr.op("scalar", lambda e, h=h, pv=pv, stg=stg: e.copy(
                        stg[:, h * C:(h + 1) * C], pv), reads=[bk], writes=[(stg, h)])
            if fused:
                tgt, lo = o_lo, c
            else:
                tgt, lo = (o_lo, c) if c < half else (o_hi, c - half)
            pr.op("scalar", lambda e, tgt=tgt, lo=lo, stg=stg: e.dma_start(
                out=tgt[lo * C:(lo + 1) * C, :], in_=stg[:]),
                  reads=[(stg, None)], writes=[(tgt, lo)], dma=True)
            if fused and coll_hook is not None:
                coll_hook(pr, io["o_send"], c)
    io["pool"] = {
        "ofl_blocks": ([(xn_fm, i) for i in range(16)] +
                       [(expq, i) for i in range(8)] +
                       [(rq_ar, i) for i in range(8)]),
        "wring": [xn_tm[0][:, :], xn_tm[1][:, :], wqb[0][:, :], wqb[1][:, :],
                  wkvb[0][:, :], wkvb[1][:, :],
                  x_tm[0][:, :].bitcast(BF16)[:, 0:4096],
                  x_tm[0][:, :].bitcast(BF16)[:, 4096:8192]],
        "x1r": x_tm[0],
        "f32s": [gs0, gs1, gtm, expk, enG, kt1, wtmp],
        "hsb": qt_ar, "hwf": o_tm,
        "xn2tm": [wqb[0], wqb[1]],
        "c_idn": c_idn, "c_eps": c_eps, "c_two": c_two,
        "small": T < 2048,
        "bcb": k_tm[:, :].bitcast(BF16),
        "wrab_pieces": [c_cos[:, :], c_sin[:, :], ktfm[:, :]],
        "k_tm": k_tm, "c_cos_t": c_cos, "c_sin_t": c_sin, "ktfm_t": ktfm,
        "rk1": rk1, "rk2": rk2, "ptm": ptm,
        "router_scratch": eBC[:, :].bitcast(F32),
        "eBC_t": eBC,
        "stats": stats, "mv": mv, "rstd": rstd, "nmean": nmean,
        "PS": PS,
    }
    return io


def build_phase_b(nc, pr, NTOK=1024, pool=None, o_parts=None, fstage=9):
    NCHB = NTOK // C
    NH = 32
    fused = pool is not None
    io = {}

    def din(name, shape, dt=F32, kind="ExternalInput"):
        h = nc.dram_tensor(name, shape, dt, kind=kind)
        io[name] = h
        return h.ap()

    x_own = din("x_own", [NTOK, HID])
    wo = din("wo", [NH * C, HID])
    wbT = din("wbT", [HID, HID])
    wra = din("wra", [HID, 136])
    bc = din("bc", [C, HID])
    bbias = din("bbias", [1, HID])
    onerow_d = din("onerow", [1, C])
    y = din("y", [NTOK, HID], kind="ExternalOutput")
    x1 = nc.dram_tensor("x1", [NTOK, HID], F32, kind="Internal").ap()

    A = lambda name, shape, dt: nc.alloc_sbuf_tensor(name, shape, dt)

    if fused:
        PS = pool["PS"]
        c_idn2 = pool["c_idn"]
        c_eps2 = pool["c_eps"]
        blocks = pool["ofl_blocks"]
        wring = pool["wring"]
        wring_units = [ap.tensor for ap in wring]
        x1r0 = pool["x1r"]
        f32s = pool["f32s"]
        xsl = [f32s[0], f32s[1]]
        x1s = [f32s[2], f32s[3]]
        outs = [f32s[4], f32s[5]]
        hsb_t = pool["hsb"]
        hwf_t = pool["hwf"]
        xn2tm = pool["xn2tm"]
        o_send, cc_out = o_parts
        bar = [t for t, _ in blocks] + [x1r0, hsb_t, hwf_t] + list(xn2tm)
        seen = set()
        bar2 = []
        for t in bar:
            if t.name not in seen:
                seen.add(t.name)
                bar2.append(t)
        pr.op("vector", lambda e: e.engine_nop(),
              writes=[(t, None) for t in bar2])
    else:
        PS = [nc.alloc_psum_tensor(f"bankB{i}", [C, 512], F32) for i in range(8)]
        pr.register_psum(PS)
        c_idn2 = A("c_idn2", [C, C], BF16)
        idn2_d = din("idn2", [C, C])
        c_eps2 = A("c_eps2", [C, 1], F32)
        epscol2_d = din("epscol2", [C, 1])
        pr.op("gpsimd", lambda e: e.dma_start(out=c_idn2[:], in_=idn2_d[:, :]),
              writes=[c_idn2], dma=True)
        pr.op("sync", lambda e: e.dma_start(out=c_eps2[:], in_=epscol2_d[:, :]),
              writes=[c_eps2], dma=True)
        o_full = din("o_full", [NCHB * C, NH * C], BF16)
        ofl = A("ofl", [C, NH * NCHB * C], BF16)
        blocks = [(ofl, i) for i in range(NH)]
        wr0 = [A(f"wrg{i}", [C, 4096], BF16) for i in range(8)]
        wring = [t[:, :] for t in wr0]
        wring_units = wr0
        x1r0 = A("x1r0", [C, HID], F32)
        xsl = [A(f"xsl{i}", [C, 512], F32) for i in range(2)]
        x1s = [A(f"x1s{i}", [C, 512], F32) for i in range(2)]
        outs = [A(f"outs{i}", [C, 512], F32) for i in range(2)]
        hsb_t = A("hsb", [C, NCHB * C], BF16)
        hwf_t = A("hwf", [C, NCHB * C], BF16)
        xn2tm = [A(f"xn2tm{i}", [C, HID], BF16) for i in range(2)]

    BST = NCHB * C
    if fused:
        assert NCHB == 8, "fused phase B requires NTOK=1024"

    def blk_ap(b, lo, hi):
        t, i = blocks[b]
        return t[:, i * BST + lo: i * BST + hi]

    if fused:
        c_one2 = pool["rk2"]
        c_one2_u = (pool["rk2"], None)
    else:
        c_one2 = A("c_one2", [1, C], BF16)
        c_one2_u = (c_one2, None)
    pr.op("gpsimd", lambda e: e.dma_start(out=c_one2[0:1, 0:C], in_=onerow_d[:, :]),
          writes=[c_one2_u], dma=True)
    if fused and not pool.get("small"):
        wp = pool["wrab_pieces"]
        wp_units = [pool["c_cos_t"], pool["c_sin_t"], pool["ktfm_t"]]
        def wrab_ap(dc):
            if dc < 15:
                return wp[0][:, dc * 136:(dc + 1) * 136]
            if dc < 30:
                return wp[1][:, (dc - 15) * 136:(dc - 14) * 136]
            return wp[2][:, (dc - 30) * 136:(dc - 29) * 136]
        def wrab_unit(dc):
            return (wp_units[0 if dc < 15 else (1 if dc < 30 else 2)], None)
        pr.op("vector", lambda e: e.engine_nop(),
              writes=[(t, None) for t in wp_units] + [(pool["k_tm"], None)])
        for dcg, (lo, hi) in enumerate([(0, 15), (15, 30), (30, 32)]):
            pr.op("gpsimd", lambda e, lo=lo, hi=hi: e.dma_start(
                out=wp[0 if lo == 0 else (1 if lo == 15 else 2)][
                    :, 0:(hi - lo) * 136].rearrange("p (a n) -> p a n", a=hi - lo),
                in_=wra[lo * C:hi * C, :].rearrange("(a p) n -> p a n", p=C)),
                  writes=[(wp_units[dcg], None)], dma=True)
        bcb = pool["bcb"]
        bcb_unit = (pool["k_tm"], None)
        pr.op("gpsimd", lambda e: e.dma_start(out=bcb[:, 0:HID], in_=bc[:, :]),
              writes=[bcb_unit], dma=True)
        bbr = pool["rk1"]
        bbr_u = (pool["rk1"], None)
        bbr_per_fcw = True
    else:
        wrab = A("wrab", [C, DC * 136], BF16)
        pr.op("gpsimd", lambda e: e.dma_start(
            out=wrab[:, :].rearrange("p (a n) -> p a n", a=DC),
            in_=wra[:, :].rearrange("(a p) n -> p a n", p=C)), writes=[wrab], dma=True)
        def wrab_ap(dc):
            return wrab[:, dc * 136:(dc + 1) * 136]
        def wrab_unit(dc):
            return (wrab, None)
        bcb = A("bcb", [C, HID], BF16)
        bcb_unit = (bcb, None)
        pr.op("gpsimd", lambda e: e.dma_start(out=bcb[:, :], in_=bc[:, :]),
              writes=[bcb_unit], dma=True)
        bbr = A("bbr", [1, HID], BF16)
        bbr_u = (bbr, None)
        bbr_per_fcw = False
        pr.op("gpsimd", lambda e: e.dma_start(out=bbr[:], in_=bbias[:, :]),
              writes=[bbr], dma=True)
    if fused:
        stats = pool["stats"]; mv = pool["mv"]; rstd = pool["rstd"]; nmean = pool["nmean"]
        logits_all = f32s[6]
        hwb = pool["ptm"]
        _rt = pool["router_scratch"]
    else:
        stats = A("statsB", [C, 48], F32)
        mv = A("mvB", [C, 2], F32)
        rstd = A("rstdB", [C, 1], F32)
        nmean = A("nmeanB", [C, 1], F32)
        logits_all = A("logits_all", [C, 8 * NCHB], F32)
        hwb = A("hwb", [C, C], BF16)
    if fused:
        probs = _rt[:, 0:8]
        oh1 = _rt[:, 8:16]
        probs2 = _rt[:, 16:24]
        oh2 = _rt[:, 24:32]
        cmb = _rt[:, 32:40]
        pmax = _rt[:, 40:41]
        psum_r = _rt[:, 41:42]
        m2c = _rt[:, 42:43]
        den = _rt[:, 43:44]
        rden = _rt[:, 44:45]
        w1c = _rt[:, 45:46]
        w2c = _rt[:, 46:47]
    else:
        pmax = A("pmax", [C, 1], F32)
        psum_r = A("psum_r", [C, 1], F32)
        probs = A("probs", [C, 8], F32)
        oh1 = A("oh1", [C, 8], F32)
        probs2 = A("probs2", [C, 8], F32)
        m2c = A("m2c", [C, 1], F32)
        oh2 = A("oh2", [C, 8], F32)
        den = A("den", [C, 1], F32)
        rden = A("rden", [C, 1], F32)
        w1c = A("w1c", [C, 1], F32)
        w2c = A("w2c", [C, 1], F32)
        cmb = A("cmb", [C, 8], F32)

    def mm(out_ap, bank, lhsT, rhs, reads, start, stop, label="", tp=None):
        pr.op("tensor",
              lambda e: e.matmul(out_ap, lhsT, rhs, start=start, stop=stop,
                                 tile_position=tp, skip_group_check=True),
              reads=reads, writes=[bank], label=label)

    if fused:
        msel_d = din("msel", [C, 2])
        msel = A("mselt", [C, 2], F32)
        pr.op("sync", lambda e: e.dma_start(out=msel[:], in_=msel_d[:, :]),
              writes=[msel], dma=True)
        candA = pool["hwf"]
        candB = pool["hsb"]
        cc_a, cc_b1, cc_b2 = o_parts[1]
        ccA_ap, ccB1_ap, ccB2_ap = cc_a.ap(), cc_b1.ap(), cc_b2.ap()
        for t in range(NCHB):
            for hb in range(2):
                rA = hb * 1024 + t * C
                if t < NCHB // 2:
                    cb_t, cb_ap, rB = cc_b1, ccB1_ap, hb * 512 + t * C
                else:
                    cb_t, cb_ap, rB = cc_b2, ccB2_ap, hb * 512 + (t - NCHB // 2) * C
                pr.op("sync", lambda e, rA=rA: e.dma_start(
                    out=candA[:, :], in_=ccA_ap[rA:rA + C, :]),
                      reads=[(cc_a, None)], writes=[(candA, None)], dma=True)
                pr.op("sync", lambda e, rB=rB, cb_ap=cb_ap: e.dma_start(
                    out=candB[:, :], in_=cb_ap[rB:rB + C, :]),
                      reads=[(cb_t, None)], writes=[(candB, None)], dma=True)
                pr.op("vector", lambda e: e.tensor_scalar(
                    candA[:, :], candA[:, :], msel[:, 0:1], None, ALU.mult),
                      reads=[(candA, None), msel], writes=[(candA, None)])
                hb0 = hb * 16
                b0 = hb0
                while b0 < hb0 + 16:
                    tns, i0 = blocks[b0]
                    b1 = b0
                    while (b1 + 1 < hb0 + 16 and blocks[b1 + 1][0] is tns
                           and blocks[b1 + 1][1] == blocks[b1][1] + 1):
                        b1 += 1
                    nb = b1 - b0 + 1
                    hoff = b0 - hb0
                    dst = tns[:, i0 * BST:(i0 + nb) * BST].rearrange(
                        "p (h n) -> p h n", h=nb)[:, :, t * C:(t + 1) * C]
                    pr.op("vector", lambda e, dst=dst, hoff=hoff, nb=nb: e.scalar_tensor_tensor(
                        dst, candB[:, hoff * C:(hoff + nb) * C].rearrange(
                            "p (h n) -> p h n", h=nb),
                        msel[:, 1:2],
                        candA[:, hoff * C:(hoff + nb) * C].rearrange(
                            "p (h n) -> p h n", h=nb),
                        ALU.mult, ALU.add),
                          reads=[(candA, None), (candB, None), msel],
                          writes=[(tns, ("c", t, hb0 + hoff))], label="oblend")
                    b0 = b1 + 1
    else:
        o_full_ap = io["o_full"].ap()
        for t in range(NCHB):
            dst = blocks[0][0][:, :].rearrange("p (h n) -> p h n", h=NH)[
                :, :, t * C:(t + 1) * C]
            pr.op("sync", lambda e, t=t, dst=dst: e.dma_start(
                out=dst, in_=o_full_ap[t * C:(t + 1) * C, :].rearrange(
                    "p (h n) -> p h n", h=NH)),
                  writes=[(blocks[0][0], ("c", t))], dma=True)

    def wslot(ocw, q):
        return (ocw * 4 + q) % 8

    if fstage < 3:
        for t in range(NCHB):
            ob = x1s[0]
            pr.op("vector", lambda e, t=t, ob=ob: e.tensor_copy(
                ob[:, 0:512], blk_ap(0, t * C, (t + 1) * C).bitcast(F32)[:, 0:64].broadcast_to((C, 512)) if False else ob[:, 0:512]),
                  reads=[], writes=[(ob, None)]) if False else None
            pr.op("sync", lambda e, t=t: e.dma_start(
                out=y[t * C:(t + 1) * C, 0:512],
                in_=blk_ap(0, t * C, (t + 1) * C).bitcast(F32)[:, 0:64]) if False else e.dma_start(
                out=y[t * C:(t + 1) * C, 0:1024].bitcast(BF16)[:, 0:1024] if False else y[t * C:(t + 1) * C, 0:512],
                in_=x_own[t * C:(t + 1) * C, 0:512]),
                  reads=[(blocks[0][0], ("c", t))], writes=[(y, t)], dma=True)
        pr.op("gpsimd", lambda e: e.nop(), reads=[(y, None)])
        return io

    for ocw in range(8):
        for q in range(4):
            sl = wring[wslot(ocw, q)]
            un = wring_units[wslot(ocw, q)]
            pr.op("gpsimd", lambda e, sl=sl, ocw=ocw, q=q: e.dma_start(
                out=sl.rearrange("p (a n) -> p a n", a=8),
                in_=wo[q * 8 * C:(q + 1) * 8 * C,
                       ocw * 512:(ocw + 1) * 512].rearrange("(a p) n -> p a n", p=C)),
                  writes=[(un, None)], dma=True)
        for hd in range(NH):
            sl = wring[wslot(ocw, hd // 8)]
            un = wring_units[wslot(ocw, hd // 8)]
            for t in range(NCHB):
                mm(PS[t][:, :], PS[t], blk_ap(hd, t * C, (t + 1) * C),
                   sl[:, (hd % 8) * 512:(hd % 8 + 1) * 512],
                   [(blocks[hd][0], ("c", t, hd) if fused else ("c", t)),
                    (un, None)], hd == 0, hd == NH - 1, "womm")
        for t in range(NCHB):
            x1b = x1s[t % 2]
            xs = xsl[t % 2]
            pr.op("sync", lambda e, t=t, ocw=ocw, xs=xs: e.dma_start(
                out=xs[:, 0:512], in_=x_own[t * C:(t + 1) * C, ocw * 512:(ocw + 1) * 512]),
                  writes=[(xs, None)], dma=True)
            pr.op("vector", lambda e, t=t, x1b=x1b, xs=xs: e.tensor_tensor(
                x1b[:, 0:512], PS[t][:, :], xs[:, 0:512], ALU.add),
                  reads=[PS[t], (xs, None)], writes=[(x1b, None)])
            pr.op("sync", lambda e, t=t, ocw=ocw, x1b=x1b: e.dma_start(
                out=x1[t * C:(t + 1) * C, ocw * 512:(ocw + 1) * 512], in_=x1b[:, 0:512]),
                  reads=[(x1b, None)], writes=[(x1, (t, ocw))], dma=True)

    if fstage < 4:
        pr.op("gpsimd", lambda e: e.nop(), reads=[(x1, None)])
        io["y"] = y
        for t in range(NCHB):
            xs = xsl[0]
            pr.op("sync", lambda e, t=t, xs=xs: e.dma_start(
                out=xs[:, 0:512], in_=x1[t * C:(t + 1) * C, 0:512]),
                  reads=[(x1, None)], writes=[(xs, None)], dma=True)
            pr.op("sync", lambda e, t=t, xs=xs: e.dma_start(
                out=y[t * C:(t + 1) * C, 0:512], in_=xs[:, 0:512]),
                  reads=[(xs, None)], writes=[(y, t)], dma=True)
        pr.op("gpsimd", lambda e: e.nop(), reads=[(y, None)])
        return io

    pr.op("vector", lambda e: e.engine_nop(),
          writes=[(blocks[b][0], None) for b in (0, 16, 24)] +
                 [(t, None) for t in xn2tm] + [(x1r0, None)])

    for t in range(NCHB):
        xr = x1r0
        nb = xn2tm[t % 2]
        pr.op("sync", lambda e, t=t, xr=xr: e.dma_start(
            out=xr[:, :], in_=x1[t * C:(t + 1) * C, :]),
              reads=[(x1, None)], writes=[(xr, None)], dma=True)
        for sl8 in range(8):
            pr.op("vector", lambda e, xr=xr, sl8=sl8: e.bn_stats(
                stats[:, sl8 * 6:(sl8 + 1) * 6], xr[:, sl8 * 512:(sl8 + 1) * 512]),
                  reads=[(xr, None)], writes=[(stats, sl8)])
        pr.op("vector", lambda e: e.bn_aggr(mv[:], stats[:]),
              reads=[(stats, None)], writes=[mv])
        pr.op("scalar", lambda e: e.activation(rstd[:], mv[:, 1:2], AF.Ln,
                                               bias=c_eps2[:, 0:1]),
              reads=[mv, c_eps2], writes=[(rstd, "ln")])
        pr.op("scalar", lambda e: e.activation(rstd[:], rstd[:], AF.Exp, scale=-0.5),
              reads=[(rstd, "ln")], writes=[rstd])
        pr.op("vector", lambda e: e.tensor_scalar(nmean[:], mv[:, 0:1], -1.0, None,
                                                  ALU.mult),
              reads=[mv], writes=[nmean])
        pr.op("vector", lambda e, xr=xr, nb=nb: e.tensor_scalar(
            nb[:, :], xr[:, :], nmean[:, 0:1], rstd[:, 0:1], ALU.add, ALU.mult),
              reads=[(xr, None), nmean, rstd], writes=[(nb, None)])
        for dc in range(DC):
            if DMAT:
                q = "sync"
                pr.op(q, lambda e, nb=nb, dc=dc, t=t: e.dma_start(
                    out=blk_ap(dc, t * C, (t + 1) * C), in_=nb[:, dc * C:(dc + 1) * C],
                    transpose=True),
                      reads=[(nb, None)], writes=[(blocks[dc][0], ("x", dc, t))],
                      dma=True, label="xn2T")
            else:
                bk = PS[6 + (dc % 2)]
                pv = bk[:, 0:64].bitcast(BF16)
                pr.op("tensor", lambda e, nb=nb, dc=dc, pv=pv: e.transpose(
                    pv, nb[:, dc * C:(dc + 1) * C], c_idn2[:]),
                      reads=[(nb, None), c_idn2], writes=[bk], label="xn2T")
                pr.op("scalar", lambda e, dc=dc, t=t, pv=pv: e.copy(
                    blk_ap(dc, t * C, (t + 1) * C), pv),
                      reads=[bk], writes=[(blocks[dc][0], ("x", dc, t))])

    def xn2read(dc, t):
        return (blocks[dc][0], ("x", dc, t))

    for t in range(NCHB):
        for dc in range(DC):
            mm(PS[t][:, 0:136], PS[t], blk_ap(dc, t * C, (t + 1) * C),
               wrab_ap(dc),
               [xn2read(dc, t), wrab_unit(dc)], dc == 0, dc == DC - 1, "ramm")
    pr.op("vector", lambda e: e.engine_nop(),
          writes=[(hsb_t, None), (hwf_t, None)])
    for t in range(NCHB):
        pr.op("vector", lambda e, t=t: e.tensor_copy(
            logits_all[:, t * 8:(t + 1) * 8], PS[t][:, 0:8]),
              reads=[PS[t]], writes=[(logits_all, t)])
        pr.op("vector", lambda e, t=t: e.tensor_copy(
            hsb_t[:, t * C:(t + 1) * C], PS[t][:, 8:136]),
              reads=[PS[t]], writes=[(hsb_t, ("h", t))])
    for t in range(NCHB):
        pr.op("vector", lambda e, t=t: e.tensor_reduce(
            pmax[:], logits_all[:, t * 8:(t + 1) * 8], mybir.AxisListType.X, ALU.max),
              reads=[(logits_all, t)], writes=[pmax])
        pr.op("vector", lambda e, t=t: e.tensor_scalar(
            probs[:], logits_all[:, t * 8:(t + 1) * 8], pmax[:, 0:1],
            None, ALU.subtract),
              reads=[(logits_all, t), pmax], writes=[(probs, "s")])
        pr.op("scalar", lambda e: e.activation(probs[:], probs[:], AF.Exp,
                                               accum_out=psum_r[:, 0:1]),
              reads=[(probs, "s")], writes=[(probs, "e"), psum_r])
        pr.op("vector", lambda e: e.reciprocal(pmax[:], psum_r[:]),
              reads=[psum_r], writes=[(pmax, "r")])
        pr.op("vector", lambda e: e.tensor_scalar(probs[:], probs[:], pmax[:, 0:1],
                                                  None, ALU.mult),
              reads=[(probs, "e"), (pmax, "r")], writes=[(probs, None)])
        pr.op("vector", lambda e: e.tensor_reduce(pmax[:], probs[:],
                                                  mybir.AxisListType.X, ALU.max),
              reads=[(probs, None)], writes=[(pmax, "m1")])
        pr.op("vector", lambda e: e.tensor_scalar(oh1[:], probs[:], pmax[:, 0:1], None,
                                                  ALU.is_ge),
              reads=[(probs, None), (pmax, "m1")], writes=[oh1])
        pr.op("vector", lambda e: e.scalar_tensor_tensor(probs2[:], oh1[:], -2.0,
                                                         probs[:], ALU.mult, ALU.add),
              reads=[oh1, (probs, None)], writes=[probs2])
        pr.op("vector", lambda e: e.tensor_reduce(m2c[:], probs2[:],
                                                  mybir.AxisListType.X, ALU.max),
              reads=[probs2], writes=[m2c])
        pr.op("vector", lambda e: e.tensor_scalar(oh2[:], probs2[:], m2c[:, 0:1], None,
                                                  ALU.is_ge),
              reads=[probs2, m2c], writes=[oh2])
        pr.op("vector", lambda e: e.tensor_tensor(den[:], pmax[:], m2c[:], ALU.add),
              reads=[(pmax, "m1"), m2c], writes=[den])
        pr.op("vector", lambda e: e.reciprocal(rden[:], den[:]),
              reads=[den], writes=[rden])
        pr.op("vector", lambda e: e.tensor_tensor(w1c[:], pmax[:], rden[:], ALU.mult),
              reads=[(pmax, "m1"), rden], writes=[w1c])
        pr.op("vector", lambda e: e.tensor_tensor(w2c[:], m2c[:], rden[:], ALU.mult),
              reads=[m2c, rden], writes=[w2c])
        pr.op("vector", lambda e: e.tensor_scalar(oh1[:], oh1[:], w1c[:, 0:1], None,
                                                  ALU.mult),
              reads=[(oh1, None), w1c], writes=[(oh1, "w")])
        pr.op("vector", lambda e: e.scalar_tensor_tensor(cmb[:], oh2[:], w2c[:, 0:1],
                                                         oh1[:], ALU.mult, ALU.add),
              reads=[(oh2, None), w2c, (oh1, "w")], writes=[cmb])
        cmb3 = cmb[:, :].unsqueeze(2).broadcast_to((C, 8, 16))
        pr.op("vector", lambda e, t=t, cmb3=cmb3: e.scalar_tensor_tensor(
            hwb[:, 0:C].rearrange("p (a b) -> p a b", a=8),
            hsb_t[:, t * C:(t + 1) * C].rearrange("p (a b) -> p a b", a=8), 2.0,
            cmb3, ALU.mult, ALU.mult),
              reads=[(hsb_t, ("h", t)), cmb], writes=[(hwb, None)])
        if DMAT:
            pr.op("sync", lambda e, t=t: e.dma_start(
                out=hwf_t[:, t * C:(t + 1) * C], in_=hwb[:, 0:C], transpose=True),
                  reads=[(hwb, None)], writes=[(hwf_t, ("w", t))], dma=True, label="hwT")
        else:
            bk = PS[6 + t % 2]
            pv = bk[:, 64:128].bitcast(BF16)
            pr.op("tensor", lambda e, pv=pv: e.transpose(pv, hwb[:, 0:C], c_idn2[:]),
                  reads=[(hwb, None), c_idn2], writes=[bk], label="hwT")
            pr.op("scalar", lambda e, t=t, pv=pv: e.copy(
                hwf_t[:, t * C:(t + 1) * C], pv), reads=[bk], writes=[(hwf_t, ("w", t))])

    for fcw in range(8):
        for q in range(4):
            sl = wring[wslot(fcw, q)]
            un = wring_units[wslot(fcw, q)]
            pr.op("gpsimd", lambda e, sl=sl, fcw=fcw, q=q: e.dma_start(
                out=sl.rearrange("p (a n) -> p a n", a=8),
                in_=wbT[q * 8 * C:(q + 1) * 8 * C,
                        fcw * 512:(fcw + 1) * 512].rearrange("(a p) n -> p a n", p=C)),
                  writes=[(un, None)], dma=True)
        for dc in range(DC):
            sl = wring[wslot(fcw, dc // 8)]
            un = wring_units[wslot(fcw, dc // 8)]
            for t in range(NCHB):
                mm(PS[t][:, :], PS[t], blk_ap(dc, t * C, (t + 1) * C),
                   sl[:, (dc % 8) * 512:(dc % 8 + 1) * 512],
                   [xn2read(dc, t), (un, None)], dc == 0, False, "bmm")
        if bbr_per_fcw:
            pr.op("gpsimd", lambda e, fcw=fcw: e.dma_start(
                out=bbr[0:1, 0:512], in_=bbias[0:1, fcw * 512:(fcw + 1) * 512]),
                  writes=[bbr_u], dma=True)
        for t in range(NCHB):
            mm(PS[t][:, :], PS[t], hwf_t[:, t * C:(t + 1) * C],
               bcb[:, fcw * 512:(fcw + 1) * 512],
               [(hwf_t, ("w", t)), bcb_unit], False, False, "emm")
            bslice = bbr[0:1, 0:512] if bbr_per_fcw else bbr[0:1, fcw * 512:(fcw + 1) * 512]
            mm(PS[t][:, :], PS[t], c_one2[0:1, 0:C], bslice,
               [c_one2_u, bbr_u], False, True, "biasmm")
        for t in range(NCHB):
            xs = xsl[t % 2]
            pr.op("sync", lambda e, t=t, fcw=fcw, xs=xs: e.dma_start(
                out=xs[:, 0:512], in_=x1[t * C:(t + 1) * C, fcw * 512:(fcw + 1) * 512]),
                  reads=[(x1, None)], writes=[(xs, None)], dma=True)
            ob = outs[t % 2]
            pr.op("vector", lambda e, t=t, ob=ob, xs=xs: e.tensor_tensor(
                ob[:, 0:512], PS[t][:, :], xs[:, 0:512], ALU.add),
                  reads=[PS[t], (xs, None)], writes=[(ob, None)])
            pr.op("scalar", lambda e, t=t, fcw=fcw, ob=ob: e.dma_start(
                out=y[t * C:(t + 1) * C, fcw * 512:(fcw + 1) * 512], in_=ob[:, 0:512]),
                  reads=[(ob, None)], writes=[(y, (t, fcw))], dma=True)
    return io


def build_fused(nc, pr, T=2048, SCH=4):
    import os
    fstage = int(os.environ.get("FSTAGE", "9"))
    NCH = T // C
    half_rows = (NCH // 2) * C
    qr = (NCH // 4) * C
    cc_outA = nc.dram_tensor("cc_outA", [2 * half_rows, NQ * C], BF16, kind="Internal")
    cc_outB1 = nc.dram_tensor("cc_outB1", [2 * qr, NQ * C], BF16, kind="Internal")
    cc_outB2 = nc.dram_tensor("cc_outB2", [2 * qr, NQ * C], BF16, kind="Internal")

    def emit_coll(osap, rows, out_t):
        def coll(e):
            return e.collective_compute(
                "AllGather", mybir.AluOpType.bypass,
                replica_groups=[[0, 1], [2, 3], [4, 5], [6, 7]],
                ins=[osap[rows[0]:rows[1], :].opt()],
                outs=[out_t.ap()[:, :].opt()])
        return coll

    def coll_hook(pr_, o_send_h, after_chunk):
        osap = o_send_h.ap()
        if after_chunk == NCH // 2 - 1:
            pr_.op("gpsimd", emit_coll(osap, (0, half_rows), cc_outA),
                   reads=[(o_send_h, None)], writes=[(cc_outA, None)],
                   private=True, label="agA")
        elif after_chunk == 3 * NCH // 4 - 1:
            pr_.op("gpsimd", emit_coll(osap, (half_rows, half_rows + qr), cc_outB1),
                   reads=[(o_send_h, None)], writes=[(cc_outB1, None)],
                   private=True, label="agB1")
        elif after_chunk == NCH - 1:
            pr_.op("gpsimd", emit_coll(osap, (half_rows + qr, 2 * half_rows), cc_outB2),
                   reads=[(o_send_h, None)], writes=[(cc_outB2, None)],
                   private=True, label="agB2")

    io_a = build_phase_a(nc, pr, T=T, SCH=SCH, fused=True, coll_hook=coll_hook)
    pool = io_a["pool"]
    o_send = io_a["o_send"]
    cc_out = (cc_outA, cc_outB1, cc_outB2)

    if fstage < 2:
        dbg = nc.dram_tensor("y", [T // 2, HID], F32, kind="ExternalOutput").ap()
        sb = nc.alloc_sbuf_tensor("dbgsb", [C, 512], F32)
        pr.op("sync", lambda e: e.dma_start(out=sb[:, 0:512],
                                            in_=cc_out.ap()[0:C, 0:512]),
              reads=[(cc_out, None)], writes=[sb], dma=True)
        pr.op("sync", lambda e: e.dma_start(out=dbg[0:C, 0:512], in_=sb[:, 0:512]),
              reads=[sb], writes=[("yo", None)], dma=True)
        pr.op("gpsimd", lambda e: e.nop(), reads=[("yo", None)])
        return io_a, {"y": dbg}
    io_b = build_phase_b(nc, pr, NTOK=T // 2, pool=pool, o_parts=(o_send, cc_out),
                         fstage=fstage)
    pr.op("gpsimd", lambda e: e.nop(), reads=[io_b["y"]])
    return io_a, io_b


import concourse.bass as _bass
from concourse.bass_utils import run_bass_kernel_spmd as _run_spmd

_CACHE = {}


def _build_fused():
    if "f" not in _CACHE:
        nc = _bass.Bass("TRN2", target_bir_lowering=False, debug=False)
        pr = Prog(nc)
        build_fused(nc, pr, T=2048, SCH=4)
        pr.finalize_and_emit()
        _CACHE["f"] = nc
    return _CACHE["f"]


def kernel(L0_attn_hidden_states, L0_attn_Wq, L0_attn_Wk, L0_attn_Wv, L0_attn_Wo,
           L0_ffn_W_base, L0_ffn_b_base, L0_ffn_W_router, L0_ffn_A, L0_ffn_B):
    x = np.ascontiguousarray(np.asarray(L0_attn_hidden_states, dtype=np.float32))
    Wq = np.asarray(L0_attn_Wq, dtype=np.float32)
    Wk = np.asarray(L0_attn_Wk, dtype=np.float32)
    Wv = np.asarray(L0_attn_Wv, dtype=np.float32)
    Wo = np.ascontiguousarray(np.asarray(L0_attn_Wo, dtype=np.float32))
    Wb = np.asarray(L0_ffn_W_base, dtype=np.float32)
    bb = np.asarray(L0_ffn_b_base, dtype=np.float32)
    Wr = np.asarray(L0_ffn_W_router, dtype=np.float32)
    Af = np.asarray(L0_ffn_A, dtype=np.float32)
    Bf = np.asarray(L0_ffn_B, dtype=np.float32)

    cons = host_constants()
    cosf, sinf = rope_tables_fm(2048)
    wra = np.ascontiguousarray(np.concatenate(
        [Wr.T, Af.reshape(128, HID).T], axis=1).astype(np.float32))
    bcc = np.ascontiguousarray(Bf.transpose(0, 2, 1).reshape(128, HID).astype(np.float32))
    shared = {"cosf": cosf, "sinf": sinf, "wo": Wo,
              "wbT": np.ascontiguousarray(Wb.T), "wra": wra, "bc": bcc,
              "bbias": np.ascontiguousarray(bb[None, :]),
              "onerow": np.ones((1, 128), np.float32)}
    for k in ["triUs", "oness", "sixcol", "trimask", "maskA", "maskB", "mrot",
              "idn", "twocol", "epscol"]:
        shared[k] = cons[k]

    in_maps = []
    for c in range(8):
        bi, hh = c >> 1, c & 1
        m = dict(shared)
        m["x"] = np.ascontiguousarray(x[bi])
        m["wq"] = np.ascontiguousarray(Wq[:, hh * 2048:(hh + 1) * 2048])
        m["wkv"] = np.ascontiguousarray(
            np.concatenate([Wk[:, hh * 512:(hh + 1) * 512],
                            Wv[:, hh * 512:(hh + 1) * 512]], axis=1))
        m["x_own"] = np.ascontiguousarray(x[bi, hh * 1024:(hh + 1) * 1024])
        ms = np.zeros((128, 2), np.float32)
        ms[:, 0] = 1.0 if hh == 0 else 0.0
        ms[:, 1] = 1.0 if hh == 1 else 0.0
        m["msel"] = ms
        in_maps.append(m)

    res = _run_spmd(_build_fused(), in_maps, core_ids=list(range(8)))

    out = np.zeros((4, 2048, 4096), np.float32)
    for c in range(8):
        bi, hh = c >> 1, c & 1
        out[bi, hh * 1024:(hh + 1) * 1024] = np.asarray(res.results[c]["y"])
    return out


# version 05
# speedup vs baseline: 1.1513x; 1.1513x over previous
import concourse.bass as bass
import concourse.mybir as mybir

ENGINES = ("tensor", "vector", "scalar", "gpsimd", "sync")


def _tname(x):
    if isinstance(x, str):
        return x
    t = getattr(x, "tensor", None)
    if t is not None:
        return t.name
    return x.name


class _Op:
    __slots__ = ("idx", "engine", "emit", "reads", "writes", "dma", "deps",
                 "needs_inc", "rank", "label", "private", "switch_n")

    def __init__(self, idx, engine, emit, reads, writes, dma, label,
                 private=False, switch_n=0):
        self.idx = idx
        self.engine = engine
        self.emit = emit
        self.reads = reads
        self.writes = writes
        self.dma = dma
        self.deps = set()
        self.needs_inc = False
        self.rank = None
        self.label = label
        self.private = private
        self.switch_n = switch_n


class Prog:
    def __init__(self, nc):
        self.nc = nc
        self.ops = []
        self.state = {}
        self.alias = {}
        self.psum_names = set()

    def register_psum(self, tensors):
        for t in tensors:
            self.psum_names.add(_tname(t))

    def add_alias(self, virt, base):
        self.alias[_tname(virt)] = _tname(base)

    def _norm(self, u):
        if isinstance(u, tuple):
            x, tag = u
        else:
            x, tag = u, None
        n = _tname(x)
        n = self.alias.get(n, n)
        return (n, tag)

    def _recs(self, name):
        return self.state.setdefault(name, {})

    def _dep_read(self, op, name, tag):
        recs = self._recs(name)
        if tag is None:
            for t, rec in recs.items():
                if rec["w"] is not None:
                    op.deps.add(rec["w"])
        else:
            for t in (tag, None):
                rec = recs.get(t)
                if rec and rec["w"] is not None:
                    op.deps.add(rec["w"])

    def _dep_write(self, op, name, tag):
        recs = self._recs(name)
        if tag is None:
            for t, rec in recs.items():
                if rec["w"] is not None:
                    op.deps.add(rec["w"])
                op.deps.update(rec["r"])
        else:
            for t in (tag, None):
                rec = recs.get(t)
                if rec:
                    if rec["w"] is not None:
                        op.deps.add(rec["w"])
                    op.deps.update(rec["r"])

    def _mark_read(self, idx, name, tag):
        rec = self._recs(name).setdefault(tag, {"w": None, "r": set()})
        rec["r"].add(idx)

    def _mark_write(self, idx, name, tag):
        recs = self._recs(name)
        if tag is None:
            recs.clear()
        recs[tag] = {"w": idx, "r": set()}

    def op(self, engine, emit, reads=(), writes=(), dma=False, label="",
           private=False, switch_n=0):
        idx = len(self.ops)
        nreads = [self._norm(u) for u in reads]
        nwrites = [self._norm(u) for u in writes]
        promoted = [u for u in nreads if u[0] in self.psum_names]
        if promoted:
            nreads = [u for u in nreads if u[0] not in self.psum_names]
            nwrites = nwrites + promoted
        o = _Op(idx, engine, emit, nreads, nwrites, dma, label,
                private=private, switch_n=switch_n)
        for n, t in o.reads:
            self._dep_read(o, n, t)
        for n, t in o.writes:
            self._dep_write(o, n, t)
        o.deps.discard(idx)
        if engine == "tensor":
            o.deps = {d for d in o.deps
                      if not (self.ops[d].engine == "tensor" and not self.ops[d].dma)}
        for n, t in o.reads:
            self._mark_read(idx, n, t)
        for n, t in o.writes:
            self._mark_write(idx, n, t)
        self.ops.append(o)
        return idx

    def barrier_unit(self, name):
        return (name, None)

    NSEM = 8

    def finalize_and_emit(self):
        nc = self.nc
        R = self.NSEM
        for o in self.ops:
            for d in o.deps:
                self.ops[d].needs_inc = True
        eng_count = {e: 0 for e in ENGINES}
        dma_count = {e: 0 for e in ENGINES}
        self._psems = {}
        for o in self.ops:
            if o.private:
                o.rank = None
                o.needs_inc = True
            elif o.switch_n:
                slots = []
                for _ in range(o.switch_n):
                    j = dma_count[o.engine]
                    dma_count[o.engine] += 1
                    slots.append((j % R, j // R + 1))
                o.rank = slots
                o.needs_inc = True
            elif o.dma:
                j = dma_count[o.engine]
                dma_count[o.engine] += 1
                o.rank = (j % R, j // R + 1)
                o.needs_inc = True
            elif o.needs_inc:
                eng_count[o.engine] += 1
                o.rank = eng_count[o.engine]
        self._esem = {}
        self._qsem = {}
        import contextlib
        self._stack = contextlib.ExitStack()
        for o in self.ops:
            if o.private:
                self._psems[o.idx] = self._stack.enter_context(
                    nc.semaphore(f"sp_{o.idx}"))
        for e in ENGINES:
            if eng_count[e]:
                self._esem[e] = self._stack.enter_context(nc.semaphore(f"se_{e}"))
            if dma_count[e]:
                self._qsem[e] = [
                    self._stack.enter_context(nc.semaphore(f"sq_{e}{i}"))
                    for i in range(min(R, dma_count[e]))
                ]
        ops_by_engine = {e: [o for o in self.ops if o.engine == e] for e in ENGINES}

        block_cm = nc.Block()
        block = block_cm.__enter__()

        def emit_stream(eng_name, eng):
            hwm = {}
            def do_wait(key, val):
                if hwm.get(key, 0) >= val:
                    return
                kind = key[0]
                if kind == "e":
                    sem = self._esem[key[1]]
                elif kind == "p":
                    sem = self._psems[key[1]]
                else:
                    sem = self._qsem[key[1]][key[2]]
                eng.wait_ge(sem, val)
                hwm[key] = val
            for o in ops_by_engine[eng_name]:
                waits = {}
                for d in o.deps:
                    dop = self.ops[d]
                    if (dop.engine == eng_name and not dop.dma and not dop.private
                            and not dop.switch_n and eng_name == "tensor"):
                        continue
                    if dop.private:
                        kvs = [(("p", d), 1)]
                    elif dop.switch_n:
                        kvs = [(("q", dop.engine, s), 16 * k) for s, k in dop.rank]
                    elif dop.dma:
                        slot, k = dop.rank
                        kvs = [(("q", dop.engine, slot), 16 * k)]
                    else:
                        kvs = [(("e", dop.engine), dop.rank)]
                    for key, val in kvs:
                        if val > waits.get(key, 0):
                            waits[key] = val
                if o.dma and not o.switch_n and not o.private:
                    slot, k = o.rank
                    if k > 1:
                        key = ("q", o.engine, slot)
                        v = 16 * (k - 1)
                        if v > waits.get(key, 0):
                            waits[key] = v
                if o.switch_n:
                    for slot, k in o.rank:
                        if k > 1:
                            key = ("q", o.engine, slot)
                            v = 16 * (k - 1)
                            if v > waits.get(key, 0):
                                waits[key] = v
                for key in sorted(waits):
                    do_wait(key, waits[key])
                if o.private:
                    inst = o.emit(eng)
                    inst.then_inc(self._psems[o.idx], 1)
                elif o.switch_n:
                    slots = list(o.rank)
                    qs = self._qsem[o.engine]
                    if o.label.endswith("_nosw"):
                        it = iter(slots)
                        def inc(inst, _it=it, _qs=qs):
                            s, k = next(_it)
                            inst.then_inc(_qs[s], 16)
                        o.emit(eng, 0, inc)
                    else:
                        for arm in eng.switch_core_id(n=512):
                            me = int(arm.logical) % 8
                            if int(arm.logical) < 8:
                                it = iter(slots)
                                def inc(inst, _it=it, _qs=qs):
                                    s, k = next(_it)
                                    inst.then_inc(_qs[s], 16)
                                o.emit(eng, me, inc)
                elif o.dma:
                    slot, k = o.rank
                    inst = o.emit(eng)
                    inst.then_inc(self._qsem[o.engine][slot], 16)
                else:
                    inst = o.emit(eng)
                    if o.needs_inc:
                        inst.then_inc(self._esem[o.engine], 1)

        for e in ENGINES:
            if not ops_by_engine[e]:
                continue
            dec = getattr(block, e)
            def body(eng, _e=e):
                emit_stream(_e, eng)
            dec(body)
        block_cm.__exit__(None, None, None)

    def final_wait(self, engine="sync"):
        pass


import numpy as np
import ml_dtypes
import concourse.mybir as mybir
import os
DMAT = os.environ.get("DMAT", "0") == "1"

F32 = mybir.dt.float32
BF16 = mybir.dt.bfloat16
AF = mybir.ActivationFunctionType
ALU = mybir.AluOpType
bfd = ml_dtypes.bfloat16

D = 128
HID = 4096
DC = HID // 128
NQ = 16
NKV = 4
GRP = 4
C = 128
SCALE = float(1.0 / np.sqrt(np.float32(D)))
NEG = -1e9


def host_constants():
    tri01 = np.triu(np.ones((C, C), np.float32))
    tt = np.arange(C)[None, :]
    cons = {}
    cons["triUs"] = np.ascontiguousarray(tri01 * 0.0625)
    cons["oness"] = np.full((C, C), 0.0625, np.float32)
    cons["sixcol"] = np.full((C, 1), 0.0625, np.float32)
    cons["trimask"] = np.ascontiguousarray(np.tile(tri01, (1, 4)))
    kkA = np.arange(64)[:, None]
    mA_full = np.full((C, C), NEG, np.float32)
    mA_full[64:128, :] = np.where(kkA >= tt, 0.0, NEG)
    cons["maskA"] = np.ascontiguousarray(np.tile(mA_full, (1, 4)))
    kkB = np.arange(C)[:, None]
    mB = np.where((kkB <= tt) & (kkB >= tt - 64), 0.0, NEG).astype(np.float32)
    cons["maskB"] = np.ascontiguousarray(np.tile(mB, (1, 4)))
    M = np.zeros((C, C), np.float32)
    for p in range(64):
        M[p + 64, p] = -1.0
        M[p, p + 64] = 1.0
    cons["mrot"] = M
    cons["idn"] = np.eye(C, dtype=np.float32)
    cons["twocol"] = np.full((C, 1), 2.0, np.float32)
    cons["epscol"] = np.full((C, 1), 1e-5, np.float32)
    return cons


def rope_tables_fm(T):
    half = np.arange(D // 2, dtype=np.float32)
    inv = 1.0 / 10000.0 ** (2.0 * half / D)
    fr = np.arange(T, dtype=np.float32)[:, None] * inv[None, :]
    emb = np.concatenate([fr, fr], axis=-1)
    return (np.ascontiguousarray(np.cos(emb).T.astype(np.float32)),
            np.ascontiguousarray(np.sin(emb).T.astype(np.float32)))


def build_phase_a(nc, pr, T=2048, SCH=4, KVG=4, stage=9, fused=False, coll_hook=None):
    NCH = T // C
    NSC = NCH // SCH
    SCT = SCH * C
    half = NCH // 2
    io = {}

    def din(name, shape, dt=F32, kind="ExternalInput"):
        h = nc.dram_tensor(name, shape, dt, kind=kind)
        io[name] = h
        return h.ap()

    x = din("x", [T, HID])
    wq = din("wq", [HID, NQ * D])
    wkv = din("wkv", [HID, 1024])
    cosf = din("cosf", [D, T])
    sinf = din("sinf", [D, T])
    triUs_d = din("triUs", [C, C])
    oness_d = din("oness", [C, C])
    sixcol_d = din("sixcol", [C, 1])
    trimask_d = din("trimask", [C, 512])
    maskA_d = din("maskA", [C, 512])
    maskB_d = din("maskB", [C, 512])
    mrot_d = din("mrot", [C, C])
    idn_d = din("idn", [C, C])
    twocol_d = din("twocol", [C, 1])
    epscol_d = din("epscol", [C, 1])
    if fused:
        o_send = nc.dram_tensor("o_send", [NCH * C, NQ * C], BF16, kind="Internal")
        io["o_send"] = o_send
        o_lo = o_hi = o_send.ap()
    else:
        o_lo = din("o_lo", [half * C, NQ * C], BF16, kind="ExternalOutput")
        o_hi = din("o_hi", [half * C, NQ * C], BF16, kind="ExternalOutput")

    A = lambda name, shape, dt: nc.alloc_sbuf_tensor(name, shape, dt)
    c_triUs = A("c_triUs", [C, C], F32)
    c_oness = A("c_oness", [C, C], F32)
    c_sixcol = A("c_sixcol", [C, 1], F32)
    c_trimask = A("c_trimask", [C, 512], F32)
    c_maskA = A("c_maskA", [C, 512], F32)
    c_maskB = A("c_maskB", [C, 512], F32)
    c_mrot = A("c_mrot", [C, C], BF16)
    c_idn = A("c_idn", [C, C], BF16)
    c_two = A("c_two", [C, 1], BF16)
    c_eps = A("c_eps", [C, 1], F32)
    c_cos = A("c_cos", [D, T], BF16)
    c_sin = A("c_sin", [D, T], BF16)
    for t_, src in [(c_triUs, triUs_d), (c_oness, oness_d), (c_sixcol, sixcol_d),
                    (c_trimask, trimask_d), (c_maskA, maskA_d), (c_maskB, maskB_d),
                    (c_eps, epscol_d)]:
        pr.op("sync", lambda e, t_=t_, s=src: e.dma_start(out=t_[:], in_=s[:, :]),
              writes=[t_], dma=True)
    for t_, src in [(c_mrot, mrot_d), (c_idn, idn_d), (c_two, twocol_d),
                    (c_cos, cosf), (c_sin, sinf)]:
        pr.op("gpsimd", lambda e, t_=t_, s=src: e.dma_start(out=t_[:], in_=s[:, :]),
              writes=[t_], dma=True)

    x_tm = [A("x_tm0", [C, HID], F32)]
    xn_tm = [A(f"xn_tm{i}", [C, HID], BF16) for i in range(2)]
    xn_fm = A("xn_fm", [C, DC * SCT], BF16)
    wqb = [A(f"wqb{i}", [C, DC * D], BF16) for i in range(2)]
    wkvb = [A(f"wkvb{i}", [C, KVG * 1024], BF16) for i in range(2)]
    k_tm = A("k_tm", [C, SCH * 512], F32)
    expq = A("expq", [C, NQ * SCT], BF16)
    rq_ar = A("rq_ar", [C, NQ * SCT], BF16)
    q_tmp = [A(f"q_tmp{i}", [C, SCT], BF16) for i in range(2)]
    rt1 = [A("rt1_0", [C, SCT], BF16)]
    NS = SCH + 1
    v_ring = A("v_ring", [C, NKV * NS * C], BF16)
    rk_ring = A("rk_ring", [C, NKV * 2 * C], BF16)
    S_f32 = A("S_f32", [C, NKV * C], F32)
    S_bf = A("S_bf", [C, NKV * C], BF16)
    stats = A("stats", [C, 48], F32)
    mv = A("mv", [C, 2], F32)
    rstd = A("rstd", [C, 1], F32)
    nmean = A("nmean", [C, 1], F32)
    gs0 = A("gs0", [C, 512], F32)
    gs1 = A("gs1", [C, 512], F32)
    gtm = A("gtm", [C, 512], F32)
    expk = A("expk", [C, 512], F32)
    ksum = A("ksum", [C, NKV], F32)
    rk_col = A("rk_col", [C, NKV], F32)
    enG = A("enG", [C, 512], F32)
    eBC = A("eBC", [C, 512], BF16)
    lamc = A("lamc", [C, NKV], F32)
    eGf = A("eGf", [C, 512], BF16)
    kt1 = A("kt1", [C, 512], F32)
    ktm = A("ktm", [C, 512], BF16)
    khm = A("khm", [C, 512], BF16)
    kb16 = A("kb16", [C, 512], BF16)
    kfm = A("kfm", [C, NKV * C], BF16)
    ktfm = A("ktfm", [C, NKV * C], BF16)
    rk1 = A("rk1", [C, 512], BF16)
    rk2 = A("rk2", [C, 512], BF16)
    qt_ar = A("qt_ar", [C, NQ * C], BF16)
    rq_col = A("rq_col", [C, NQ], F32)
    ptm = A("ptm", [C, 512], BF16)
    wtmp = A("wtmp", [C, 512], F32)
    expsA = A("expsA", [C, 512], BF16)
    expsB = A("expsB", [C, 512], BF16)
    rw_col = A("rw_col", [C, NQ], F32)
    t_y = A("t_y", [C, C], F32)
    o_tm = A("o_tm", [C, NQ * C], BF16)
    ostg = [A(f"ostg{i}", [C, NQ * C], BF16) for i in range(2)]

    PS = [nc.alloc_psum_tensor(f"bank{i}", [C, 512], F32) for i in range(8)]
    pr.register_psum(PS)

    def mm(out_ap, bank, lhsT, rhs, reads, start, stop, label="", tp=None):
        pr.op("tensor",
              lambda e: e.matmul(out_ap, lhsT, rhs, start=start, stop=stop,
                                 tile_position=tp, skip_group_check=True),
              reads=reads, writes=[bank], label=label)

    pr.op("vector", lambda e: e.memset(o_tm[:], 0.0), writes=[(o_tm, None)])
    pr.op("vector", lambda e: e.memset(S_f32[:], 0.0), writes=[S_f32])
    pr.op("vector", lambda e: e.memset(S_bf[:], 0.0), writes=[S_bf])

    for sc in range(NSC):
        ts0 = sc * SCT
        for cl in range(SCH):
            c = sc * SCH + cl
            xb = x_tm[0]
            nb = xn_tm[c % 2]
            for xh in range(2):
                pr.op("sync", lambda e, xb=xb, c=c, xh=xh: e.dma_start(
                    out=xb[:, xh * 2048:(xh + 1) * 2048],
                    in_=x[c * C:(c + 1) * C, xh * 2048:(xh + 1) * 2048]),
                      writes=[(xb, xh)], dma=True)
                for sl4 in range(4):
                    sl8 = xh * 4 + sl4
                    pr.op("vector", lambda e, xb=xb, sl8=sl8, xh=xh: e.bn_stats(
                        stats[:, sl8 * 6:(sl8 + 1) * 6],
                        xb[:, sl8 * 512:(sl8 + 1) * 512]),
                          reads=[(xb, xh)], writes=[(stats, sl8)])
            pr.op("vector", lambda e: e.bn_aggr(mv[:], stats[:]),
                  reads=[(stats, None)], writes=[mv])
            pr.op("scalar", lambda e: e.activation(rstd[:], mv[:, 1:2], AF.Ln, bias=c_eps[:, 0:1]),
                  reads=[mv, c_eps], writes=[(rstd, "ln")])
            pr.op("scalar", lambda e: e.activation(rstd[:], rstd[:], AF.Exp, scale=-0.5),
                  reads=[(rstd, "ln")], writes=[rstd])
            pr.op("vector", lambda e: e.tensor_scalar(nmean[:], mv[:, 0:1], -1.0, None,
                                                      ALU.mult),
                  reads=[mv], writes=[nmean])
            for xh in range(2):
                pr.op("vector", lambda e, xb=xb, nb=nb, xh=xh: e.tensor_scalar(
                    nb[:, xh * 2048:(xh + 1) * 2048], xb[:, xh * 2048:(xh + 1) * 2048],
                    nmean[:, 0:1], rstd[:, 0:1], ALU.add, ALU.mult),
                      reads=[(xb, xh), nmean, rstd], writes=[(nb, xh)])
            for dc in range(DC):
                if DMAT:
                    q = "sync"
                    pr.op(q, lambda e, nb=nb, dc=dc, cl=cl: e.dma_start(
                        out=xn_fm[:, dc * SCT + cl * C: dc * SCT + (cl + 1) * C],
                        in_=nb[:, dc * C:(dc + 1) * C], transpose=True),
                          reads=[(nb, dc // 16)], writes=[(xn_fm, (dc, cl))],
                          dma=True, label="xnT")
                else:
                    bk = PS[6 + (dc % 2)]
                    pv = bk[:, 0:64].bitcast(BF16)
                    pr.op("tensor", lambda e, nb=nb, dc=dc, pv=pv: e.transpose(
                        pv, nb[:, dc * C:(dc + 1) * C], c_idn[:]),
                          reads=[(nb, dc // 16), c_idn], writes=[bk], label="xnT")
                    pr.op("scalar", lambda e, dc=dc, cl=cl, pv=pv: e.copy(
                        xn_fm[:, dc * SCT + cl * C: dc * SCT + (cl + 1) * C], pv),
                          reads=[bk], writes=[(xn_fm, (dc, cl))])
        if stage < 2:
            continue
        for gg in range(DC // KVG):
            wb = wkvb[gg % 2]
            pr.op("gpsimd", lambda e, wb=wb, gg=gg: e.dma_start(
                out=wb[:, :].rearrange("p (a n) -> p a n", a=KVG),
                in_=wkv[gg * KVG * C:(gg + 1) * KVG * C, :].rearrange(
                    "(a p) n -> p a n", p=C)),
                  writes=[wb], dma=True)
            for di in range(KVG):
                dc = gg * KVG + di
                for t in range(SCH):
                    lhs = xn_fm[:, dc * SCT + t * C: dc * SCT + (t + 1) * C]
                    mm(PS[t][:, :], PS[t], lhs, wb[:, di * 1024: di * 1024 + 512],
                       [(xn_fm, (dc, t)), wb], dc == 0, dc == DC - 1, "kmm")
                    mm(PS[4 + t][:, :], PS[4 + t], lhs,
                       wb[:, di * 1024 + 512: (di + 1) * 1024],
                       [(xn_fm, (dc, t)), wb], dc == 0, dc == DC - 1, "vmm")
        for t in range(SCH):
            c = sc * SCH + t
            pr.op("vector", lambda e, t=t: e.tensor_copy(
                k_tm[:, t * 512:(t + 1) * 512], PS[t][:, :]),
                  reads=[PS[t]], writes=[(k_tm, t)])
            dst = v_ring[:, :].rearrange("p (g s) -> p g s", g=NKV)[
                :, :, (c % NS) * C:(c % NS + 1) * C]
            pr.op("vector", lambda e, t=t, dst=dst: e.tensor_copy(
                dst, PS[4 + t][:, :].rearrange("p (g s) -> p g s", g=NKV)),
                  reads=[PS[4 + t]], writes=[(v_ring, c % NS)])
        for h in range(NQ if stage >= 3 else 0):
            wb = wqb[h % 2]
            for qg in range(8):
                pr.op("gpsimd", lambda e, wb=wb, h=h, qg=qg: e.dma_start(
                    out=wb[:, qg * 4 * D:(qg + 1) * 4 * D].rearrange(
                        "p (a n) -> p a n", a=4),
                    in_=wq[qg * 4 * C:(qg + 1) * 4 * C,
                           h * D:(h + 1) * D].rearrange("(a p) n -> p a n", p=C)),
                      writes=[(wb, qg)], dma=True)
            import os
            qsub = int(os.environ.get("QSUB", "9"))
            psq = PS[h % 2]
            for dc in range(DC if qsub >= 1 else 0):
                mm(psq[:, 0:SCT], psq, wb[:, dc * D:(dc + 1) * D],
                   xn_fm[:, dc * SCT:(dc + 1) * SCT],
                   [wb, (xn_fm, None)], dc == 0, dc == DC - 1, "qmm")
            if qsub < 2:
                continue
            pr.op("scalar", lambda e, h=h, psq=psq: e.activation(
                expq[:, h * SCT:(h + 1) * SCT], psq[:, 0:SCT], AF.Exp),
                  reads=[psq], writes=[(expq, h)])
            qt = q_tmp[h % 2]
            if qsub < 3:
                continue
            pr.op("vector", lambda e, qt=qt, psq=psq: e.tensor_copy(qt[:], psq[:, 0:SCT]),
                  reads=[psq], writes=[qt])
            if qsub < 4:
                continue
            rb = PS[2 + h % 2]
            rbv = rb[:, 0:SCT]
            mm(rbv, rb, c_mrot[:], qt[:], [c_mrot, qt], True, True, "rotq")
            if qsub < 5:
                continue
            t1 = rt1[0]
            pr.op("vector", lambda e, qt=qt, t1=t1, ts0=ts0: e.tensor_tensor(
                t1[:], qt[:], c_cos[:, ts0:ts0 + SCT], ALU.mult),
                  reads=[qt, c_cos], writes=[t1])
            pr.op("vector", lambda e, h=h, rbv=rbv, rb=rb, ts0=ts0: e.tensor_tensor(
                rq_ar[:, h * SCT:(h + 1) * SCT], rbv, c_sin[:, ts0:ts0 + SCT],
                ALU.mult),
                  reads=[rb, c_sin], writes=[(rq_ar, (h, 0))], label="rq_sin")
            pr.op("vector", lambda e, h=h, t1=t1: e.tensor_tensor(
                rq_ar[:, h * SCT:(h + 1) * SCT], rq_ar[:, h * SCT:(h + 1) * SCT],
                t1[:], ALU.add),
                  reads=[(rq_ar, (h, 0)), t1], writes=[(rq_ar, h)], label="rq_add")
        for cl in range(SCH if stage >= 4 else 0):
            c = sc * SCH + cl
            ks = k_tm[:, cl * 512:(cl + 1) * 512]
            kread = (k_tm, cl)
            cur = c % 2
            prv = (c - 1) % 2
            vcur = c % NS
            vprv = (c - 1) % NS
            pr.op("scalar", lambda e, ks=ks: e.activation(gs0[:], ks, AF.Abs),
                  reads=[kread], writes=[gs0])
            pr.op("scalar", lambda e: e.activation(gs1[:], gs0[:], AF.Exp, scale=-1.0),
                  reads=[gs0], writes=[gs1])
            pr.op("scalar", lambda e: e.activation(gs1[:], gs1[:], AF.Ln, bias=1.0),
                  reads=[(gs1, None)], writes=[(gs1, "l")])
            pr.op("vector", lambda e, ks=ks: e.tensor_scalar(
                gs0[:], ks, 0.0, None, ALU.min), reads=[kread], writes=[(gs0, "m")])
            pr.op("vector", lambda e: e.tensor_tensor(
                gtm[:], gs0[:], gs1[:], ALU.subtract),
                  reads=[(gs0, "m"), (gs1, "l")], writes=[gtm])
            for g in range(NKV):
                pr.op("scalar", lambda e, g=g, ks=ks: e.activation(
                    expk[:, g * C:(g + 1) * C], ks[:, g * C:(g + 1) * C], AF.Exp,
                    accum_out=ksum[:, g:g + 1]),
                      reads=[kread], writes=[(expk, g), (ksum, g)])
            pr.op("vector", lambda e: e.reciprocal(rk_col[:], ksum[:]),
                  reads=[(ksum, None)], writes=[rk_col])
            mm(PS[0][:, :], PS[0], c_triUs[:], gtm[:], [c_triUs, gtm], True, True, "Gtm")
            mm(PS[1][:, :], PS[1], c_oness[:], gtm[:], [c_oness, gtm], True, True, "GBC")
            for g in range(NKV):
                mm(PS[2][:, g:g + 1], PS[2], gtm[:, g * C:(g + 1) * C], c_sixcol[:],
                   [gtm, c_sixcol], True, True, "lam")
                mm(PS[3][:, g * C:(g + 1) * C], PS[3], gtm[:, g * C:(g + 1) * C],
                   c_triUs[:], [gtm, c_triUs], g == 0, g == NKV - 1, "Gfm")
            pr.op("scalar", lambda e: e.activation(enG[:], PS[0][:, :], AF.Exp, scale=-1.0),
                  reads=[PS[0]], writes=[enG])
            pr.op("scalar", lambda e: e.activation(eBC[:], PS[1][:, :], AF.Exp),
                  reads=[PS[1]], writes=[eBC])
            pr.op("scalar", lambda e: e.activation(lamc[:], PS[2][:, 0:NKV], AF.Exp),
                  reads=[PS[2]], writes=[lamc])
            pr.op("scalar", lambda e: e.activation(eGf[:], PS[3][:, :], AF.Exp),
                  reads=[PS[3]], writes=[eGf])
            if stage < 5:
                continue
            pr.op("vector", lambda e: e.tensor_tensor(kt1[:], expk[:], enG[:], ALU.mult),
                  reads=[(expk, None), enG], writes=[kt1])
            for g in range(NKV):
                pr.op("vector", lambda e, g=g: e.tensor_scalar(
                    ktm[:, g * C:(g + 1) * C], kt1[:, g * C:(g + 1) * C],
                    rk_col[:, g:g + 1], None, ALU.mult),
                      reads=[kt1, rk_col], writes=[(ktm, g)])
            pr.op("vector", lambda e: e.tensor_tensor(khm[:], ktm[:], eBC[:], ALU.mult),
                  reads=[(ktm, None), eBC], writes=[khm])
            pr.op("vector", lambda e, ks=ks: e.tensor_copy(kb16[:], ks),
                  reads=[kread], writes=[kb16])
            for g in range(NKV):
                bk = PS[6 + g % 2]
                pv = bk[:, 0:64].bitcast(BF16)
                pr.op("tensor", lambda e, g=g, pv=pv: e.transpose(
                    pv, kb16[:, g * C:(g + 1) * C], c_idn[:]),
                      reads=[kb16, c_idn], writes=[bk], label="kT")
                pr.op("scalar", lambda e, g=g, pv=pv: e.copy(
                    kfm[:, g * C:(g + 1) * C], pv), reads=[bk], writes=[(kfm, g)])
                bk2 = PS[6 + (g + 1) % 2]
                pv2 = bk2[:, 64:128].bitcast(BF16)
                pr.op("tensor", lambda e, g=g, pv2=pv2: e.transpose(
                    pv2, ktm[:, g * C:(g + 1) * C], c_idn[:]),
                      reads=[(ktm, g), c_idn], writes=[bk2], label="KtT")
                pr.op("scalar", lambda e, g=g, pv2=pv2: e.copy(
                    ktfm[:, g * C:(g + 1) * C], pv2), reads=[bk2], writes=[(ktfm, g)])
            rb = PS[4]
            rbv = rb[:, 0:512]
            mm(rbv, rb, c_mrot[:], kfm[:, :], [c_mrot, (kfm, None)], True, True, "rotk")
            cos3 = c_cos[:, c * C:(c + 1) * C].unsqueeze(1).broadcast_to((C, NKV, C))
            sin3 = c_sin[:, c * C:(c + 1) * C].unsqueeze(1).broadcast_to((C, NKV, C))
            pr.op("vector", lambda e, cos3=cos3: e.tensor_tensor(
                rk1[:, :].rearrange("p (g s) -> p g s", g=NKV),
                kfm[:, :].rearrange("p (g s) -> p g s", g=NKV), cos3, ALU.mult),
                  reads=[(kfm, None), c_cos], writes=[rk1])
            pr.op("vector", lambda e, sin3=sin3, rbv=rbv, rb=rb: e.tensor_tensor(
                rk2[:, :].rearrange("p (g s) -> p g s", g=NKV),
                rbv.rearrange("p (g s) -> p g s", g=NKV), sin3, ALU.mult),
                  reads=[rb, c_sin], writes=[rk2])
            rks = rk_ring[:, :].rearrange("p (g s) -> p g s", g=NKV)[
                :, :, cur * C:(cur + 1) * C]
            pr.op("vector", lambda e, rks=rks: e.tensor_tensor(
                rks, rk1[:, :].rearrange("p (g s) -> p g s", g=NKV),
                rk2[:, :].rearrange("p (g s) -> p g s", g=NKV), ALU.add),
                  reads=[rk1, rk2], writes=[(rk_ring, cur)])
            if stage < 6:
                continue
            for h in range(NQ):
                g = h // GRP
                pr.op("vector", lambda e, h=h, g=g, cl=cl: e.tensor_tensor(
                    qt_ar[:, h * C:(h + 1) * C],
                    expq[:, h * SCT + cl * C: h * SCT + (cl + 1) * C],
                    eGf[:, g * C:(g + 1) * C], ALU.mult),
                      reads=[(expq, h), eGf], writes=[(qt_ar, h)])
                mm(PS[5][:, h:h + 1], PS[5],
                   expq[:, h * SCT + cl * C: h * SCT + (cl + 1) * C], c_two[:],
                   [(expq, h), c_two], True, True, "qsum")
            pr.op("vector", lambda e: e.reciprocal(rq_col[:], PS[5][:, 0:NQ]),
                  reads=[PS[5]], writes=[rq_col])
            for g in range(NKV if stage >= 7 else 0):
                vsl = v_ring[:, g * NS * C + vcur * C: g * NS * C + (vcur + 1) * C]
                vrc = (v_ring, vcur)
                mm(PS[4][:, :], PS[4], ktfm[:, g * C:(g + 1) * C],
                   qt_ar[:, g * GRP * C:(g + 1) * GRP * C],
                   [(ktfm, g), (qt_ar, None)], True, True, "PT")
                pr.op("vector", lambda e: e.tensor_tensor(
                    ptm[:], PS[4][:, :], c_trimask[:], ALU.mult),
                      reads=[PS[4], c_trimask], writes=[ptm])
                for j in range(GRP):
                    h = g * GRP + j
                    mm(PS[5][:, j * C:(j + 1) * C], PS[5], qt_ar[:, h * C:(h + 1) * C],
                       S_bf[:, g * C:(g + 1) * C],
                       [(qt_ar, h), (S_bf, g)], True, False, "o_int")
                    mm(PS[5][:, j * C:(j + 1) * C], PS[5], ptm[:, j * C:(j + 1) * C],
                       vsl, [ptm, vrc], False, True, "o_intra")
                mm(PS[3][:, 0:C], PS[3], khm[:, g * C:(g + 1) * C], vsl,
                   [khm, vrc], True, True, "dS")
                pr.op("vector", lambda e, g=g: e.scalar_tensor_tensor(
                    S_f32[:, g * C:(g + 1) * C], S_f32[:, g * C:(g + 1) * C],
                    lamc[:, g:g + 1], PS[3][:, 0:C], ALU.mult, ALU.add),
                      reads=[(S_f32, g), lamc, PS[3]], writes=[(S_f32, g)])
                pr.op("scalar", lambda e, g=g: e.copy(
                    S_bf[:, g * C:(g + 1) * C], S_f32[:, g * C:(g + 1) * C]),
                      reads=[(S_f32, g)], writes=[(S_bf, g)])
                if stage < 8:
                    continue
                rqv = rq_ar[:, :].rearrange("p (h s) -> p h s", h=NQ)[
                    :, g * GRP:(g + 1) * GRP, cl * C:(cl + 1) * C]
                rkB = rk_ring[:, g * 2 * C + cur * C: g * 2 * C + (cur + 1) * C]
                mm(PS[7][:, :], PS[7], rkB, rqv, [(rk_ring, cur), (rq_ar, None)],
                   True, True, "scB")
                pr.op("vector", lambda e: e.tensor_tensor(
                    wtmp[:], PS[7][:, :], c_maskB[:], ALU.add),
                      reads=[PS[7], c_maskB], writes=[wtmp])
                pr.op("scalar", lambda e: e.activation(expsB[:], wtmp[:], AF.Exp,
                                                       scale=SCALE),
                      reads=[wtmp], writes=[expsB])
                if c > 0:
                    rkA = rk_ring[:, g * 2 * C + prv * C + 64:
                                  g * 2 * C + prv * C + 128]
                    mm(PS[6][64:128, :], PS[6], rkA, rqv,
                       [(rk_ring, prv), (rq_ar, None)], True, True, "scA", tp=(0, 64))
                    pr.op("vector", lambda e: e.tensor_tensor(
                        wtmp[64:128, :], PS[6][64:128, :], c_maskA[64:128, :], ALU.add),
                          reads=[PS[6], c_maskA], writes=[(wtmp, "A")])
                    pr.op("scalar", lambda e: e.activation(
                        expsA[64:128, :], wtmp[64:128, :], AF.Exp, scale=SCALE),
                          reads=[(wtmp, "A")], writes=[expsA])
                for j in range(GRP):
                    h = g * GRP + j
                    if c > 0:
                        mm(PS[2][:, 16 + h:17 + h], PS[2],
                           expsA[64:128, j * C:(j + 1) * C], c_two[64:128, :],
                           [expsA, c_two], True, False, "wsA", tp=(64, 0))
                    mm(PS[2][:, 16 + h:17 + h], PS[2], expsB[:, j * C:(j + 1) * C],
                       c_two[:], [expsB, c_two], c == 0, True, "wsB")
                    if c > 0:
                        vprev = v_ring[64:128, g * NS * C + vprv * C:
                                       g * NS * C + vprv * C + C]
                        mm(PS[0][:, j * C:(j + 1) * C], PS[0],
                           expsA[64:128, j * C:(j + 1) * C], vprev,
                           [expsA, (v_ring, vprv)], True, False, "yA", tp=(64, 0))
                    mm(PS[0][:, j * C:(j + 1) * C], PS[0],
                       expsB[:, j * C:(j + 1) * C], vsl,
                       [expsB, vrc], c == 0, True, "yB")
                pr.op("vector", lambda e, g=g: e.reciprocal(
                    rw_col[:, g * GRP:(g + 1) * GRP],
                    PS[2][:, 16 + g * GRP:16 + (g + 1) * GRP]),
                      reads=[PS[2]], writes=[(rw_col, g)])
                for j in range(GRP):
                    h = g * GRP + j
                    pr.op("scalar", lambda e, h=h, j=j: e.activation(
                        t_y[:], PS[0][:, j * C:(j + 1) * C], AF.Copy,
                        scale=rw_col[:, h:h + 1]),
                          reads=[PS[0], (rw_col, g)], writes=[t_y])
                    pr.op("vector", lambda e, h=h, j=j: e.scalar_tensor_tensor(
                        o_tm[:, h * C:(h + 1) * C], PS[5][:, j * C:(j + 1) * C],
                        rq_col[:, h:h + 1], t_y[:], ALU.mult, ALU.add),
                          reads=[PS[5], rq_col, t_y], writes=[(o_tm, h)])
            stg = ostg[c % 2]
            for h in range(NQ):
                if DMAT:
                    q = "sync"
                    pr.op(q, lambda e, h=h, stg=stg: e.dma_start(
                        out=stg[:, h * C:(h + 1) * C], in_=o_tm[:, h * C:(h + 1) * C],
                        transpose=True),
                          reads=[(o_tm, h)], writes=[(stg, h)], dma=True, label="oT")
                else:
                    bk = PS[6 + h % 2]
                    pv = bk[:, 0:64].bitcast(BF16)
                    pr.op("tensor", lambda e, h=h, pv=pv: e.transpose(
                        pv, o_tm[:, h * C:(h + 1) * C], c_idn[:]),
                          reads=[(o_tm, h), c_idn], writes=[bk], label="oT")
                    pr.op("scalar", lambda e, h=h, pv=pv, stg=stg: e.copy(
                        stg[:, h * C:(h + 1) * C], pv), reads=[bk], writes=[(stg, h)])
            if fused:
                tgt, lo = o_lo, c
            else:
                tgt, lo = (o_lo, c) if c < half else (o_hi, c - half)
            pr.op("scalar", lambda e, tgt=tgt, lo=lo, stg=stg: e.dma_start(
                out=tgt[lo * C:(lo + 1) * C, :], in_=stg[:]),
                  reads=[(stg, None)], writes=[(tgt, lo)], dma=True)
            if fused and coll_hook is not None:
                coll_hook(pr, io["o_send"], c)
    io["pool"] = {
        "ofl_blocks": ([(xn_fm, i) for i in range(16)] +
                       [(expq, i) for i in range(8)] +
                       [(rq_ar, i) for i in range(8)]),
        "wring": [xn_tm[0][:, :], xn_tm[1][:, :], wqb[0][:, :], wqb[1][:, :],
                  wkvb[0][:, :], wkvb[1][:, :],
                  x_tm[0][:, :].bitcast(BF16)[:, 0:4096],
                  x_tm[0][:, :].bitcast(BF16)[:, 4096:8192]],
        "x1r": x_tm[0],
        "f32s": [gs0, gs1, gtm, expk, enG, kt1, wtmp],
        "hsb": qt_ar, "hwf": o_tm,
        "xn2tm": [wqb[0], wqb[1]],
        "c_idn": c_idn, "c_eps": c_eps, "c_two": c_two,
        "small": T < 2048,
        "bcb": k_tm[:, :].bitcast(BF16),
        "wrab_pieces": [c_cos[:, :], c_sin[:, :], ktfm[:, :]],
        "k_tm": k_tm, "c_cos_t": c_cos, "c_sin_t": c_sin, "ktfm_t": ktfm,
        "rk1": rk1, "rk2": rk2, "ptm": ptm,
        "router_scratch": eBC[:, :].bitcast(F32),
        "eBC_t": eBC,
        "stats": stats, "mv": mv, "rstd": rstd, "nmean": nmean,
        "PS": PS,
    }
    return io


def build_phase_b(nc, pr, NTOK=1024, pool=None, o_parts=None, fstage=9):
    NCHB = NTOK // C
    NH = 32
    fused = pool is not None
    io = {}

    def din(name, shape, dt=F32, kind="ExternalInput"):
        h = nc.dram_tensor(name, shape, dt, kind=kind)
        io[name] = h
        return h.ap()

    x_own = din("x_own", [NTOK, HID])
    wo = din("wo", [NH * C, HID])
    wbT = din("wbT", [HID, HID])
    wra = din("wra", [HID, 136])
    bc = din("bc", [C, HID])
    bbias = din("bbias", [1, HID])
    onerow_d = din("onerow", [1, C])
    y = din("y", [NTOK, HID], kind="ExternalOutput")
    x1 = nc.dram_tensor("x1", [NTOK, HID], F32, kind="Internal").ap()

    A = lambda name, shape, dt: nc.alloc_sbuf_tensor(name, shape, dt)

    if fused:
        PS = pool["PS"]
        c_idn2 = pool["c_idn"]
        c_eps2 = pool["c_eps"]
        blocks = pool["ofl_blocks"]
        wring = pool["wring"]
        wring_units = [ap.tensor for ap in wring]
        x1r0 = pool["x1r"]
        f32s = pool["f32s"]
        xsl = [f32s[0], f32s[1]]
        x1s = [f32s[2], f32s[3]]
        outs = [f32s[4], f32s[5]]
        hsb_t = pool["hsb"]
        hwf_t = pool["hwf"]
        xn2tm = pool["xn2tm"]
        o_send, cc_out = o_parts
        bar = [t for t, _ in blocks] + [x1r0, hsb_t, hwf_t] + list(xn2tm)
        seen = set()
        bar2 = []
        for t in bar:
            if t.name not in seen:
                seen.add(t.name)
                bar2.append(t)
        pr.op("vector", lambda e: e.engine_nop(),
              writes=[(t, None) for t in bar2])
    else:
        PS = [nc.alloc_psum_tensor(f"bankB{i}", [C, 512], F32) for i in range(8)]
        pr.register_psum(PS)
        c_idn2 = A("c_idn2", [C, C], BF16)
        idn2_d = din("idn2", [C, C])
        c_eps2 = A("c_eps2", [C, 1], F32)
        epscol2_d = din("epscol2", [C, 1])
        pr.op("gpsimd", lambda e: e.dma_start(out=c_idn2[:], in_=idn2_d[:, :]),
              writes=[c_idn2], dma=True)
        pr.op("sync", lambda e: e.dma_start(out=c_eps2[:], in_=epscol2_d[:, :]),
              writes=[c_eps2], dma=True)
        o_full = din("o_full", [NCHB * C, NH * C], BF16)
        ofl = A("ofl", [C, NH * NCHB * C], BF16)
        blocks = [(ofl, i) for i in range(NH)]
        wr0 = [A(f"wrg{i}", [C, 4096], BF16) for i in range(8)]
        wring = [t[:, :] for t in wr0]
        wring_units = wr0
        x1r0 = A("x1r0", [C, HID], F32)
        xsl = [A(f"xsl{i}", [C, 512], F32) for i in range(2)]
        x1s = [A(f"x1s{i}", [C, 512], F32) for i in range(2)]
        outs = [A(f"outs{i}", [C, 512], F32) for i in range(2)]
        hsb_t = A("hsb", [C, NCHB * C], BF16)
        hwf_t = A("hwf", [C, NCHB * C], BF16)
        xn2tm = [A(f"xn2tm{i}", [C, HID], BF16) for i in range(2)]

    BST = NCHB * C
    if fused:
        assert NCHB == 8, "fused phase B requires NTOK=1024"

    def blk_ap(b, lo, hi):
        t, i = blocks[b]
        return t[:, i * BST + lo: i * BST + hi]

    if fused:
        c_one2 = pool["rk2"]
        c_one2_u = (pool["rk2"], None)
    else:
        c_one2 = A("c_one2", [1, C], BF16)
        c_one2_u = (c_one2, None)
    pr.op("gpsimd", lambda e: e.dma_start(out=c_one2[0:1, 0:C], in_=onerow_d[:, :]),
          writes=[c_one2_u], dma=True)
    if fused and not pool.get("small"):
        wp = pool["wrab_pieces"]
        wp_units = [pool["c_cos_t"], pool["c_sin_t"], pool["ktfm_t"]]
        def wrab_ap(dc):
            if dc < 15:
                return wp[0][:, dc * 136:(dc + 1) * 136]
            if dc < 30:
                return wp[1][:, (dc - 15) * 136:(dc - 14) * 136]
            return wp[2][:, (dc - 30) * 136:(dc - 29) * 136]
        def wrab_unit(dc):
            return (wp_units[0 if dc < 15 else (1 if dc < 30 else 2)], None)
        pr.op("vector", lambda e: e.engine_nop(),
              writes=[(t, None) for t in wp_units] + [(pool["k_tm"], None)])
        for dcg, (lo, hi) in enumerate([(0, 15), (15, 30), (30, 32)]):
            pr.op("gpsimd", lambda e, lo=lo, hi=hi: e.dma_start(
                out=wp[0 if lo == 0 else (1 if lo == 15 else 2)][
                    :, 0:(hi - lo) * 136].rearrange("p (a n) -> p a n", a=hi - lo),
                in_=wra[lo * C:hi * C, :].rearrange("(a p) n -> p a n", p=C)),
                  writes=[(wp_units[dcg], None)], dma=True)
        bcb = pool["bcb"]
        bcb_unit = (pool["k_tm"], None)
        pr.op("gpsimd", lambda e: e.dma_start(out=bcb[:, 0:HID], in_=bc[:, :]),
              writes=[bcb_unit], dma=True)
        bbr = pool["rk1"]
        bbr_u = (pool["rk1"], None)
        bbr_per_fcw = True
    else:
        wrab = A("wrab", [C, DC * 136], BF16)
        pr.op("gpsimd", lambda e: e.dma_start(
            out=wrab[:, :].rearrange("p (a n) -> p a n", a=DC),
            in_=wra[:, :].rearrange("(a p) n -> p a n", p=C)), writes=[wrab], dma=True)
        def wrab_ap(dc):
            return wrab[:, dc * 136:(dc + 1) * 136]
        def wrab_unit(dc):
            return (wrab, None)
        bcb = A("bcb", [C, HID], BF16)
        bcb_unit = (bcb, None)
        pr.op("gpsimd", lambda e: e.dma_start(out=bcb[:, :], in_=bc[:, :]),
              writes=[bcb_unit], dma=True)
        bbr = A("bbr", [1, HID], BF16)
        bbr_u = (bbr, None)
        bbr_per_fcw = False
        pr.op("gpsimd", lambda e: e.dma_start(out=bbr[:], in_=bbias[:, :]),
              writes=[bbr], dma=True)
    if fused:
        stats = pool["stats"]; mv = pool["mv"]; rstd = pool["rstd"]; nmean = pool["nmean"]
        logits_all = f32s[6]
        hwb = pool["ptm"]
        _rt = pool["router_scratch"]
    else:
        stats = A("statsB", [C, 48], F32)
        mv = A("mvB", [C, 2], F32)
        rstd = A("rstdB", [C, 1], F32)
        nmean = A("nmeanB", [C, 1], F32)
        logits_all = A("logits_all", [C, 8 * NCHB], F32)
        hwb = A("hwb", [C, C], BF16)
    if fused:
        probs = _rt[:, 0:8]
        oh1 = _rt[:, 8:16]
        probs2 = _rt[:, 16:24]
        oh2 = _rt[:, 24:32]
        cmb = _rt[:, 32:40]
        pmax = _rt[:, 40:41]
        psum_r = _rt[:, 41:42]
        m2c = _rt[:, 42:43]
        den = _rt[:, 43:44]
        rden = _rt[:, 44:45]
        w1c = _rt[:, 45:46]
        w2c = _rt[:, 46:47]
    else:
        pmax = A("pmax", [C, 1], F32)
        psum_r = A("psum_r", [C, 1], F32)
        probs = A("probs", [C, 8], F32)
        oh1 = A("oh1", [C, 8], F32)
        probs2 = A("probs2", [C, 8], F32)
        m2c = A("m2c", [C, 1], F32)
        oh2 = A("oh2", [C, 8], F32)
        den = A("den", [C, 1], F32)
        rden = A("rden", [C, 1], F32)
        w1c = A("w1c", [C, 1], F32)
        w2c = A("w2c", [C, 1], F32)
        cmb = A("cmb", [C, 8], F32)

    def mm(out_ap, bank, lhsT, rhs, reads, start, stop, label="", tp=None):
        pr.op("tensor",
              lambda e: e.matmul(out_ap, lhsT, rhs, start=start, stop=stop,
                                 tile_position=tp, skip_group_check=True),
              reads=reads, writes=[bank], label=label)

    if fused:
        msel_d = din("msel", [C, 2])
        msel = A("mselt", [C, 2], F32)
        pr.op("sync", lambda e: e.dma_start(out=msel[:], in_=msel_d[:, :]),
              writes=[msel], dma=True)
        candA = pool["hwf"]
        candB = pool["hsb"]
        cc_a, cc_b1, cc_b2 = o_parts[1]
        ccA_ap, ccB1_ap, ccB2_ap = cc_a.ap(), cc_b1.ap(), cc_b2.ap()
        for t in range(NCHB):
            for hb in range(2):
                rA = hb * 1024 + t * C
                if t < NCHB // 2:
                    cb_t, cb_ap, rB = cc_b1, ccB1_ap, hb * 512 + t * C
                else:
                    cb_t, cb_ap, rB = cc_b2, ccB2_ap, hb * 512 + (t - NCHB // 2) * C
                pr.op("sync", lambda e, rA=rA: e.dma_start(
                    out=candA[:, :], in_=ccA_ap[rA:rA + C, :]),
                      reads=[(cc_a, None)], writes=[(candA, None)], dma=True)
                pr.op("sync", lambda e, rB=rB, cb_ap=cb_ap: e.dma_start(
                    out=candB[:, :], in_=cb_ap[rB:rB + C, :]),
                      reads=[(cb_t, None)], writes=[(candB, None)], dma=True)
                pr.op("vector", lambda e: e.tensor_scalar(
                    candA[:, :], candA[:, :], msel[:, 0:1], None, ALU.mult),
                      reads=[(candA, None), msel], writes=[(candA, None)])
                hb0 = hb * 16
                b0 = hb0
                while b0 < hb0 + 16:
                    tns, i0 = blocks[b0]
                    b1 = b0
                    while (b1 + 1 < hb0 + 16 and blocks[b1 + 1][0] is tns
                           and blocks[b1 + 1][1] == blocks[b1][1] + 1):
                        b1 += 1
                    nb = b1 - b0 + 1
                    hoff = b0 - hb0
                    dst = tns[:, i0 * BST:(i0 + nb) * BST].rearrange(
                        "p (h n) -> p h n", h=nb)[:, :, t * C:(t + 1) * C]
                    pr.op("vector", lambda e, dst=dst, hoff=hoff, nb=nb: e.scalar_tensor_tensor(
                        dst, candB[:, hoff * C:(hoff + nb) * C].rearrange(
                            "p (h n) -> p h n", h=nb),
                        msel[:, 1:2],
                        candA[:, hoff * C:(hoff + nb) * C].rearrange(
                            "p (h n) -> p h n", h=nb),
                        ALU.mult, ALU.add),
                          reads=[(candA, None), (candB, None), msel],
                          writes=[(tns, ("c", t, hb0 + hoff))], label="oblend")
                    b0 = b1 + 1
    else:
        o_full_ap = io["o_full"].ap()
        for t in range(NCHB):
            dst = blocks[0][0][:, :].rearrange("p (h n) -> p h n", h=NH)[
                :, :, t * C:(t + 1) * C]
            pr.op("sync", lambda e, t=t, dst=dst: e.dma_start(
                out=dst, in_=o_full_ap[t * C:(t + 1) * C, :].rearrange(
                    "p (h n) -> p h n", h=NH)),
                  writes=[(blocks[0][0], ("c", t))], dma=True)

    def wslot(ocw, q):
        return (ocw * 4 + q) % 8

    if fstage < 3:
        for t in range(NCHB):
            ob = x1s[0]
            pr.op("vector", lambda e, t=t, ob=ob: e.tensor_copy(
                ob[:, 0:512], blk_ap(0, t * C, (t + 1) * C).bitcast(F32)[:, 0:64].broadcast_to((C, 512)) if False else ob[:, 0:512]),
                  reads=[], writes=[(ob, None)]) if False else None
            pr.op("sync", lambda e, t=t: e.dma_start(
                out=y[t * C:(t + 1) * C, 0:512],
                in_=blk_ap(0, t * C, (t + 1) * C).bitcast(F32)[:, 0:64]) if False else e.dma_start(
                out=y[t * C:(t + 1) * C, 0:1024].bitcast(BF16)[:, 0:1024] if False else y[t * C:(t + 1) * C, 0:512],
                in_=x_own[t * C:(t + 1) * C, 0:512]),
                  reads=[(blocks[0][0], ("c", t))], writes=[(y, t)], dma=True)
        pr.op("gpsimd", lambda e: e.nop(), reads=[(y, None)])
        return io

    for ocw in range(8):
        for q in range(4):
            sl = wring[wslot(ocw, q)]
            un = wring_units[wslot(ocw, q)]
            pr.op("gpsimd", lambda e, sl=sl, ocw=ocw, q=q: e.dma_start(
                out=sl.rearrange("p (a n) -> p a n", a=8),
                in_=wo[q * 8 * C:(q + 1) * 8 * C,
                       ocw * 512:(ocw + 1) * 512].rearrange("(a p) n -> p a n", p=C)),
                  writes=[(un, None)], dma=True)
        for hd in range(NH):
            sl = wring[wslot(ocw, hd // 8)]
            un = wring_units[wslot(ocw, hd // 8)]
            for t in range(NCHB):
                mm(PS[t][:, :], PS[t], blk_ap(hd, t * C, (t + 1) * C),
                   sl[:, (hd % 8) * 512:(hd % 8 + 1) * 512],
                   [(blocks[hd][0], ("c", t, hd) if fused else ("c", t)),
                    (un, None)], hd == 0, hd == NH - 1, "womm")
        for t in range(NCHB):
            x1b = x1s[t % 2]
            xs = xsl[t % 2]
            pr.op("sync", lambda e, t=t, ocw=ocw, xs=xs: e.dma_start(
                out=xs[:, 0:512], in_=x_own[t * C:(t + 1) * C, ocw * 512:(ocw + 1) * 512]),
                  writes=[(xs, None)], dma=True)
            pr.op("vector", lambda e, t=t, x1b=x1b, xs=xs: e.tensor_tensor(
                x1b[:, 0:512], PS[t][:, :], xs[:, 0:512], ALU.add),
                  reads=[PS[t], (xs, None)], writes=[(x1b, None)])
            pr.op("sync", lambda e, t=t, ocw=ocw, x1b=x1b: e.dma_start(
                out=x1[t * C:(t + 1) * C, ocw * 512:(ocw + 1) * 512], in_=x1b[:, 0:512]),
                  reads=[(x1b, None)], writes=[(x1, (t, ocw))], dma=True)

    if fstage < 4:
        pr.op("gpsimd", lambda e: e.nop(), reads=[(x1, None)])
        io["y"] = y
        for t in range(NCHB):
            xs = xsl[0]
            pr.op("sync", lambda e, t=t, xs=xs: e.dma_start(
                out=xs[:, 0:512], in_=x1[t * C:(t + 1) * C, 0:512]),
                  reads=[(x1, None)], writes=[(xs, None)], dma=True)
            pr.op("sync", lambda e, t=t, xs=xs: e.dma_start(
                out=y[t * C:(t + 1) * C, 0:512], in_=xs[:, 0:512]),
                  reads=[(xs, None)], writes=[(y, t)], dma=True)
        pr.op("gpsimd", lambda e: e.nop(), reads=[(y, None)])
        return io

    pr.op("vector", lambda e: e.engine_nop(),
          writes=[(blocks[b][0], None) for b in (0, 16, 24)] +
                 [(t, None) for t in xn2tm] + [(x1r0, None)])

    for t in range(NCHB):
        xr = x1r0
        nb = xn2tm[t % 2]
        pr.op("sync", lambda e, t=t, xr=xr: e.dma_start(
            out=xr[:, :], in_=x1[t * C:(t + 1) * C, :]),
              reads=[(x1, None)], writes=[(xr, None)], dma=True)
        for sl8 in range(8):
            pr.op("vector", lambda e, xr=xr, sl8=sl8: e.bn_stats(
                stats[:, sl8 * 6:(sl8 + 1) * 6], xr[:, sl8 * 512:(sl8 + 1) * 512]),
                  reads=[(xr, None)], writes=[(stats, sl8)])
        pr.op("vector", lambda e: e.bn_aggr(mv[:], stats[:]),
              reads=[(stats, None)], writes=[mv])
        pr.op("scalar", lambda e: e.activation(rstd[:], mv[:, 1:2], AF.Ln,
                                               bias=c_eps2[:, 0:1]),
              reads=[mv, c_eps2], writes=[(rstd, "ln")])
        pr.op("scalar", lambda e: e.activation(rstd[:], rstd[:], AF.Exp, scale=-0.5),
              reads=[(rstd, "ln")], writes=[rstd])
        pr.op("vector", lambda e: e.tensor_scalar(nmean[:], mv[:, 0:1], -1.0, None,
                                                  ALU.mult),
              reads=[mv], writes=[nmean])
        pr.op("vector", lambda e, xr=xr, nb=nb: e.tensor_scalar(
            nb[:, :], xr[:, :], nmean[:, 0:1], rstd[:, 0:1], ALU.add, ALU.mult),
              reads=[(xr, None), nmean, rstd], writes=[(nb, None)])
        for dc in range(DC):
            if DMAT:
                q = "sync"
                pr.op(q, lambda e, nb=nb, dc=dc, t=t: e.dma_start(
                    out=blk_ap(dc, t * C, (t + 1) * C), in_=nb[:, dc * C:(dc + 1) * C],
                    transpose=True),
                      reads=[(nb, None)], writes=[(blocks[dc][0], ("x", dc, t))],
                      dma=True, label="xn2T")
            else:
                bk = PS[6 + (dc % 2)]
                pv = bk[:, 0:64].bitcast(BF16)
                pr.op("tensor", lambda e, nb=nb, dc=dc, pv=pv: e.transpose(
                    pv, nb[:, dc * C:(dc + 1) * C], c_idn2[:]),
                      reads=[(nb, None), c_idn2], writes=[bk], label="xn2T")
                pr.op("scalar", lambda e, dc=dc, t=t, pv=pv: e.copy(
                    blk_ap(dc, t * C, (t + 1) * C), pv),
                      reads=[bk], writes=[(blocks[dc][0], ("x", dc, t))])

    def xn2read(dc, t):
        return (blocks[dc][0], ("x", dc, t))

    for t in range(NCHB):
        for dc in range(DC):
            mm(PS[t][:, 0:136], PS[t], blk_ap(dc, t * C, (t + 1) * C),
               wrab_ap(dc),
               [xn2read(dc, t), wrab_unit(dc)], dc == 0, dc == DC - 1, "ramm")
    pr.op("vector", lambda e: e.engine_nop(),
          writes=[(hsb_t, None), (hwf_t, None)])
    for t in range(NCHB):
        pr.op("vector", lambda e, t=t: e.tensor_copy(
            logits_all[:, t * 8:(t + 1) * 8], PS[t][:, 0:8]),
              reads=[PS[t]], writes=[(logits_all, t)])
        pr.op("vector", lambda e, t=t: e.tensor_copy(
            hsb_t[:, t * C:(t + 1) * C], PS[t][:, 8:136]),
              reads=[PS[t]], writes=[(hsb_t, ("h", t))])
    for t in range(NCHB):
        pr.op("vector", lambda e, t=t: e.tensor_reduce(
            pmax[:], logits_all[:, t * 8:(t + 1) * 8], mybir.AxisListType.X, ALU.max),
              reads=[(logits_all, t)], writes=[pmax])
        pr.op("vector", lambda e, t=t: e.tensor_scalar(
            probs[:], logits_all[:, t * 8:(t + 1) * 8], pmax[:, 0:1],
            None, ALU.subtract),
              reads=[(logits_all, t), pmax], writes=[(probs, "s")])
        pr.op("scalar", lambda e: e.activation(probs[:], probs[:], AF.Exp,
                                               accum_out=psum_r[:, 0:1]),
              reads=[(probs, "s")], writes=[(probs, "e"), psum_r])
        pr.op("vector", lambda e: e.reciprocal(pmax[:], psum_r[:]),
              reads=[psum_r], writes=[(pmax, "r")])
        pr.op("vector", lambda e: e.tensor_scalar(probs[:], probs[:], pmax[:, 0:1],
                                                  None, ALU.mult),
              reads=[(probs, "e"), (pmax, "r")], writes=[(probs, None)])
        pr.op("vector", lambda e: e.tensor_reduce(pmax[:], probs[:],
                                                  mybir.AxisListType.X, ALU.max),
              reads=[(probs, None)], writes=[(pmax, "m1")])
        pr.op("vector", lambda e: e.tensor_scalar(oh1[:], probs[:], pmax[:, 0:1], None,
                                                  ALU.is_ge),
              reads=[(probs, None), (pmax, "m1")], writes=[oh1])
        pr.op("vector", lambda e: e.scalar_tensor_tensor(probs2[:], oh1[:], -2.0,
                                                         probs[:], ALU.mult, ALU.add),
              reads=[oh1, (probs, None)], writes=[probs2])
        pr.op("vector", lambda e: e.tensor_reduce(m2c[:], probs2[:],
                                                  mybir.AxisListType.X, ALU.max),
              reads=[probs2], writes=[m2c])
        pr.op("vector", lambda e: e.tensor_scalar(oh2[:], probs2[:], m2c[:, 0:1], None,
                                                  ALU.is_ge),
              reads=[probs2, m2c], writes=[oh2])
        pr.op("vector", lambda e: e.tensor_tensor(den[:], pmax[:], m2c[:], ALU.add),
              reads=[(pmax, "m1"), m2c], writes=[den])
        pr.op("vector", lambda e: e.reciprocal(rden[:], den[:]),
              reads=[den], writes=[rden])
        pr.op("vector", lambda e: e.tensor_tensor(w1c[:], pmax[:], rden[:], ALU.mult),
              reads=[(pmax, "m1"), rden], writes=[w1c])
        pr.op("vector", lambda e: e.tensor_tensor(w2c[:], m2c[:], rden[:], ALU.mult),
              reads=[m2c, rden], writes=[w2c])
        pr.op("vector", lambda e: e.tensor_scalar(oh1[:], oh1[:], w1c[:, 0:1], None,
                                                  ALU.mult),
              reads=[(oh1, None), w1c], writes=[(oh1, "w")])
        pr.op("vector", lambda e: e.scalar_tensor_tensor(cmb[:], oh2[:], w2c[:, 0:1],
                                                         oh1[:], ALU.mult, ALU.add),
              reads=[(oh2, None), w2c, (oh1, "w")], writes=[cmb])
        cmb3 = cmb[:, :].unsqueeze(2).broadcast_to((C, 8, 16))
        pr.op("vector", lambda e, t=t, cmb3=cmb3: e.scalar_tensor_tensor(
            hwb[:, 0:C].rearrange("p (a b) -> p a b", a=8),
            hsb_t[:, t * C:(t + 1) * C].rearrange("p (a b) -> p a b", a=8), 2.0,
            cmb3, ALU.mult, ALU.mult),
              reads=[(hsb_t, ("h", t)), cmb], writes=[(hwb, None)])
        if DMAT:
            pr.op("sync", lambda e, t=t: e.dma_start(
                out=hwf_t[:, t * C:(t + 1) * C], in_=hwb[:, 0:C], transpose=True),
                  reads=[(hwb, None)], writes=[(hwf_t, ("w", t))], dma=True, label="hwT")
        else:
            bk = PS[6 + t % 2]
            pv = bk[:, 64:128].bitcast(BF16)
            pr.op("tensor", lambda e, pv=pv: e.transpose(pv, hwb[:, 0:C], c_idn2[:]),
                  reads=[(hwb, None), c_idn2], writes=[bk], label="hwT")
            pr.op("scalar", lambda e, t=t, pv=pv: e.copy(
                hwf_t[:, t * C:(t + 1) * C], pv), reads=[bk], writes=[(hwf_t, ("w", t))])

    for fcw in range(8):
        for q in range(4):
            sl = wring[wslot(fcw, q)]
            un = wring_units[wslot(fcw, q)]
            pr.op("gpsimd", lambda e, sl=sl, fcw=fcw, q=q: e.dma_start(
                out=sl.rearrange("p (a n) -> p a n", a=8),
                in_=wbT[q * 8 * C:(q + 1) * 8 * C,
                        fcw * 512:(fcw + 1) * 512].rearrange("(a p) n -> p a n", p=C)),
                  writes=[(un, None)], dma=True)
        for dc in range(DC):
            sl = wring[wslot(fcw, dc // 8)]
            un = wring_units[wslot(fcw, dc // 8)]
            for t in range(NCHB):
                mm(PS[t][:, :], PS[t], blk_ap(dc, t * C, (t + 1) * C),
                   sl[:, (dc % 8) * 512:(dc % 8 + 1) * 512],
                   [xn2read(dc, t), (un, None)], dc == 0, False, "bmm")
        if bbr_per_fcw:
            pr.op("gpsimd", lambda e, fcw=fcw: e.dma_start(
                out=bbr[0:1, 0:512], in_=bbias[0:1, fcw * 512:(fcw + 1) * 512]),
                  writes=[bbr_u], dma=True)
        for t in range(NCHB):
            mm(PS[t][:, :], PS[t], hwf_t[:, t * C:(t + 1) * C],
               bcb[:, fcw * 512:(fcw + 1) * 512],
               [(hwf_t, ("w", t)), bcb_unit], False, False, "emm")
            bslice = bbr[0:1, 0:512] if bbr_per_fcw else bbr[0:1, fcw * 512:(fcw + 1) * 512]
            mm(PS[t][:, :], PS[t], c_one2[0:1, 0:C], bslice,
               [c_one2_u, bbr_u], False, True, "biasmm")
        for t in range(NCHB):
            xs = xsl[t % 2]
            pr.op("sync", lambda e, t=t, fcw=fcw, xs=xs: e.dma_start(
                out=xs[:, 0:512], in_=x1[t * C:(t + 1) * C, fcw * 512:(fcw + 1) * 512]),
                  reads=[(x1, None)], writes=[(xs, None)], dma=True)
            ob = outs[t % 2]
            pr.op("vector", lambda e, t=t, ob=ob, xs=xs: e.tensor_tensor(
                ob[:, 0:512], PS[t][:, :], xs[:, 0:512], ALU.add),
                  reads=[PS[t], (xs, None)], writes=[(ob, None)])
            pr.op("scalar", lambda e, t=t, fcw=fcw, ob=ob: e.dma_start(
                out=y[t * C:(t + 1) * C, fcw * 512:(fcw + 1) * 512], in_=ob[:, 0:512]),
                  reads=[(ob, None)], writes=[(y, (t, fcw))], dma=True)
    return io


def build_fused(nc, pr, T=2048, SCH=4):
    import os
    fstage = int(os.environ.get("FSTAGE", "9"))
    NCH = T // C
    half_rows = (NCH // 2) * C
    qr = (NCH // 4) * C
    cc_outA = nc.dram_tensor("cc_outA", [2 * half_rows, NQ * C], BF16, kind="Internal")
    cc_outB1 = nc.dram_tensor("cc_outB1", [2 * qr, NQ * C], BF16, kind="Internal")
    cc_outB2 = nc.dram_tensor("cc_outB2", [2 * qr, NQ * C], BF16, kind="Internal")

    def emit_coll(osap, rows, out_t):
        def coll(e):
            return e.collective_compute(
                "AllGather", mybir.AluOpType.bypass,
                replica_groups=[[0, 1], [2, 3], [4, 5], [6, 7]],
                ins=[osap[rows[0]:rows[1], :].opt()],
                outs=[out_t.ap()[:, :].opt()])
        return coll

    def coll_hook(pr_, o_send_h, after_chunk):
        osap = o_send_h.ap()
        if after_chunk == NCH // 2 - 1:
            pr_.op("gpsimd", emit_coll(osap, (0, half_rows), cc_outA),
                   reads=[(o_send_h, None)], writes=[(cc_outA, None)],
                   private=True, label="agA")
        elif after_chunk == 3 * NCH // 4 - 1:
            pr_.op("gpsimd", emit_coll(osap, (half_rows, half_rows + qr), cc_outB1),
                   reads=[(o_send_h, None)], writes=[(cc_outB1, None)],
                   private=True, label="agB1")
        elif after_chunk == NCH - 1:
            pr_.op("gpsimd", emit_coll(osap, (half_rows + qr, 2 * half_rows), cc_outB2),
                   reads=[(o_send_h, None)], writes=[(cc_outB2, None)],
                   private=True, label="agB2")

    io_a = build_phase_a(nc, pr, T=T, SCH=SCH, fused=True, coll_hook=coll_hook)
    pool = io_a["pool"]
    o_send = io_a["o_send"]
    cc_out = (cc_outA, cc_outB1, cc_outB2)

    if fstage < 2:
        dbg = nc.dram_tensor("y", [T // 2, HID], F32, kind="ExternalOutput").ap()
        sb = nc.alloc_sbuf_tensor("dbgsb", [C, 512], F32)
        pr.op("sync", lambda e: e.dma_start(out=sb[:, 0:512],
                                            in_=cc_out.ap()[0:C, 0:512]),
              reads=[(cc_out, None)], writes=[sb], dma=True)
        pr.op("sync", lambda e: e.dma_start(out=dbg[0:C, 0:512], in_=sb[:, 0:512]),
              reads=[sb], writes=[("yo", None)], dma=True)
        pr.op("gpsimd", lambda e: e.nop(), reads=[("yo", None)])
        return io_a, {"y": dbg}
    io_b = build_phase_b(nc, pr, NTOK=T // 2, pool=pool, o_parts=(o_send, cc_out),
                         fstage=fstage)
    pr.op("gpsimd", lambda e: e.nop(), reads=[io_b["y"]])
    return io_a, io_b


import concourse.bass as _bass
from concourse.bass_utils import run_bass_kernel_spmd as _run_spmd

_CACHE = {}


def _build_fused():
    if "f" not in _CACHE:
        nc = _bass.Bass("TRN2", target_bir_lowering=False, debug=False)
        pr = Prog(nc)
        build_fused(nc, pr, T=2048, SCH=4)
        pr.finalize_and_emit()
        _CACHE["f"] = nc
    return _CACHE["f"]


def kernel(L0_attn_hidden_states, L0_attn_Wq, L0_attn_Wk, L0_attn_Wv, L0_attn_Wo,
           L0_ffn_W_base, L0_ffn_b_base, L0_ffn_W_router, L0_ffn_A, L0_ffn_B):
    x = np.ascontiguousarray(np.asarray(L0_attn_hidden_states, dtype=np.float32))
    Wq = np.asarray(L0_attn_Wq, dtype=np.float32)
    Wk = np.asarray(L0_attn_Wk, dtype=np.float32)
    Wv = np.asarray(L0_attn_Wv, dtype=np.float32)
    Wo = np.ascontiguousarray(np.asarray(L0_attn_Wo, dtype=np.float32))
    Wb = np.asarray(L0_ffn_W_base, dtype=np.float32)
    bb = np.asarray(L0_ffn_b_base, dtype=np.float32)
    Wr = np.asarray(L0_ffn_W_router, dtype=np.float32)
    Af = np.asarray(L0_ffn_A, dtype=np.float32)
    Bf = np.asarray(L0_ffn_B, dtype=np.float32)

    cons = host_constants()
    cosf, sinf = rope_tables_fm(2048)
    wra = np.ascontiguousarray(np.concatenate(
        [Wr.T, Af.reshape(128, HID).T], axis=1).astype(np.float32))
    bcc = np.ascontiguousarray(Bf.transpose(0, 2, 1).reshape(128, HID).astype(np.float32))
    shared = {"cosf": cosf, "sinf": sinf, "wo": Wo,
              "wbT": np.ascontiguousarray(Wb.T), "wra": wra, "bc": bcc,
              "bbias": np.ascontiguousarray(bb[None, :]),
              "onerow": np.ones((1, 128), np.float32)}
    for k in ["triUs", "oness", "sixcol", "trimask", "maskA", "maskB", "mrot",
              "idn", "twocol", "epscol"]:
        shared[k] = cons[k]

    in_maps = []
    for c in range(8):
        bi, hh = c >> 1, c & 1
        m = dict(shared)
        m["x"] = np.ascontiguousarray(x[bi])
        m["wq"] = np.ascontiguousarray(Wq[:, hh * 2048:(hh + 1) * 2048])
        m["wkv"] = np.ascontiguousarray(
            np.concatenate([Wk[:, hh * 512:(hh + 1) * 512],
                            Wv[:, hh * 512:(hh + 1) * 512]], axis=1))
        m["x_own"] = np.ascontiguousarray(x[bi, hh * 1024:(hh + 1) * 1024])
        ms = np.zeros((128, 2), np.float32)
        ms[:, 0] = 1.0 if hh == 0 else 0.0
        ms[:, 1] = 1.0 if hh == 1 else 0.0
        m["msel"] = ms
        in_maps.append(m)

    res = _run_spmd(_build_fused(), in_maps, core_ids=list(range(8)))

    out = np.zeros((4, 2048, 4096), np.float32)
    for c in range(8):
        bi, hh = c >> 1, c & 1
        out[bi, hh * 1024:(hh + 1) * 1024] = np.asarray(res.results[c]["y"])
    return out


# version 06
# speedup vs baseline: 1.1571x; 1.0050x over previous
import concourse.bass as bass
import concourse.mybir as mybir

ENGINES = ("tensor", "vector", "scalar", "gpsimd", "sync")


def _tname(x):
    if isinstance(x, str):
        return x
    t = getattr(x, "tensor", None)
    if t is not None:
        return t.name
    return x.name


class _Op:
    __slots__ = ("idx", "engine", "emit", "reads", "writes", "dma", "deps",
                 "needs_inc", "rank", "label", "private", "switch_n")

    def __init__(self, idx, engine, emit, reads, writes, dma, label,
                 private=False, switch_n=0):
        self.idx = idx
        self.engine = engine
        self.emit = emit
        self.reads = reads
        self.writes = writes
        self.dma = dma
        self.deps = set()
        self.needs_inc = False
        self.rank = None
        self.label = label
        self.private = private
        self.switch_n = switch_n


class Prog:
    def __init__(self, nc):
        self.nc = nc
        self.ops = []
        self.state = {}
        self.alias = {}
        self.psum_names = set()

    def register_psum(self, tensors):
        for t in tensors:
            self.psum_names.add(_tname(t))

    def add_alias(self, virt, base):
        self.alias[_tname(virt)] = _tname(base)

    def _norm(self, u):
        if isinstance(u, tuple):
            x, tag = u
        else:
            x, tag = u, None
        n = _tname(x)
        n = self.alias.get(n, n)
        return (n, tag)

    def _recs(self, name):
        return self.state.setdefault(name, {})

    def _dep_read(self, op, name, tag):
        recs = self._recs(name)
        if tag is None:
            for t, rec in recs.items():
                if rec["w"] is not None:
                    op.deps.add(rec["w"])
        else:
            for t in (tag, None):
                rec = recs.get(t)
                if rec and rec["w"] is not None:
                    op.deps.add(rec["w"])

    def _dep_write(self, op, name, tag):
        recs = self._recs(name)
        if tag is None:
            for t, rec in recs.items():
                if rec["w"] is not None:
                    op.deps.add(rec["w"])
                op.deps.update(rec["r"])
        else:
            for t in (tag, None):
                rec = recs.get(t)
                if rec:
                    if rec["w"] is not None:
                        op.deps.add(rec["w"])
                    op.deps.update(rec["r"])

    def _mark_read(self, idx, name, tag):
        rec = self._recs(name).setdefault(tag, {"w": None, "r": set()})
        rec["r"].add(idx)

    def _mark_write(self, idx, name, tag):
        recs = self._recs(name)
        if tag is None:
            recs.clear()
        recs[tag] = {"w": idx, "r": set()}

    def op(self, engine, emit, reads=(), writes=(), dma=False, label="",
           private=False, switch_n=0):
        idx = len(self.ops)
        nreads = [self._norm(u) for u in reads]
        nwrites = [self._norm(u) for u in writes]
        promoted = [u for u in nreads if u[0] in self.psum_names]
        if promoted:
            nreads = [u for u in nreads if u[0] not in self.psum_names]
            nwrites = nwrites + promoted
        o = _Op(idx, engine, emit, nreads, nwrites, dma, label,
                private=private, switch_n=switch_n)
        for n, t in o.reads:
            self._dep_read(o, n, t)
        for n, t in o.writes:
            self._dep_write(o, n, t)
        o.deps.discard(idx)
        if engine == "tensor":
            o.deps = {d for d in o.deps
                      if not (self.ops[d].engine == "tensor" and not self.ops[d].dma)}
        for n, t in o.reads:
            self._mark_read(idx, n, t)
        for n, t in o.writes:
            self._mark_write(idx, n, t)
        self.ops.append(o)
        return idx

    def barrier_unit(self, name):
        return (name, None)

    NSEM = 8

    def finalize_and_emit(self):
        nc = self.nc
        R = self.NSEM
        for o in self.ops:
            for d in o.deps:
                self.ops[d].needs_inc = True
        eng_count = {e: 0 for e in ENGINES}
        dma_count = {e: 0 for e in ENGINES}
        self._psems = {}
        for o in self.ops:
            if o.private:
                o.rank = None
                o.needs_inc = True
            elif o.switch_n:
                slots = []
                for _ in range(o.switch_n):
                    j = dma_count[o.engine]
                    dma_count[o.engine] += 1
                    slots.append((j % R, j // R + 1))
                o.rank = slots
                o.needs_inc = True
            elif o.dma:
                j = dma_count[o.engine]
                dma_count[o.engine] += 1
                o.rank = (j % R, j // R + 1)
                o.needs_inc = True
            elif o.needs_inc:
                eng_count[o.engine] += 1
                o.rank = eng_count[o.engine]
        self._esem = {}
        self._qsem = {}
        import contextlib
        self._stack = contextlib.ExitStack()
        for o in self.ops:
            if o.private:
                self._psems[o.idx] = self._stack.enter_context(
                    nc.semaphore(f"sp_{o.idx}"))
        for e in ENGINES:
            if eng_count[e]:
                self._esem[e] = self._stack.enter_context(nc.semaphore(f"se_{e}"))
            if dma_count[e]:
                self._qsem[e] = [
                    self._stack.enter_context(nc.semaphore(f"sq_{e}{i}"))
                    for i in range(min(R, dma_count[e]))
                ]
        ops_by_engine = {e: [o for o in self.ops if o.engine == e] for e in ENGINES}

        block_cm = nc.Block()
        block = block_cm.__enter__()

        def emit_stream(eng_name, eng):
            hwm = {}
            def do_wait(key, val):
                if hwm.get(key, 0) >= val:
                    return
                kind = key[0]
                if kind == "e":
                    sem = self._esem[key[1]]
                elif kind == "p":
                    sem = self._psems[key[1]]
                else:
                    sem = self._qsem[key[1]][key[2]]
                eng.wait_ge(sem, val)
                hwm[key] = val
            for o in ops_by_engine[eng_name]:
                waits = {}
                for d in o.deps:
                    dop = self.ops[d]
                    if (dop.engine == eng_name and not dop.dma and not dop.private
                            and not dop.switch_n and eng_name == "tensor"):
                        continue
                    if dop.private:
                        kvs = [(("p", d), 1)]
                    elif dop.switch_n:
                        kvs = [(("q", dop.engine, s), 16 * k) for s, k in dop.rank]
                    elif dop.dma:
                        slot, k = dop.rank
                        kvs = [(("q", dop.engine, slot), 16 * k)]
                    else:
                        kvs = [(("e", dop.engine), dop.rank)]
                    for key, val in kvs:
                        if val > waits.get(key, 0):
                            waits[key] = val
                if o.dma and not o.switch_n and not o.private:
                    slot, k = o.rank
                    if k > 1:
                        key = ("q", o.engine, slot)
                        v = 16 * (k - 1)
                        if v > waits.get(key, 0):
                            waits[key] = v
                if o.switch_n:
                    for slot, k in o.rank:
                        if k > 1:
                            key = ("q", o.engine, slot)
                            v = 16 * (k - 1)
                            if v > waits.get(key, 0):
                                waits[key] = v
                for key in sorted(waits):
                    do_wait(key, waits[key])
                if o.private:
                    inst = o.emit(eng)
                    inst.then_inc(self._psems[o.idx], 1)
                elif o.switch_n:
                    slots = list(o.rank)
                    qs = self._qsem[o.engine]
                    if o.label.endswith("_nosw"):
                        it = iter(slots)
                        def inc(inst, _it=it, _qs=qs):
                            s, k = next(_it)
                            inst.then_inc(_qs[s], 16)
                        o.emit(eng, 0, inc)
                    else:
                        for arm in eng.switch_core_id(n=512):
                            me = int(arm.logical) % 8
                            if int(arm.logical) < 8:
                                it = iter(slots)
                                def inc(inst, _it=it, _qs=qs):
                                    s, k = next(_it)
                                    inst.then_inc(_qs[s], 16)
                                o.emit(eng, me, inc)
                elif o.dma:
                    slot, k = o.rank
                    inst = o.emit(eng)
                    inst.then_inc(self._qsem[o.engine][slot], 16)
                else:
                    inst = o.emit(eng)
                    if o.needs_inc:
                        inst.then_inc(self._esem[o.engine], 1)

        for e in ENGINES:
            if not ops_by_engine[e]:
                continue
            dec = getattr(block, e)
            def body(eng, _e=e):
                emit_stream(_e, eng)
            dec(body)
        block_cm.__exit__(None, None, None)

    def final_wait(self, engine="sync"):
        pass


import numpy as np
import ml_dtypes
import concourse.mybir as mybir
import os
DMAT = os.environ.get("DMAT", "0") == "1"

F32 = mybir.dt.float32
BF16 = mybir.dt.bfloat16
AF = mybir.ActivationFunctionType
ALU = mybir.AluOpType
bfd = ml_dtypes.bfloat16

D = 128
HID = 4096
DC = HID // 128
NQ = 16
NKV = 4
GRP = 4
C = 128
SCALE = float(1.0 / np.sqrt(np.float32(D)))
NEG = -1e9


def host_constants():
    tri01 = np.triu(np.ones((C, C), np.float32))
    tt = np.arange(C)[None, :]
    cons = {}
    cons["triUs"] = np.ascontiguousarray(tri01 * 0.0625)
    cons["oness"] = np.full((C, C), 0.0625, np.float32)
    cons["sixcol"] = np.full((C, 1), 0.0625, np.float32)
    cons["trimask"] = np.ascontiguousarray(np.tile(tri01, (1, 4)))
    kkA = np.arange(64)[:, None]
    mA_full = np.full((C, C), NEG, np.float32)
    mA_full[64:128, :] = np.where(kkA >= tt, 0.0, NEG)
    cons["maskA"] = np.ascontiguousarray(np.tile(mA_full, (1, 4)))
    kkB = np.arange(C)[:, None]
    mB = np.where((kkB <= tt) & (kkB >= tt - 64), 0.0, NEG).astype(np.float32)
    cons["maskB"] = np.ascontiguousarray(np.tile(mB, (1, 4)))
    M = np.zeros((C, C), np.float32)
    for p in range(64):
        M[p + 64, p] = -1.0
        M[p, p + 64] = 1.0
    cons["mrot"] = M
    cons["idn"] = np.eye(C, dtype=np.float32)
    cons["twocol"] = np.full((C, 1), 2.0, np.float32)
    cons["epscol"] = np.full((C, 1), 1e-5, np.float32)
    return cons


def rope_tables_fm(T):
    half = np.arange(D // 2, dtype=np.float32)
    inv = 1.0 / 10000.0 ** (2.0 * half / D)
    fr = np.arange(T, dtype=np.float32)[:, None] * inv[None, :]
    emb = np.concatenate([fr, fr], axis=-1)
    return (np.ascontiguousarray(np.cos(emb).T.astype(np.float32)),
            np.ascontiguousarray(np.sin(emb).T.astype(np.float32)))


def build_phase_a(nc, pr, T=2048, SCH=4, KVG=4, stage=9, fused=False, coll_hook=None):
    NCH = T // C
    NSC = NCH // SCH
    SCT = SCH * C
    half = NCH // 2
    io = {}

    def din(name, shape, dt=F32, kind="ExternalInput"):
        h = nc.dram_tensor(name, shape, dt, kind=kind)
        io[name] = h
        return h.ap()

    x = din("x", [T, HID])
    wq = din("wq", [HID, NQ * D])
    wkv = din("wkv", [HID, 1024])
    cosf = din("cosf", [D, T])
    sinf = din("sinf", [D, T])
    triUs_d = din("triUs", [C, C])
    oness_d = din("oness", [C, C])
    sixcol_d = din("sixcol", [C, 1])
    trimask_d = din("trimask", [C, 512])
    maskA_d = din("maskA", [C, 512])
    maskB_d = din("maskB", [C, 512])
    mrot_d = din("mrot", [C, C])
    idn_d = din("idn", [C, C])
    twocol_d = din("twocol", [C, 1])
    epscol_d = din("epscol", [C, 1])
    if fused:
        o_send = nc.dram_tensor("o_send", [NCH * C, NQ * C], BF16, kind="Internal")
        io["o_send"] = o_send
        o_lo = o_hi = o_send.ap()
    else:
        o_lo = din("o_lo", [half * C, NQ * C], BF16, kind="ExternalOutput")
        o_hi = din("o_hi", [half * C, NQ * C], BF16, kind="ExternalOutput")

    A = lambda name, shape, dt: nc.alloc_sbuf_tensor(name, shape, dt)
    c_triUs = A("c_triUs", [C, C], F32)
    c_oness = A("c_oness", [C, C], F32)
    c_sixcol = A("c_sixcol", [C, 1], F32)
    c_trimask = A("c_trimask", [C, 512], F32)
    c_maskA = A("c_maskA", [C, 512], F32)
    c_maskB = A("c_maskB", [C, 512], F32)
    c_mrot = A("c_mrot", [C, C], BF16)
    c_idn = A("c_idn", [C, C], BF16)
    c_two = A("c_two", [C, 1], BF16)
    c_eps = A("c_eps", [C, 1], F32)
    c_cos = A("c_cos", [D, T], BF16)
    c_sin = A("c_sin", [D, T], BF16)
    for t_, src in [(c_triUs, triUs_d), (c_oness, oness_d), (c_sixcol, sixcol_d),
                    (c_trimask, trimask_d), (c_maskA, maskA_d), (c_maskB, maskB_d),
                    (c_eps, epscol_d)]:
        pr.op("sync", lambda e, t_=t_, s=src: e.dma_start(out=t_[:], in_=s[:, :]),
              writes=[t_], dma=True)
    for t_, src in [(c_mrot, mrot_d), (c_idn, idn_d), (c_two, twocol_d),
                    (c_cos, cosf), (c_sin, sinf)]:
        pr.op("gpsimd", lambda e, t_=t_, s=src: e.dma_start(out=t_[:], in_=s[:, :]),
              writes=[t_], dma=True)

    x_tm = [A("x_tm0", [C, HID], F32)]
    xn_tm = [A(f"xn_tm{i}", [C, HID], BF16) for i in range(2)]
    xn_fm = A("xn_fm", [C, DC * SCT], BF16)
    wqb = [A(f"wqb{i}", [C, DC * D], BF16) for i in range(2)]
    wkvb = [A(f"wkvb{i}", [C, KVG * 1024], BF16) for i in range(2)]
    k_tm = A("k_tm", [C, SCH * 512], F32)
    expq = A("expq", [C, NQ * SCT], BF16)
    rq_ar = A("rq_ar", [C, NQ * SCT], BF16)
    q_tmp = [A(f"q_tmp{i}", [C, SCT], BF16) for i in range(2)]
    rt1 = [A("rt1_0", [C, SCT], BF16)]
    NS = SCH + 1
    v_ring = A("v_ring", [C, NKV * NS * C], BF16)
    rk_ring = A("rk_ring", [C, NKV * 2 * C], BF16)
    S_f32 = A("S_f32", [C, NKV * C], F32)
    S_bf = A("S_bf", [C, NKV * C], BF16)
    stats = A("stats", [C, 48], F32)
    mv = A("mv", [C, 2], F32)
    rstd = A("rstd", [C, 1], F32)
    nmean = A("nmean", [C, 1], F32)
    gs0 = A("gs0", [C, 512], F32)
    gs1 = A("gs1", [C, 512], F32)
    gtm = A("gtm", [C, 512], F32)
    expk = A("expk", [C, 512], F32)
    ksum = A("ksum", [C, NKV], F32)
    rk_col = A("rk_col", [C, NKV], F32)
    enG = A("enG", [C, 512], F32)
    eBC = A("eBC", [C, 512], BF16)
    lamc = A("lamc", [C, NKV], F32)
    eGf = A("eGf", [C, 512], BF16)
    kt1 = A("kt1", [C, 512], F32)
    ktm = A("ktm", [C, 512], BF16)
    khm = A("khm", [C, 512], BF16)
    kb16 = A("kb16", [C, 512], BF16)
    kfm = A("kfm", [C, NKV * C], BF16)
    ktfm = A("ktfm", [C, NKV * C], BF16)
    rk1 = A("rk1", [C, 512], BF16)
    rk2 = A("rk2", [C, 512], BF16)
    qt_ar = A("qt_ar", [C, NQ * C], BF16)
    rq_col = A("rq_col", [C, NQ], F32)
    ptm = A("ptm", [C, 512], BF16)
    wtmp = A("wtmp", [C, 512], F32)
    expsA = A("expsA", [C, 512], BF16)
    expsB = A("expsB", [C, 512], BF16)
    rw_col = A("rw_col", [C, NQ], F32)
    t_y = A("t_y", [C, C], F32)
    o_tm = A("o_tm", [C, NQ * C], BF16)
    ostg = [A(f"ostg{i}", [C, NQ * C], BF16) for i in range(2)]

    PS = [nc.alloc_psum_tensor(f"bank{i}", [C, 512], F32) for i in range(8)]
    pr.register_psum(PS)

    def mm(out_ap, bank, lhsT, rhs, reads, start, stop, label="", tp=None):
        pr.op("tensor",
              lambda e: e.matmul(out_ap, lhsT, rhs, start=start, stop=stop,
                                 tile_position=tp, skip_group_check=True),
              reads=reads, writes=[bank], label=label)

    pr.op("vector", lambda e: e.memset(o_tm[:], 0.0), writes=[(o_tm, None)])
    pr.op("vector", lambda e: e.memset(S_f32[:], 0.0), writes=[S_f32])
    pr.op("vector", lambda e: e.memset(S_bf[:], 0.0), writes=[S_bf])

    for sc in range(NSC):
        ts0 = sc * SCT
        for cl in range(SCH):
            c = sc * SCH + cl
            xb = x_tm[0]
            nb = xn_tm[c % 2]
            for xh in range(2):
                pr.op("sync", lambda e, xb=xb, c=c, xh=xh: e.dma_start(
                    out=xb[:, xh * 2048:(xh + 1) * 2048],
                    in_=x[c * C:(c + 1) * C, xh * 2048:(xh + 1) * 2048]),
                      writes=[(xb, xh)], dma=True)
                for sl4 in range(4):
                    sl8 = xh * 4 + sl4
                    pr.op("vector", lambda e, xb=xb, sl8=sl8, xh=xh: e.bn_stats(
                        stats[:, sl8 * 6:(sl8 + 1) * 6],
                        xb[:, sl8 * 512:(sl8 + 1) * 512]),
                          reads=[(xb, xh)], writes=[(stats, sl8)])
            pr.op("vector", lambda e: e.bn_aggr(mv[:], stats[:]),
                  reads=[(stats, None)], writes=[mv])
            pr.op("scalar", lambda e: e.activation(rstd[:], mv[:, 1:2], AF.Ln, bias=c_eps[:, 0:1]),
                  reads=[mv, c_eps], writes=[(rstd, "ln")])
            pr.op("scalar", lambda e: e.activation(rstd[:], rstd[:], AF.Exp, scale=-0.5),
                  reads=[(rstd, "ln")], writes=[rstd])
            pr.op("vector", lambda e: e.tensor_scalar(nmean[:], mv[:, 0:1], -1.0, None,
                                                      ALU.mult),
                  reads=[mv], writes=[nmean])
            for xh in range(2):
                pr.op("vector", lambda e, xb=xb, nb=nb, xh=xh: e.tensor_scalar(
                    nb[:, xh * 2048:(xh + 1) * 2048], xb[:, xh * 2048:(xh + 1) * 2048],
                    nmean[:, 0:1], rstd[:, 0:1], ALU.add, ALU.mult),
                      reads=[(xb, xh), nmean, rstd], writes=[(nb, xh)])
            for dc in range(DC):
                if DMAT:
                    q = "sync"
                    pr.op(q, lambda e, nb=nb, dc=dc, cl=cl: e.dma_start(
                        out=xn_fm[:, dc * SCT + cl * C: dc * SCT + (cl + 1) * C],
                        in_=nb[:, dc * C:(dc + 1) * C], transpose=True),
                          reads=[(nb, dc // 16)], writes=[(xn_fm, (dc, cl))],
                          dma=True, label="xnT")
                else:
                    bk = PS[6 + (dc % 2)]
                    pv = bk[:, 0:64].bitcast(BF16)
                    pr.op("tensor", lambda e, nb=nb, dc=dc, pv=pv: e.transpose(
                        pv, nb[:, dc * C:(dc + 1) * C], c_idn[:]),
                          reads=[(nb, dc // 16), c_idn], writes=[bk], label="xnT")
                    pr.op("scalar", lambda e, dc=dc, cl=cl, pv=pv: e.copy(
                        xn_fm[:, dc * SCT + cl * C: dc * SCT + (cl + 1) * C], pv),
                          reads=[bk], writes=[(xn_fm, (dc, cl))])
        if stage < 2:
            continue
        for gg in range(DC // KVG):
            wb = wkvb[gg % 2]
            pr.op("gpsimd", lambda e, wb=wb, gg=gg: e.dma_start(
                out=wb[:, :].rearrange("p (a n) -> p a n", a=KVG),
                in_=wkv[gg * KVG * C:(gg + 1) * KVG * C, :].rearrange(
                    "(a p) n -> p a n", p=C)),
                  writes=[wb], dma=True)
            for di in range(KVG):
                dc = gg * KVG + di
                for t in range(SCH):
                    lhs = xn_fm[:, dc * SCT + t * C: dc * SCT + (t + 1) * C]
                    mm(PS[t][:, :], PS[t], lhs, wb[:, di * 1024: di * 1024 + 512],
                       [(xn_fm, (dc, t)), wb], dc == 0, dc == DC - 1, "kmm")
                    mm(PS[4 + t][:, :], PS[4 + t], lhs,
                       wb[:, di * 1024 + 512: (di + 1) * 1024],
                       [(xn_fm, (dc, t)), wb], dc == 0, dc == DC - 1, "vmm")
        for t in range(SCH):
            c = sc * SCH + t
            pr.op("vector", lambda e, t=t: e.tensor_copy(
                k_tm[:, t * 512:(t + 1) * 512], PS[t][:, :]),
                  reads=[PS[t]], writes=[(k_tm, t)])
            dst = v_ring[:, :].rearrange("p (g s) -> p g s", g=NKV)[
                :, :, (c % NS) * C:(c % NS + 1) * C]
            pr.op("vector", lambda e, t=t, dst=dst: e.tensor_copy(
                dst, PS[4 + t][:, :].rearrange("p (g s) -> p g s", g=NKV)),
                  reads=[PS[4 + t]], writes=[(v_ring, c % NS)])
        for h in range(NQ if stage >= 3 else 0):
            wb = wqb[h % 2]
            for qg in range(8):
                pr.op("gpsimd", lambda e, wb=wb, h=h, qg=qg: e.dma_start(
                    out=wb[:, qg * 4 * D:(qg + 1) * 4 * D].rearrange(
                        "p (a n) -> p a n", a=4),
                    in_=wq[qg * 4 * C:(qg + 1) * 4 * C,
                           h * D:(h + 1) * D].rearrange("(a p) n -> p a n", p=C)),
                      writes=[(wb, qg)], dma=True)
            import os
            qsub = int(os.environ.get("QSUB", "9"))
            psq = PS[h % 2]
            for dc in range(DC if qsub >= 1 else 0):
                mm(psq[:, 0:SCT], psq, wb[:, dc * D:(dc + 1) * D],
                   xn_fm[:, dc * SCT:(dc + 1) * SCT],
                   [wb, (xn_fm, None)], dc == 0, dc == DC - 1, "qmm")
            if qsub < 2:
                continue
            pr.op("scalar", lambda e, h=h, psq=psq: e.activation(
                expq[:, h * SCT:(h + 1) * SCT], psq[:, 0:SCT], AF.Exp),
                  reads=[psq], writes=[(expq, h)])
            qt = q_tmp[h % 2]
            if qsub < 3:
                continue
            pr.op("vector", lambda e, qt=qt, psq=psq: e.tensor_copy(qt[:], psq[:, 0:SCT]),
                  reads=[psq], writes=[qt])
            if qsub < 4:
                continue
            rb = PS[2 + h % 2]
            rbv = rb[:, 0:SCT]
            mm(rbv, rb, c_mrot[:], qt[:], [c_mrot, qt], True, True, "rotq")
            if qsub < 5:
                continue
            t1 = rt1[0]
            pr.op("vector", lambda e, qt=qt, t1=t1, ts0=ts0: e.tensor_tensor(
                t1[:], qt[:], c_cos[:, ts0:ts0 + SCT], ALU.mult),
                  reads=[qt, c_cos], writes=[t1])
            pr.op("vector", lambda e, h=h, rbv=rbv, rb=rb, ts0=ts0: e.tensor_tensor(
                rq_ar[:, h * SCT:(h + 1) * SCT], rbv, c_sin[:, ts0:ts0 + SCT],
                ALU.mult),
                  reads=[rb, c_sin], writes=[(rq_ar, (h, 0))], label="rq_sin")
            pr.op("vector", lambda e, h=h, t1=t1: e.tensor_tensor(
                rq_ar[:, h * SCT:(h + 1) * SCT], rq_ar[:, h * SCT:(h + 1) * SCT],
                t1[:], ALU.add),
                  reads=[(rq_ar, (h, 0)), t1], writes=[(rq_ar, h)], label="rq_add")
        for cl in range(SCH if stage >= 4 else 0):
            c = sc * SCH + cl
            ks = k_tm[:, cl * 512:(cl + 1) * 512]
            kread = (k_tm, cl)
            cur = c % 2
            prv = (c - 1) % 2
            vcur = c % NS
            vprv = (c - 1) % NS
            pr.op("scalar", lambda e, ks=ks: e.activation(gs0[:], ks, AF.Abs),
                  reads=[kread], writes=[gs0])
            pr.op("scalar", lambda e: e.activation(gs1[:], gs0[:], AF.Exp, scale=-1.0),
                  reads=[gs0], writes=[gs1])
            pr.op("scalar", lambda e: e.activation(gs1[:], gs1[:], AF.Ln, bias=1.0),
                  reads=[(gs1, None)], writes=[(gs1, "l")])
            pr.op("vector", lambda e, ks=ks: e.tensor_scalar(
                gs0[:], ks, 0.0, None, ALU.min), reads=[kread], writes=[(gs0, "m")])
            pr.op("vector", lambda e: e.tensor_tensor(
                gtm[:], gs0[:], gs1[:], ALU.subtract),
                  reads=[(gs0, "m"), (gs1, "l")], writes=[gtm])
            for g in range(NKV):
                pr.op("scalar", lambda e, g=g, ks=ks: e.activation(
                    expk[:, g * C:(g + 1) * C], ks[:, g * C:(g + 1) * C], AF.Exp,
                    accum_out=ksum[:, g:g + 1]),
                      reads=[kread], writes=[(expk, g), (ksum, g)])
            pr.op("vector", lambda e: e.reciprocal(rk_col[:], ksum[:]),
                  reads=[(ksum, None)], writes=[rk_col])
            mm(PS[0][:, :], PS[0], c_triUs[:], gtm[:], [c_triUs, gtm], True, True, "Gtm")
            mm(PS[1][:, :], PS[1], c_oness[:], gtm[:], [c_oness, gtm], True, True, "GBC")
            for g in range(NKV):
                mm(PS[2][:, g:g + 1], PS[2], gtm[:, g * C:(g + 1) * C], c_sixcol[:],
                   [gtm, c_sixcol], True, True, "lam")
                mm(PS[3][:, g * C:(g + 1) * C], PS[3], gtm[:, g * C:(g + 1) * C],
                   c_triUs[:], [gtm, c_triUs], g == 0, g == NKV - 1, "Gfm")
            pr.op("scalar", lambda e: e.activation(enG[:], PS[0][:, :], AF.Exp, scale=-1.0),
                  reads=[PS[0]], writes=[enG])
            pr.op("scalar", lambda e: e.activation(eBC[:], PS[1][:, :], AF.Exp),
                  reads=[PS[1]], writes=[eBC])
            pr.op("scalar", lambda e: e.activation(lamc[:], PS[2][:, 0:NKV], AF.Exp),
                  reads=[PS[2]], writes=[lamc])
            pr.op("scalar", lambda e: e.activation(eGf[:], PS[3][:, :], AF.Exp),
                  reads=[PS[3]], writes=[eGf])
            if stage < 5:
                continue
            pr.op("vector", lambda e: e.tensor_tensor(kt1[:], expk[:], enG[:], ALU.mult),
                  reads=[(expk, None), enG], writes=[kt1])
            for g in range(NKV):
                pr.op("vector", lambda e, g=g: e.tensor_scalar(
                    ktm[:, g * C:(g + 1) * C], kt1[:, g * C:(g + 1) * C],
                    rk_col[:, g:g + 1], None, ALU.mult),
                      reads=[kt1, rk_col], writes=[(ktm, g)])
            pr.op("vector", lambda e: e.tensor_tensor(khm[:], ktm[:], eBC[:], ALU.mult),
                  reads=[(ktm, None), eBC], writes=[khm])
            pr.op("vector", lambda e, ks=ks: e.tensor_copy(kb16[:], ks),
                  reads=[kread], writes=[kb16])
            for g in range(NKV):
                bk = PS[6 + g % 2]
                pv = bk[:, 0:64].bitcast(BF16)
                pr.op("tensor", lambda e, g=g, pv=pv: e.transpose(
                    pv, kb16[:, g * C:(g + 1) * C], c_idn[:]),
                      reads=[kb16, c_idn], writes=[bk], label="kT")
                pr.op("scalar", lambda e, g=g, pv=pv: e.copy(
                    kfm[:, g * C:(g + 1) * C], pv), reads=[bk], writes=[(kfm, g)])
                bk2 = PS[6 + (g + 1) % 2]
                pv2 = bk2[:, 64:128].bitcast(BF16)
                pr.op("tensor", lambda e, g=g, pv2=pv2: e.transpose(
                    pv2, ktm[:, g * C:(g + 1) * C], c_idn[:]),
                      reads=[(ktm, g), c_idn], writes=[bk2], label="KtT")
                pr.op("scalar", lambda e, g=g, pv2=pv2: e.copy(
                    ktfm[:, g * C:(g + 1) * C], pv2), reads=[bk2], writes=[(ktfm, g)])
            rb = PS[4]
            rbv = rb[:, 0:512]
            mm(rbv, rb, c_mrot[:], kfm[:, :], [c_mrot, (kfm, None)], True, True, "rotk")
            cos3 = c_cos[:, c * C:(c + 1) * C].unsqueeze(1).broadcast_to((C, NKV, C))
            sin3 = c_sin[:, c * C:(c + 1) * C].unsqueeze(1).broadcast_to((C, NKV, C))
            pr.op("vector", lambda e, cos3=cos3: e.tensor_tensor(
                rk1[:, :].rearrange("p (g s) -> p g s", g=NKV),
                kfm[:, :].rearrange("p (g s) -> p g s", g=NKV), cos3, ALU.mult),
                  reads=[(kfm, None), c_cos], writes=[rk1])
            pr.op("vector", lambda e, sin3=sin3, rbv=rbv, rb=rb: e.tensor_tensor(
                rk2[:, :].rearrange("p (g s) -> p g s", g=NKV),
                rbv.rearrange("p (g s) -> p g s", g=NKV), sin3, ALU.mult),
                  reads=[rb, c_sin], writes=[rk2])
            rks = rk_ring[:, :].rearrange("p (g s) -> p g s", g=NKV)[
                :, :, cur * C:(cur + 1) * C]
            pr.op("vector", lambda e, rks=rks: e.tensor_tensor(
                rks, rk1[:, :].rearrange("p (g s) -> p g s", g=NKV),
                rk2[:, :].rearrange("p (g s) -> p g s", g=NKV), ALU.add),
                  reads=[rk1, rk2], writes=[(rk_ring, cur)])
            if stage < 6:
                continue
            for h in range(NQ):
                g = h // GRP
                pr.op("vector", lambda e, h=h, g=g, cl=cl: e.tensor_tensor(
                    qt_ar[:, h * C:(h + 1) * C],
                    expq[:, h * SCT + cl * C: h * SCT + (cl + 1) * C],
                    eGf[:, g * C:(g + 1) * C], ALU.mult),
                      reads=[(expq, h), eGf], writes=[(qt_ar, h)])
                mm(PS[5][:, h:h + 1], PS[5],
                   expq[:, h * SCT + cl * C: h * SCT + (cl + 1) * C], c_two[:],
                   [(expq, h), c_two], True, True, "qsum")
            pr.op("vector", lambda e: e.reciprocal(rq_col[:], PS[5][:, 0:NQ]),
                  reads=[PS[5]], writes=[rq_col])
            for g in range(NKV if stage >= 7 else 0):
                vsl = v_ring[:, g * NS * C + vcur * C: g * NS * C + (vcur + 1) * C]
                vrc = (v_ring, vcur)
                mm(PS[4][:, :], PS[4], ktfm[:, g * C:(g + 1) * C],
                   qt_ar[:, g * GRP * C:(g + 1) * GRP * C],
                   [(ktfm, g), (qt_ar, None)], True, True, "PT")
                pr.op("vector", lambda e: e.tensor_tensor(
                    ptm[:], PS[4][:, :], c_trimask[:], ALU.mult),
                      reads=[PS[4], c_trimask], writes=[ptm])
                for j in range(GRP):
                    h = g * GRP + j
                    mm(PS[5][:, j * C:(j + 1) * C], PS[5], qt_ar[:, h * C:(h + 1) * C],
                       S_bf[:, g * C:(g + 1) * C],
                       [(qt_ar, h), (S_bf, g)], True, False, "o_int")
                    mm(PS[5][:, j * C:(j + 1) * C], PS[5], ptm[:, j * C:(j + 1) * C],
                       vsl, [ptm, vrc], False, True, "o_intra")
                mm(PS[3][:, 0:C], PS[3], khm[:, g * C:(g + 1) * C], vsl,
                   [khm, vrc], True, True, "dS")
                pr.op("vector", lambda e, g=g: e.scalar_tensor_tensor(
                    S_f32[:, g * C:(g + 1) * C], S_f32[:, g * C:(g + 1) * C],
                    lamc[:, g:g + 1], PS[3][:, 0:C], ALU.mult, ALU.add),
                      reads=[(S_f32, g), lamc, PS[3]], writes=[(S_f32, g)])
                pr.op("scalar", lambda e, g=g: e.copy(
                    S_bf[:, g * C:(g + 1) * C], S_f32[:, g * C:(g + 1) * C]),
                      reads=[(S_f32, g)], writes=[(S_bf, g)])
                if stage < 8:
                    continue
                rqv = rq_ar[:, :].rearrange("p (h s) -> p h s", h=NQ)[
                    :, g * GRP:(g + 1) * GRP, cl * C:(cl + 1) * C]
                rkB = rk_ring[:, g * 2 * C + cur * C: g * 2 * C + (cur + 1) * C]
                mm(PS[7][:, :], PS[7], rkB, rqv, [(rk_ring, cur), (rq_ar, None)],
                   True, True, "scB")
                pr.op("vector", lambda e: e.tensor_tensor(
                    wtmp[:], PS[7][:, :], c_maskB[:], ALU.add),
                      reads=[PS[7], c_maskB], writes=[wtmp])
                pr.op("scalar", lambda e: e.activation(expsB[:], wtmp[:], AF.Exp,
                                                       scale=SCALE),
                      reads=[wtmp], writes=[expsB])
                if c > 0:
                    rkA = rk_ring[:, g * 2 * C + prv * C + 64:
                                  g * 2 * C + prv * C + 128]
                    mm(PS[6][64:128, :], PS[6], rkA, rqv,
                       [(rk_ring, prv), (rq_ar, None)], True, True, "scA", tp=(0, 64))
                    pr.op("vector", lambda e: e.tensor_tensor(
                        wtmp[64:128, :], PS[6][64:128, :], c_maskA[64:128, :], ALU.add),
                          reads=[PS[6], c_maskA], writes=[(wtmp, "A")])
                    pr.op("scalar", lambda e: e.activation(
                        expsA[64:128, :], wtmp[64:128, :], AF.Exp, scale=SCALE),
                          reads=[(wtmp, "A")], writes=[expsA])
                for j in range(GRP):
                    h = g * GRP + j
                    if c > 0:
                        mm(PS[2][:, 16 + h:17 + h], PS[2],
                           expsA[64:128, j * C:(j + 1) * C], c_two[64:128, :],
                           [expsA, c_two], True, False, "wsA", tp=(64, 0))
                    mm(PS[2][:, 16 + h:17 + h], PS[2], expsB[:, j * C:(j + 1) * C],
                       c_two[:], [expsB, c_two], c == 0, True, "wsB")
                    if c > 0:
                        vprev = v_ring[64:128, g * NS * C + vprv * C:
                                       g * NS * C + vprv * C + C]
                        mm(PS[0][:, j * C:(j + 1) * C], PS[0],
                           expsA[64:128, j * C:(j + 1) * C], vprev,
                           [expsA, (v_ring, vprv)], True, False, "yA", tp=(64, 0))
                    mm(PS[0][:, j * C:(j + 1) * C], PS[0],
                       expsB[:, j * C:(j + 1) * C], vsl,
                       [expsB, vrc], c == 0, True, "yB")
                pr.op("vector", lambda e, g=g: e.reciprocal(
                    rw_col[:, g * GRP:(g + 1) * GRP],
                    PS[2][:, 16 + g * GRP:16 + (g + 1) * GRP]),
                      reads=[PS[2]], writes=[(rw_col, g)])
                for j in range(GRP):
                    h = g * GRP + j
                    pr.op("scalar", lambda e, h=h, j=j: e.activation(
                        t_y[:], PS[0][:, j * C:(j + 1) * C], AF.Copy,
                        scale=rw_col[:, h:h + 1]),
                          reads=[PS[0], (rw_col, g)], writes=[t_y])
                    pr.op("vector", lambda e, h=h, j=j: e.scalar_tensor_tensor(
                        o_tm[:, h * C:(h + 1) * C], PS[5][:, j * C:(j + 1) * C],
                        rq_col[:, h:h + 1], t_y[:], ALU.mult, ALU.add),
                          reads=[PS[5], rq_col, t_y], writes=[(o_tm, h)])
            stg = ostg[c % 2]
            for h in range(NQ):
                if DMAT:
                    q = "sync"
                    pr.op(q, lambda e, h=h, stg=stg: e.dma_start(
                        out=stg[:, h * C:(h + 1) * C], in_=o_tm[:, h * C:(h + 1) * C],
                        transpose=True),
                          reads=[(o_tm, h)], writes=[(stg, h)], dma=True, label="oT")
                else:
                    bk = PS[6 + h % 2]
                    pv = bk[:, 0:64].bitcast(BF16)
                    pr.op("tensor", lambda e, h=h, pv=pv: e.transpose(
                        pv, o_tm[:, h * C:(h + 1) * C], c_idn[:]),
                          reads=[(o_tm, h), c_idn], writes=[bk], label="oT")
                    pr.op("scalar", lambda e, h=h, pv=pv, stg=stg: e.copy(
                        stg[:, h * C:(h + 1) * C], pv), reads=[bk], writes=[(stg, h)])
            if fused:
                tgt, lo = o_lo, c
            else:
                tgt, lo = (o_lo, c) if c < half else (o_hi, c - half)
            pr.op("scalar", lambda e, tgt=tgt, lo=lo, stg=stg: e.dma_start(
                out=tgt[lo * C:(lo + 1) * C, :], in_=stg[:]),
                  reads=[(stg, None)], writes=[(tgt, lo)], dma=True)
            if fused and coll_hook is not None:
                coll_hook(pr, io["o_send"], c)
    io["pool"] = {
        "ofl_blocks": ([(xn_fm, i) for i in range(16)] +
                       [(expq, i) for i in range(8)] +
                       [(rq_ar, i) for i in range(8)]),
        "wring": [xn_tm[0][:, :], xn_tm[1][:, :], wqb[0][:, :], wqb[1][:, :],
                  wkvb[0][:, :], wkvb[1][:, :],
                  x_tm[0][:, :].bitcast(BF16)[:, 0:4096],
                  x_tm[0][:, :].bitcast(BF16)[:, 4096:8192]],
        "x1r": x_tm[0],
        "f32s": [gs0, gs1, gtm, expk, enG, kt1, wtmp],
        "hsb": qt_ar, "hwf": o_tm,
        "xn2tm": [wqb[0], wqb[1]],
        "c_idn": c_idn, "c_eps": c_eps, "c_two": c_two,
        "small": T < 2048,
        "bcb": k_tm[:, :].bitcast(BF16),
        "wrab_pieces": [c_cos[:, :], c_sin[:, :], ktfm[:, :]],
        "k_tm": k_tm, "c_cos_t": c_cos, "c_sin_t": c_sin, "ktfm_t": ktfm,
        "rk1": rk1, "rk2": rk2, "ptm": ptm,
        "router_scratch": eBC[:, :].bitcast(F32),
        "statsAll": wtmp[:, 64:448],
        "eBC_t": eBC,
        "stats": stats, "mv": mv, "rstd": rstd, "nmean": nmean,
        "PS": PS,
    }
    return io


def build_phase_b(nc, pr, NTOK=1024, pool=None, o_parts=None, fstage=9):
    NCHB = NTOK // C
    NH = 32
    fused = pool is not None
    io = {}

    def din(name, shape, dt=F32, kind="ExternalInput"):
        h = nc.dram_tensor(name, shape, dt, kind=kind)
        io[name] = h
        return h.ap()

    x_own = din("x_own", [NTOK, HID])
    wo = din("wo", [NH * C, HID])
    wbT = din("wbT", [HID, HID])
    wra = din("wra", [HID, 136])
    bc = din("bc", [C, HID])
    bbias = din("bbias", [1, HID])
    onerow_d = din("onerow", [1, C])
    y = din("y", [NTOK, HID], kind="ExternalOutput")
    x1 = nc.dram_tensor("x1", [NTOK, HID], F32, kind="Internal").ap()

    A = lambda name, shape, dt: nc.alloc_sbuf_tensor(name, shape, dt)

    if fused:
        PS = pool["PS"]
        c_idn2 = pool["c_idn"]
        c_eps2 = pool["c_eps"]
        blocks = pool["ofl_blocks"]
        wring = pool["wring"]
        wring_units = [ap.tensor for ap in wring]
        x1r0 = pool["x1r"]
        f32s = pool["f32s"]
        xsl = [f32s[0], f32s[1]]
        x1s = [f32s[2], f32s[3]]
        outs = [f32s[4], f32s[5]]
        hsb_t = pool["hsb"]
        hwf_t = pool["hwf"]
        xn2tm = pool["xn2tm"]
        o_send, cc_out = o_parts
        bar = [t for t, _ in blocks] + [x1r0, hsb_t, hwf_t] + list(xn2tm)
        seen = set()
        bar2 = []
        for t in bar:
            if t.name not in seen:
                seen.add(t.name)
                bar2.append(t)
        pr.op("vector", lambda e: e.engine_nop(),
              writes=[(t, None) for t in bar2])
    else:
        PS = [nc.alloc_psum_tensor(f"bankB{i}", [C, 512], F32) for i in range(8)]
        pr.register_psum(PS)
        c_idn2 = A("c_idn2", [C, C], BF16)
        idn2_d = din("idn2", [C, C])
        c_eps2 = A("c_eps2", [C, 1], F32)
        epscol2_d = din("epscol2", [C, 1])
        pr.op("gpsimd", lambda e: e.dma_start(out=c_idn2[:], in_=idn2_d[:, :]),
              writes=[c_idn2], dma=True)
        pr.op("sync", lambda e: e.dma_start(out=c_eps2[:], in_=epscol2_d[:, :]),
              writes=[c_eps2], dma=True)
        o_full = din("o_full", [NCHB * C, NH * C], BF16)
        ofl = A("ofl", [C, NH * NCHB * C], BF16)
        blocks = [(ofl, i) for i in range(NH)]
        wr0 = [A(f"wrg{i}", [C, 4096], BF16) for i in range(8)]
        wring = [t[:, :] for t in wr0]
        wring_units = wr0
        x1r0 = A("x1r0", [C, HID], F32)
        xsl = [A(f"xsl{i}", [C, 512], F32) for i in range(2)]
        x1s = [A(f"x1s{i}", [C, 512], F32) for i in range(2)]
        outs = [A(f"outs{i}", [C, 512], F32) for i in range(2)]
        hsb_t = A("hsb", [C, NCHB * C], BF16)
        hwf_t = A("hwf", [C, NCHB * C], BF16)
        xn2tm = [A(f"xn2tm{i}", [C, HID], BF16) for i in range(2)]

    BST = NCHB * C
    if fused:
        assert NCHB == 8, "fused phase B requires NTOK=1024"

    def blk_ap(b, lo, hi):
        t, i = blocks[b]
        return t[:, i * BST + lo: i * BST + hi]

    if fused:
        c_one2 = pool["rk2"]
        c_one2_u = (pool["rk2"], None)
    else:
        c_one2 = A("c_one2", [1, C], BF16)
        c_one2_u = (c_one2, None)
    pr.op("gpsimd", lambda e: e.dma_start(out=c_one2[0:1, 0:C], in_=onerow_d[:, :]),
          writes=[c_one2_u], dma=True)
    if fused and not pool.get("small"):
        wp = pool["wrab_pieces"]
        wp_units = [pool["c_cos_t"], pool["c_sin_t"], pool["ktfm_t"]]
        def wrab_ap(dc):
            if dc < 15:
                return wp[0][:, dc * 136:(dc + 1) * 136]
            if dc < 30:
                return wp[1][:, (dc - 15) * 136:(dc - 14) * 136]
            return wp[2][:, (dc - 30) * 136:(dc - 29) * 136]
        def wrab_unit(dc):
            return (wp_units[0 if dc < 15 else (1 if dc < 30 else 2)], None)
        pr.op("vector", lambda e: e.engine_nop(),
              writes=[(t, None) for t in wp_units] + [(pool["k_tm"], None)])
        for dcg, (lo, hi) in enumerate([(0, 15), (15, 30), (30, 32)]):
            pr.op("gpsimd", lambda e, lo=lo, hi=hi: e.dma_start(
                out=wp[0 if lo == 0 else (1 if lo == 15 else 2)][
                    :, 0:(hi - lo) * 136].rearrange("p (a n) -> p a n", a=hi - lo),
                in_=wra[lo * C:hi * C, :].rearrange("(a p) n -> p a n", p=C)),
                  writes=[(wp_units[dcg], None)], dma=True)
        bcb = pool["bcb"]
        bcb_unit = (pool["k_tm"], None)
        pr.op("gpsimd", lambda e: e.dma_start(out=bcb[:, 0:HID], in_=bc[:, :]),
              writes=[bcb_unit], dma=True)
        bbr = pool["rk1"]
        bbr_u = (pool["rk1"], None)
        bbr_per_fcw = True
    else:
        wrab = A("wrab", [C, DC * 136], BF16)
        pr.op("gpsimd", lambda e: e.dma_start(
            out=wrab[:, :].rearrange("p (a n) -> p a n", a=DC),
            in_=wra[:, :].rearrange("(a p) n -> p a n", p=C)), writes=[wrab], dma=True)
        def wrab_ap(dc):
            return wrab[:, dc * 136:(dc + 1) * 136]
        def wrab_unit(dc):
            return (wrab, None)
        bcb = A("bcb", [C, HID], BF16)
        bcb_unit = (bcb, None)
        pr.op("gpsimd", lambda e: e.dma_start(out=bcb[:, :], in_=bc[:, :]),
              writes=[bcb_unit], dma=True)
        bbr = A("bbr", [1, HID], BF16)
        bbr_u = (bbr, None)
        bbr_per_fcw = False
        pr.op("gpsimd", lambda e: e.dma_start(out=bbr[:], in_=bbias[:, :]),
              writes=[bbr], dma=True)
    if fused:
        stats = pool["stats"]; mv = pool["mv"]; rstd = pool["rstd"]; nmean = pool["nmean"]
        logits_all = f32s[6]
        statsAll = pool["statsAll"]
        _rt = pool["router_scratch"]
    else:
        stats = A("statsB", [C, 48], F32)
        mv = A("mvB", [C, 2], F32)
        rstd = A("rstdB", [C, 1], F32)
        nmean = A("nmeanB", [C, 1], F32)
        logits_all = A("logits_all", [C, 8 * NCHB], F32)
        statsAll = A("statsAll", [C, 48 * NCHB], F32)
    if fused:
        probs = _rt[:, 0:8]
        oh1 = _rt[:, 8:16]
        probs2 = _rt[:, 16:24]
        oh2 = _rt[:, 24:32]
        cmb = _rt[:, 32:40]
        pmax = _rt[:, 40:41]
        psum_r = _rt[:, 41:42]
        m2c = _rt[:, 42:43]
        den = _rt[:, 43:44]
        rden = _rt[:, 44:45]
        w1c = _rt[:, 45:46]
        w2c = _rt[:, 46:47]
    else:
        pmax = A("pmax", [C, 1], F32)
        psum_r = A("psum_r", [C, 1], F32)
        probs = A("probs", [C, 8], F32)
        oh1 = A("oh1", [C, 8], F32)
        probs2 = A("probs2", [C, 8], F32)
        m2c = A("m2c", [C, 1], F32)
        oh2 = A("oh2", [C, 8], F32)
        den = A("den", [C, 1], F32)
        rden = A("rden", [C, 1], F32)
        w1c = A("w1c", [C, 1], F32)
        w2c = A("w2c", [C, 1], F32)
        cmb = A("cmb", [C, 8], F32)

    def mm(out_ap, bank, lhsT, rhs, reads, start, stop, label="", tp=None):
        pr.op("tensor",
              lambda e: e.matmul(out_ap, lhsT, rhs, start=start, stop=stop,
                                 tile_position=tp, skip_group_check=True),
              reads=reads, writes=[bank], label=label)

    if fused:
        msel_d = din("msel", [C, 2])
        msel = A("mselt", [C, 2], F32)
        pr.op("sync", lambda e: e.dma_start(out=msel[:], in_=msel_d[:, :]),
              writes=[msel], dma=True)
        candA = pool["hwf"]
        candB = pool["hsb"]
        cc_a, cc_b1, cc_b2 = o_parts[1]
        ccA_ap, ccB1_ap, ccB2_ap = cc_a.ap(), cc_b1.ap(), cc_b2.ap()
        for t in range(NCHB):
            for hb in range(2):
                rA = hb * 1024 + t * C
                if t < NCHB // 2:
                    cb_t, cb_ap, rB = cc_b1, ccB1_ap, hb * 512 + t * C
                else:
                    cb_t, cb_ap, rB = cc_b2, ccB2_ap, hb * 512 + (t - NCHB // 2) * C
                pr.op("sync", lambda e, rA=rA: e.dma_start(
                    out=candA[:, :], in_=ccA_ap[rA:rA + C, :]),
                      reads=[(cc_a, None)], writes=[(candA, None)], dma=True)
                pr.op("sync", lambda e, rB=rB, cb_ap=cb_ap: e.dma_start(
                    out=candB[:, :], in_=cb_ap[rB:rB + C, :]),
                      reads=[(cb_t, None)], writes=[(candB, None)], dma=True)
                pr.op("vector", lambda e: e.tensor_scalar(
                    candA[:, :], candA[:, :], msel[:, 0:1], None, ALU.mult),
                      reads=[(candA, None), msel], writes=[(candA, None)])
                hb0 = hb * 16
                b0 = hb0
                while b0 < hb0 + 16:
                    tns, i0 = blocks[b0]
                    b1 = b0
                    while (b1 + 1 < hb0 + 16 and blocks[b1 + 1][0] is tns
                           and blocks[b1 + 1][1] == blocks[b1][1] + 1):
                        b1 += 1
                    nb = b1 - b0 + 1
                    hoff = b0 - hb0
                    dst = tns[:, i0 * BST:(i0 + nb) * BST].rearrange(
                        "p (h n) -> p h n", h=nb)[:, :, t * C:(t + 1) * C]
                    pr.op("vector", lambda e, dst=dst, hoff=hoff, nb=nb: e.scalar_tensor_tensor(
                        dst, candB[:, hoff * C:(hoff + nb) * C].rearrange(
                            "p (h n) -> p h n", h=nb),
                        msel[:, 1:2],
                        candA[:, hoff * C:(hoff + nb) * C].rearrange(
                            "p (h n) -> p h n", h=nb),
                        ALU.mult, ALU.add),
                          reads=[(candA, None), (candB, None), msel],
                          writes=[(tns, ("c", t, hb0 + hoff))], label="oblend")
                    b0 = b1 + 1
    else:
        o_full_ap = io["o_full"].ap()
        for t in range(NCHB):
            dst = blocks[0][0][:, :].rearrange("p (h n) -> p h n", h=NH)[
                :, :, t * C:(t + 1) * C]
            pr.op("sync", lambda e, t=t, dst=dst: e.dma_start(
                out=dst, in_=o_full_ap[t * C:(t + 1) * C, :].rearrange(
                    "p (h n) -> p h n", h=NH)),
                  writes=[(blocks[0][0], ("c", t))], dma=True)

    def wslot(ocw, q):
        return (ocw * 4 + q) % 8

    if fstage < 3:
        for t in range(NCHB):
            ob = x1s[0]
            pr.op("vector", lambda e, t=t, ob=ob: e.tensor_copy(
                ob[:, 0:512], blk_ap(0, t * C, (t + 1) * C).bitcast(F32)[:, 0:64].broadcast_to((C, 512)) if False else ob[:, 0:512]),
                  reads=[], writes=[(ob, None)]) if False else None
            pr.op("sync", lambda e, t=t: e.dma_start(
                out=y[t * C:(t + 1) * C, 0:512],
                in_=blk_ap(0, t * C, (t + 1) * C).bitcast(F32)[:, 0:64]) if False else e.dma_start(
                out=y[t * C:(t + 1) * C, 0:1024].bitcast(BF16)[:, 0:1024] if False else y[t * C:(t + 1) * C, 0:512],
                in_=x_own[t * C:(t + 1) * C, 0:512]),
                  reads=[(blocks[0][0], ("c", t))], writes=[(y, t)], dma=True)
        pr.op("gpsimd", lambda e: e.nop(), reads=[(y, None)])
        return io

    for ocw in range(8):
        for q in range(4):
            sl = wring[wslot(ocw, q)]
            un = wring_units[wslot(ocw, q)]
            pr.op("gpsimd", lambda e, sl=sl, ocw=ocw, q=q: e.dma_start(
                out=sl.rearrange("p (a n) -> p a n", a=8),
                in_=wo[q * 8 * C:(q + 1) * 8 * C,
                       ocw * 512:(ocw + 1) * 512].rearrange("(a p) n -> p a n", p=C)),
                  writes=[(un, None)], dma=True)
        for hd in range(NH):
            sl = wring[wslot(ocw, hd // 8)]
            un = wring_units[wslot(ocw, hd // 8)]
            for t in range(NCHB):
                mm(PS[t][:, :], PS[t], blk_ap(hd, t * C, (t + 1) * C),
                   sl[:, (hd % 8) * 512:(hd % 8 + 1) * 512],
                   [(blocks[hd][0], ("c", t, hd) if fused else ("c", t)),
                    (un, None)], hd == 0, hd == NH - 1, "womm")
        for t in range(NCHB):
            x1b = x1s[t % 2]
            xs = xsl[t % 2]
            pr.op("sync", lambda e, t=t, ocw=ocw, xs=xs: e.dma_start(
                out=xs[:, 0:512], in_=x_own[t * C:(t + 1) * C, ocw * 512:(ocw + 1) * 512]),
                  writes=[(xs, None)], dma=True)
            pr.op("vector", lambda e, t=t, x1b=x1b, xs=xs: e.tensor_tensor(
                x1b[:, 0:512], PS[t][:, :], xs[:, 0:512], ALU.add),
                  reads=[PS[t], (xs, None)], writes=[(x1b, None)])
            pr.op("vector", lambda e, t=t, ocw=ocw, x1b=x1b: e.bn_stats(
                statsAll[:, t * 48 + ocw * 6: t * 48 + (ocw + 1) * 6], x1b[:, 0:512]),
                  reads=[(x1b, None)], writes=[("statsAll", (t, ocw))])
            pr.op("sync", lambda e, t=t, ocw=ocw, x1b=x1b: e.dma_start(
                out=x1[t * C:(t + 1) * C, ocw * 512:(ocw + 1) * 512], in_=x1b[:, 0:512]),
                  reads=[(x1b, None)], writes=[(x1, (t, ocw))], dma=True)

    if fstage < 4:
        pr.op("gpsimd", lambda e: e.nop(), reads=[(x1, None)])
        io["y"] = y
        for t in range(NCHB):
            xs = xsl[0]
            pr.op("sync", lambda e, t=t, xs=xs: e.dma_start(
                out=xs[:, 0:512], in_=x1[t * C:(t + 1) * C, 0:512]),
                  reads=[(x1, None)], writes=[(xs, None)], dma=True)
            pr.op("sync", lambda e, t=t, xs=xs: e.dma_start(
                out=y[t * C:(t + 1) * C, 0:512], in_=xs[:, 0:512]),
                  reads=[(xs, None)], writes=[(y, t)], dma=True)
        pr.op("gpsimd", lambda e: e.nop(), reads=[(y, None)])
        return io

    pr.op("vector", lambda e: e.engine_nop(),
          writes=[(blocks[b][0], None) for b in (0, 16, 24)] +
                 [(t, None) for t in xn2tm] + [(x1r0, None)])

    for t in range(NCHB):
        xr = x1r0
        nb = xn2tm[t % 2]
        pr.op("sync", lambda e, t=t, xr=xr: e.dma_start(
            out=xr[:, :], in_=x1[t * C:(t + 1) * C, :]),
              reads=[(x1, None)], writes=[(xr, None)], dma=True)
        pr.op("vector", lambda e, t=t: e.bn_aggr(
            mv[:], statsAll[:, t * 48:(t + 1) * 48]),
              reads=[("statsAll", None)], writes=[mv])
        pr.op("scalar", lambda e: e.activation(rstd[:], mv[:, 1:2], AF.Ln,
                                               bias=c_eps2[:, 0:1]),
              reads=[mv, c_eps2], writes=[(rstd, "ln")])
        pr.op("scalar", lambda e: e.activation(rstd[:], rstd[:], AF.Exp, scale=-0.5),
              reads=[(rstd, "ln")], writes=[rstd])
        pr.op("vector", lambda e: e.tensor_scalar(nmean[:], mv[:, 0:1], -1.0, None,
                                                  ALU.mult),
              reads=[mv], writes=[nmean])
        pr.op("vector", lambda e, xr=xr, nb=nb: e.tensor_scalar(
            nb[:, :], xr[:, :], nmean[:, 0:1], rstd[:, 0:1], ALU.add, ALU.mult),
              reads=[(xr, None), nmean, rstd], writes=[(nb, None)])
        for dc in range(DC):
            if DMAT:
                q = "sync"
                pr.op(q, lambda e, nb=nb, dc=dc, t=t: e.dma_start(
                    out=blk_ap(dc, t * C, (t + 1) * C), in_=nb[:, dc * C:(dc + 1) * C],
                    transpose=True),
                      reads=[(nb, None)], writes=[(blocks[dc][0], ("x", dc, t))],
                      dma=True, label="xn2T")
            else:
                bk = PS[6 + (dc % 2)]
                pv = bk[:, 0:64].bitcast(BF16)
                pr.op("tensor", lambda e, nb=nb, dc=dc, pv=pv: e.transpose(
                    pv, nb[:, dc * C:(dc + 1) * C], c_idn2[:]),
                      reads=[(nb, None), c_idn2], writes=[bk], label="xn2T")
                pr.op("scalar", lambda e, dc=dc, t=t, pv=pv: e.copy(
                    blk_ap(dc, t * C, (t + 1) * C), pv),
                      reads=[bk], writes=[(blocks[dc][0], ("x", dc, t))])

    def xn2read(dc, t):
        return (blocks[dc][0], ("x", dc, t))

    for t in range(NCHB):
        for dc in range(DC):
            mm(PS[t][:, 0:136], PS[t], blk_ap(dc, t * C, (t + 1) * C),
               wrab_ap(dc),
               [xn2read(dc, t), wrab_unit(dc)], dc == 0, dc == DC - 1, "ramm")
    pr.op("vector", lambda e: e.engine_nop(),
          writes=[(hsb_t, None), (hwf_t, None)])
    for t in range(NCHB):
        pr.op("vector", lambda e, t=t: e.tensor_copy(
            logits_all[:, t * 8:(t + 1) * 8], PS[t][:, 0:8]),
              reads=[PS[t]], writes=[(logits_all, t)])
        pr.op("vector", lambda e, t=t: e.tensor_copy(
            hsb_t[:, t * C:(t + 1) * C], PS[t][:, 8:136]),
              reads=[PS[t]], writes=[(hsb_t, ("h", t))])
    for t in range(NCHB):
        pr.op("vector", lambda e, t=t: e.tensor_reduce(
            pmax[:], logits_all[:, t * 8:(t + 1) * 8], mybir.AxisListType.X, ALU.max),
              reads=[(logits_all, t)], writes=[pmax])
        pr.op("vector", lambda e, t=t: e.tensor_scalar(
            probs[:], logits_all[:, t * 8:(t + 1) * 8], pmax[:, 0:1],
            None, ALU.subtract),
              reads=[(logits_all, t), pmax], writes=[(probs, "s")])
        pr.op("scalar", lambda e: e.activation(probs[:], probs[:], AF.Exp,
                                               accum_out=psum_r[:, 0:1]),
              reads=[(probs, "s")], writes=[(probs, "e"), psum_r])
        pr.op("vector", lambda e: e.reciprocal(pmax[:], psum_r[:]),
              reads=[psum_r], writes=[(pmax, "r")])
        pr.op("vector", lambda e: e.tensor_scalar(probs[:], probs[:], pmax[:, 0:1],
                                                  None, ALU.mult),
              reads=[(probs, "e"), (pmax, "r")], writes=[(probs, None)])
        pr.op("vector", lambda e: e.tensor_reduce(pmax[:], probs[:],
                                                  mybir.AxisListType.X, ALU.max),
              reads=[(probs, None)], writes=[(pmax, "m1")])
        pr.op("vector", lambda e: e.tensor_scalar(oh1[:], probs[:], pmax[:, 0:1], None,
                                                  ALU.is_ge),
              reads=[(probs, None), (pmax, "m1")], writes=[oh1])
        pr.op("vector", lambda e: e.scalar_tensor_tensor(probs2[:], oh1[:], -2.0,
                                                         probs[:], ALU.mult, ALU.add),
              reads=[oh1, (probs, None)], writes=[probs2])
        pr.op("vector", lambda e: e.tensor_reduce(m2c[:], probs2[:],
                                                  mybir.AxisListType.X, ALU.max),
              reads=[probs2], writes=[m2c])
        pr.op("vector", lambda e: e.tensor_scalar(oh2[:], probs2[:], m2c[:, 0:1], None,
                                                  ALU.is_ge),
              reads=[probs2, m2c], writes=[oh2])
        pr.op("vector", lambda e: e.tensor_tensor(den[:], pmax[:], m2c[:], ALU.add),
              reads=[(pmax, "m1"), m2c], writes=[den])
        pr.op("vector", lambda e: e.reciprocal(rden[:], den[:]),
              reads=[den], writes=[rden])
        pr.op("vector", lambda e: e.tensor_tensor(w1c[:], pmax[:], rden[:], ALU.mult),
              reads=[(pmax, "m1"), rden], writes=[w1c])
        pr.op("vector", lambda e: e.tensor_tensor(w2c[:], m2c[:], rden[:], ALU.mult),
              reads=[m2c, rden], writes=[w2c])
        pr.op("vector", lambda e: e.tensor_scalar(oh1[:], oh1[:], w1c[:, 0:1], None,
                                                  ALU.mult),
              reads=[(oh1, None), w1c], writes=[(oh1, "w")])
        pr.op("vector", lambda e: e.scalar_tensor_tensor(cmb[:], oh2[:], w2c[:, 0:1],
                                                         oh1[:], ALU.mult, ALU.add),
              reads=[(oh2, None), w2c, (oh1, "w")], writes=[cmb])
        cmb3 = cmb[:, :].unsqueeze(2).broadcast_to((C, 8, 16))
        hwb_sl = hsb_t[:, 1024 + t * C: 1024 + (t + 1) * C]
        pr.op("vector", lambda e, t=t, cmb3=cmb3, hwb_sl=hwb_sl: e.scalar_tensor_tensor(
            hwb_sl.rearrange("p (a b) -> p a b", a=8),
            hsb_t[:, t * C:(t + 1) * C].rearrange("p (a b) -> p a b", a=8), 2.0,
            cmb3, ALU.mult, ALU.mult),
              reads=[(hsb_t, ("h", t)), cmb], writes=[(hsb_t, ("hw", t))])

    for t in range(NCHB):
        hwb_sl = hsb_t[:, 1024 + t * C: 1024 + (t + 1) * C]
        bk = PS[6 + t % 2]
        pv = bk[:, 64:128].bitcast(BF16)
        pr.op("tensor", lambda e, pv=pv, hwb_sl=hwb_sl: e.transpose(
            pv, hwb_sl, c_idn2[:]),
              reads=[(hsb_t, ("hw", t)), c_idn2], writes=[bk], label="hwT")
        pr.op("scalar", lambda e, t=t, pv=pv: e.copy(
            hwf_t[:, t * C:(t + 1) * C], pv), reads=[bk], writes=[(hwf_t, ("w", t))])

    for fcw in range(8):
        for q in range(4):
            sl = wring[wslot(fcw, q)]
            un = wring_units[wslot(fcw, q)]
            pr.op("gpsimd", lambda e, sl=sl, fcw=fcw, q=q: e.dma_start(
                out=sl.rearrange("p (a n) -> p a n", a=8),
                in_=wbT[q * 8 * C:(q + 1) * 8 * C,
                        fcw * 512:(fcw + 1) * 512].rearrange("(a p) n -> p a n", p=C)),
                  writes=[(un, None)], dma=True)
        for dc in range(DC):
            sl = wring[wslot(fcw, dc // 8)]
            un = wring_units[wslot(fcw, dc // 8)]
            for t in range(NCHB):
                mm(PS[t][:, :], PS[t], blk_ap(dc, t * C, (t + 1) * C),
                   sl[:, (dc % 8) * 512:(dc % 8 + 1) * 512],
                   [xn2read(dc, t), (un, None)], dc == 0, False, "bmm")
        if bbr_per_fcw:
            pr.op("gpsimd", lambda e, fcw=fcw: e.dma_start(
                out=bbr[0:1, 0:512], in_=bbias[0:1, fcw * 512:(fcw + 1) * 512]),
                  writes=[bbr_u], dma=True)
        for t in range(NCHB):
            mm(PS[t][:, :], PS[t], hwf_t[:, t * C:(t + 1) * C],
               bcb[:, fcw * 512:(fcw + 1) * 512],
               [(hwf_t, ("w", t)), bcb_unit], False, False, "emm")
            bslice = bbr[0:1, 0:512] if bbr_per_fcw else bbr[0:1, fcw * 512:(fcw + 1) * 512]
            mm(PS[t][:, :], PS[t], c_one2[0:1, 0:C], bslice,
               [c_one2_u, bbr_u], False, True, "biasmm")
        for t in range(NCHB):
            xs = xsl[t % 2]
            pr.op("sync", lambda e, t=t, fcw=fcw, xs=xs: e.dma_start(
                out=xs[:, 0:512], in_=x1[t * C:(t + 1) * C, fcw * 512:(fcw + 1) * 512]),
                  reads=[(x1, None)], writes=[(xs, None)], dma=True)
            ob = outs[t % 2]
            pr.op("vector", lambda e, t=t, ob=ob, xs=xs: e.tensor_tensor(
                ob[:, 0:512], PS[t][:, :], xs[:, 0:512], ALU.add),
                  reads=[PS[t], (xs, None)], writes=[(ob, None)])
            pr.op("scalar", lambda e, t=t, fcw=fcw, ob=ob: e.dma_start(
                out=y[t * C:(t + 1) * C, fcw * 512:(fcw + 1) * 512], in_=ob[:, 0:512]),
                  reads=[(ob, None)], writes=[(y, (t, fcw))], dma=True)
    return io


def build_fused(nc, pr, T=2048, SCH=4):
    import os
    fstage = int(os.environ.get("FSTAGE", "9"))
    NCH = T // C
    half_rows = (NCH // 2) * C
    qr = (NCH // 4) * C
    cc_outA = nc.dram_tensor("cc_outA", [2 * half_rows, NQ * C], BF16, kind="Internal")
    cc_outB1 = nc.dram_tensor("cc_outB1", [2 * qr, NQ * C], BF16, kind="Internal")
    cc_outB2 = nc.dram_tensor("cc_outB2", [2 * qr, NQ * C], BF16, kind="Internal")

    def emit_coll(osap, rows, out_t):
        def coll(e):
            return e.collective_compute(
                "AllGather", mybir.AluOpType.bypass,
                replica_groups=[[0, 1], [2, 3], [4, 5], [6, 7]],
                ins=[osap[rows[0]:rows[1], :].opt()],
                outs=[out_t.ap()[:, :].opt()])
        return coll

    def coll_hook(pr_, o_send_h, after_chunk):
        osap = o_send_h.ap()
        if after_chunk == NCH // 2 - 1:
            pr_.op("gpsimd", emit_coll(osap, (0, half_rows), cc_outA),
                   reads=[(o_send_h, None)], writes=[(cc_outA, None)],
                   private=True, label="agA")
        elif after_chunk == 3 * NCH // 4 - 1:
            pr_.op("gpsimd", emit_coll(osap, (half_rows, half_rows + qr), cc_outB1),
                   reads=[(o_send_h, None)], writes=[(cc_outB1, None)],
                   private=True, label="agB1")
        elif after_chunk == NCH - 1:
            pr_.op("gpsimd", emit_coll(osap, (half_rows + qr, 2 * half_rows), cc_outB2),
                   reads=[(o_send_h, None)], writes=[(cc_outB2, None)],
                   private=True, label="agB2")

    io_a = build_phase_a(nc, pr, T=T, SCH=SCH, fused=True, coll_hook=coll_hook)
    pool = io_a["pool"]
    o_send = io_a["o_send"]
    cc_out = (cc_outA, cc_outB1, cc_outB2)

    if fstage < 2:
        dbg = nc.dram_tensor("y", [T // 2, HID], F32, kind="ExternalOutput").ap()
        sb = nc.alloc_sbuf_tensor("dbgsb", [C, 512], F32)
        pr.op("sync", lambda e: e.dma_start(out=sb[:, 0:512],
                                            in_=cc_out.ap()[0:C, 0:512]),
              reads=[(cc_out, None)], writes=[sb], dma=True)
        pr.op("sync", lambda e: e.dma_start(out=dbg[0:C, 0:512], in_=sb[:, 0:512]),
              reads=[sb], writes=[("yo", None)], dma=True)
        pr.op("gpsimd", lambda e: e.nop(), reads=[("yo", None)])
        return io_a, {"y": dbg}
    io_b = build_phase_b(nc, pr, NTOK=T // 2, pool=pool, o_parts=(o_send, cc_out),
                         fstage=fstage)
    pr.op("gpsimd", lambda e: e.nop(), reads=[io_b["y"]])
    return io_a, io_b


import concourse.bass as _bass
from concourse.bass_utils import run_bass_kernel_spmd as _run_spmd

_CACHE = {}


def _build_fused():
    if "f" not in _CACHE:
        nc = _bass.Bass("TRN2", target_bir_lowering=False, debug=False)
        pr = Prog(nc)
        build_fused(nc, pr, T=2048, SCH=4)
        pr.finalize_and_emit()
        _CACHE["f"] = nc
    return _CACHE["f"]


def kernel(L0_attn_hidden_states, L0_attn_Wq, L0_attn_Wk, L0_attn_Wv, L0_attn_Wo,
           L0_ffn_W_base, L0_ffn_b_base, L0_ffn_W_router, L0_ffn_A, L0_ffn_B):
    x = np.ascontiguousarray(np.asarray(L0_attn_hidden_states, dtype=np.float32))
    Wq = np.asarray(L0_attn_Wq, dtype=np.float32)
    Wk = np.asarray(L0_attn_Wk, dtype=np.float32)
    Wv = np.asarray(L0_attn_Wv, dtype=np.float32)
    Wo = np.ascontiguousarray(np.asarray(L0_attn_Wo, dtype=np.float32))
    Wb = np.asarray(L0_ffn_W_base, dtype=np.float32)
    bb = np.asarray(L0_ffn_b_base, dtype=np.float32)
    Wr = np.asarray(L0_ffn_W_router, dtype=np.float32)
    Af = np.asarray(L0_ffn_A, dtype=np.float32)
    Bf = np.asarray(L0_ffn_B, dtype=np.float32)

    cons = host_constants()
    cosf, sinf = rope_tables_fm(2048)
    wra = np.ascontiguousarray(np.concatenate(
        [Wr.T, Af.reshape(128, HID).T], axis=1).astype(np.float32))
    bcc = np.ascontiguousarray(Bf.transpose(0, 2, 1).reshape(128, HID).astype(np.float32))
    shared = {"cosf": cosf, "sinf": sinf, "wo": Wo,
              "wbT": np.ascontiguousarray(Wb.T), "wra": wra, "bc": bcc,
              "bbias": np.ascontiguousarray(bb[None, :]),
              "onerow": np.ones((1, 128), np.float32)}
    for k in ["triUs", "oness", "sixcol", "trimask", "maskA", "maskB", "mrot",
              "idn", "twocol", "epscol"]:
        shared[k] = cons[k]

    in_maps = []
    for c in range(8):
        bi, hh = c >> 1, c & 1
        m = dict(shared)
        m["x"] = np.ascontiguousarray(x[bi])
        m["wq"] = np.ascontiguousarray(Wq[:, hh * 2048:(hh + 1) * 2048])
        m["wkv"] = np.ascontiguousarray(
            np.concatenate([Wk[:, hh * 512:(hh + 1) * 512],
                            Wv[:, hh * 512:(hh + 1) * 512]], axis=1))
        m["x_own"] = np.ascontiguousarray(x[bi, hh * 1024:(hh + 1) * 1024])
        ms = np.zeros((128, 2), np.float32)
        ms[:, 0] = 1.0 if hh == 0 else 0.0
        ms[:, 1] = 1.0 if hh == 1 else 0.0
        m["msel"] = ms
        in_maps.append(m)

    res = _run_spmd(_build_fused(), in_maps, core_ids=list(range(8)))

    out = np.zeros((4, 2048, 4096), np.float32)
    for c in range(8):
        bi, hh = c >> 1, c & 1
        out[bi, hh * 1024:(hh + 1) * 1024] = np.asarray(res.results[c]["y"])
    return out


# version 07
# speedup vs baseline: 1.1729x; 1.0137x over previous
import concourse.bass as bass
import concourse.mybir as mybir

ENGINES = ("tensor", "vector", "scalar", "gpsimd", "sync")


def _tname(x):
    if isinstance(x, str):
        return x
    t = getattr(x, "tensor", None)
    if t is not None:
        return t.name
    return x.name


class _Op:
    __slots__ = ("idx", "engine", "emit", "reads", "writes", "dma", "deps",
                 "needs_inc", "rank", "label", "private", "switch_n")

    def __init__(self, idx, engine, emit, reads, writes, dma, label,
                 private=False, switch_n=0):
        self.idx = idx
        self.engine = engine
        self.emit = emit
        self.reads = reads
        self.writes = writes
        self.dma = dma
        self.deps = set()
        self.needs_inc = False
        self.rank = None
        self.label = label
        self.private = private
        self.switch_n = switch_n


class Prog:
    def __init__(self, nc):
        self.nc = nc
        self.ops = []
        self.state = {}
        self.alias = {}
        self.psum_names = set()

    def register_psum(self, tensors):
        for t in tensors:
            self.psum_names.add(_tname(t))

    def add_alias(self, virt, base):
        self.alias[_tname(virt)] = _tname(base)

    def _norm(self, u):
        if isinstance(u, tuple):
            x, tag = u
        else:
            x, tag = u, None
        n = _tname(x)
        n = self.alias.get(n, n)
        return (n, tag)

    def _recs(self, name):
        return self.state.setdefault(name, {})

    def _dep_read(self, op, name, tag):
        recs = self._recs(name)
        if tag is None:
            for t, rec in recs.items():
                if rec["w"] is not None:
                    op.deps.add(rec["w"])
        else:
            for t in (tag, None):
                rec = recs.get(t)
                if rec and rec["w"] is not None:
                    op.deps.add(rec["w"])

    def _dep_write(self, op, name, tag):
        recs = self._recs(name)
        if tag is None:
            for t, rec in recs.items():
                if rec["w"] is not None:
                    op.deps.add(rec["w"])
                op.deps.update(rec["r"])
        else:
            for t in (tag, None):
                rec = recs.get(t)
                if rec:
                    if rec["w"] is not None:
                        op.deps.add(rec["w"])
                    op.deps.update(rec["r"])

    def _mark_read(self, idx, name, tag):
        rec = self._recs(name).setdefault(tag, {"w": None, "r": set()})
        rec["r"].add(idx)

    def _mark_write(self, idx, name, tag):
        recs = self._recs(name)
        if tag is None:
            recs.clear()
        recs[tag] = {"w": idx, "r": set()}

    def op(self, engine, emit, reads=(), writes=(), dma=False, label="",
           private=False, switch_n=0):
        idx = len(self.ops)
        nreads = [self._norm(u) for u in reads]
        nwrites = [self._norm(u) for u in writes]
        promoted = [u for u in nreads if u[0] in self.psum_names]
        if promoted:
            nreads = [u for u in nreads if u[0] not in self.psum_names]
            nwrites = nwrites + promoted
        o = _Op(idx, engine, emit, nreads, nwrites, dma, label,
                private=private, switch_n=switch_n)
        for n, t in o.reads:
            self._dep_read(o, n, t)
        for n, t in o.writes:
            self._dep_write(o, n, t)
        o.deps.discard(idx)
        if engine == "tensor":
            o.deps = {d for d in o.deps
                      if not (self.ops[d].engine == "tensor" and not self.ops[d].dma)}
        for n, t in o.reads:
            self._mark_read(idx, n, t)
        for n, t in o.writes:
            self._mark_write(idx, n, t)
        self.ops.append(o)
        return idx

    def barrier_unit(self, name):
        return (name, None)

    NSEM = 8

    def finalize_and_emit(self):
        nc = self.nc
        R = self.NSEM
        for o in self.ops:
            for d in o.deps:
                self.ops[d].needs_inc = True
        eng_count = {e: 0 for e in ENGINES}
        dma_count = {e: 0 for e in ENGINES}
        self._psems = {}
        for o in self.ops:
            if o.private:
                o.rank = None
                o.needs_inc = True
            elif o.switch_n:
                slots = []
                for _ in range(o.switch_n):
                    j = dma_count[o.engine]
                    dma_count[o.engine] += 1
                    slots.append((j % R, j // R + 1))
                o.rank = slots
                o.needs_inc = True
            elif o.dma:
                j = dma_count[o.engine]
                dma_count[o.engine] += 1
                o.rank = (j % R, j // R + 1)
                o.needs_inc = True
            elif o.needs_inc:
                eng_count[o.engine] += 1
                o.rank = eng_count[o.engine]
        self._esem = {}
        self._qsem = {}
        import contextlib
        self._stack = contextlib.ExitStack()
        for o in self.ops:
            if o.private:
                self._psems[o.idx] = self._stack.enter_context(
                    nc.semaphore(f"sp_{o.idx}"))
        for e in ENGINES:
            if eng_count[e]:
                self._esem[e] = self._stack.enter_context(nc.semaphore(f"se_{e}"))
            if dma_count[e]:
                self._qsem[e] = [
                    self._stack.enter_context(nc.semaphore(f"sq_{e}{i}"))
                    for i in range(min(R, dma_count[e]))
                ]
        ops_by_engine = {e: [o for o in self.ops if o.engine == e] for e in ENGINES}

        block_cm = nc.Block()
        block = block_cm.__enter__()

        def emit_stream(eng_name, eng):
            hwm = {}
            def do_wait(key, val):
                if hwm.get(key, 0) >= val:
                    return
                kind = key[0]
                if kind == "e":
                    sem = self._esem[key[1]]
                elif kind == "p":
                    sem = self._psems[key[1]]
                else:
                    sem = self._qsem[key[1]][key[2]]
                eng.wait_ge(sem, val)
                hwm[key] = val
            for o in ops_by_engine[eng_name]:
                waits = {}
                for d in o.deps:
                    dop = self.ops[d]
                    if (dop.engine == eng_name and not dop.dma and not dop.private
                            and not dop.switch_n and eng_name == "tensor"):
                        continue
                    if dop.private:
                        kvs = [(("p", d), 1)]
                    elif dop.switch_n:
                        kvs = [(("q", dop.engine, s), 16 * k) for s, k in dop.rank]
                    elif dop.dma:
                        slot, k = dop.rank
                        kvs = [(("q", dop.engine, slot), 16 * k)]
                    else:
                        kvs = [(("e", dop.engine), dop.rank)]
                    for key, val in kvs:
                        if val > waits.get(key, 0):
                            waits[key] = val
                if o.dma and not o.switch_n and not o.private:
                    slot, k = o.rank
                    if k > 1:
                        key = ("q", o.engine, slot)
                        v = 16 * (k - 1)
                        if v > waits.get(key, 0):
                            waits[key] = v
                if o.switch_n:
                    for slot, k in o.rank:
                        if k > 1:
                            key = ("q", o.engine, slot)
                            v = 16 * (k - 1)
                            if v > waits.get(key, 0):
                                waits[key] = v
                for key in sorted(waits):
                    do_wait(key, waits[key])
                if o.private:
                    inst = o.emit(eng)
                    inst.then_inc(self._psems[o.idx], 1)
                elif o.switch_n:
                    slots = list(o.rank)
                    qs = self._qsem[o.engine]
                    if o.label.endswith("_nosw"):
                        it = iter(slots)
                        def inc(inst, _it=it, _qs=qs):
                            s, k = next(_it)
                            inst.then_inc(_qs[s], 16)
                        o.emit(eng, 0, inc)
                    else:
                        for arm in eng.switch_core_id(n=512):
                            me = int(arm.logical) % 8
                            if int(arm.logical) < 8:
                                it = iter(slots)
                                def inc(inst, _it=it, _qs=qs):
                                    s, k = next(_it)
                                    inst.then_inc(_qs[s], 16)
                                o.emit(eng, me, inc)
                elif o.dma:
                    slot, k = o.rank
                    inst = o.emit(eng)
                    inst.then_inc(self._qsem[o.engine][slot], 16)
                else:
                    inst = o.emit(eng)
                    if o.needs_inc:
                        inst.then_inc(self._esem[o.engine], 1)

        for e in ENGINES:
            if not ops_by_engine[e]:
                continue
            dec = getattr(block, e)
            def body(eng, _e=e):
                emit_stream(_e, eng)
            dec(body)
        block_cm.__exit__(None, None, None)

    def final_wait(self, engine="sync"):
        pass


import numpy as np
import ml_dtypes
import concourse.mybir as mybir
import os
DMAT = os.environ.get("DMAT", "0") == "1"

F32 = mybir.dt.float32
BF16 = mybir.dt.bfloat16
AF = mybir.ActivationFunctionType
ALU = mybir.AluOpType
bfd = ml_dtypes.bfloat16

D = 128
HID = 4096
DC = HID // 128
NQ = 16
NKV = 4
GRP = 4
C = 128
SCALE = float(1.0 / np.sqrt(np.float32(D)))
NEG = -1e9


def host_constants():
    tri01 = np.triu(np.ones((C, C), np.float32))
    tt = np.arange(C)[None, :]
    cons = {}
    cons["triUs"] = np.ascontiguousarray(tri01 * 0.0625)
    cons["oness"] = np.full((C, C), 0.0625, np.float32)
    cons["sixcol"] = np.full((C, 1), 0.0625, np.float32)
    cons["trimask"] = np.ascontiguousarray(np.tile(tri01, (1, 4)))
    kkA = np.arange(64)[:, None]
    mA_full = np.full((C, C), NEG, np.float32)
    mA_full[64:128, :] = np.where(kkA >= tt, 0.0, NEG)
    cons["maskA"] = np.ascontiguousarray(np.tile(mA_full, (1, 4)))
    kkB = np.arange(C)[:, None]
    mB = np.where((kkB <= tt) & (kkB >= tt - 64), 0.0, NEG).astype(np.float32)
    cons["maskB"] = np.ascontiguousarray(np.tile(mB, (1, 4)))
    M = np.zeros((C, C), np.float32)
    for p in range(64):
        M[p + 64, p] = -1.0
        M[p, p + 64] = 1.0
    cons["mrot"] = M
    cons["idn"] = np.eye(C, dtype=np.float32)
    cons["twocol"] = np.full((C, 1), 2.0, np.float32)
    cons["epscol"] = np.full((C, 1), 1e-5, np.float32)
    return cons


def rope_tables_fm(T):
    half = np.arange(D // 2, dtype=np.float32)
    inv = 1.0 / 10000.0 ** (2.0 * half / D)
    fr = np.arange(T, dtype=np.float32)[:, None] * inv[None, :]
    emb = np.concatenate([fr, fr], axis=-1)
    return (np.ascontiguousarray(np.cos(emb).T.astype(np.float32)),
            np.ascontiguousarray(np.sin(emb).T.astype(np.float32)))


def build_phase_a(nc, pr, T=2048, SCH=4, KVG=4, stage=9, fused=False, coll_hook=None):
    NCH = T // C
    NSC = NCH // SCH
    SCT = SCH * C
    half = NCH // 2
    io = {}

    def din(name, shape, dt=F32, kind="ExternalInput"):
        h = nc.dram_tensor(name, shape, dt, kind=kind)
        io[name] = h
        return h.ap()

    x = din("x", [T, HID])
    wq = din("wq", [HID, NQ * D])
    wkv = din("wkv", [HID, 1024])
    cosf = din("cosf", [D, T])
    sinf = din("sinf", [D, T])
    triUs_d = din("triUs", [C, C])
    oness_d = din("oness", [C, C])
    sixcol_d = din("sixcol", [C, 1])
    trimask_d = din("trimask", [C, 512])
    maskA_d = din("maskA", [C, 512])
    maskB_d = din("maskB", [C, 512])
    mrot_d = din("mrot", [C, C])
    idn_d = din("idn", [C, C])
    twocol_d = din("twocol", [C, 1])
    epscol_d = din("epscol", [C, 1])
    if fused:
        o_send = nc.dram_tensor("o_send", [NCH * C, NQ * C], BF16, kind="Internal")
        io["o_send"] = o_send
        o_lo = o_hi = o_send.ap()
    else:
        o_lo = din("o_lo", [half * C, NQ * C], BF16, kind="ExternalOutput")
        o_hi = din("o_hi", [half * C, NQ * C], BF16, kind="ExternalOutput")

    A = lambda name, shape, dt: nc.alloc_sbuf_tensor(name, shape, dt)
    c_triUs = A("c_triUs", [C, C], F32)
    c_oness = A("c_oness", [C, C], F32)
    c_sixcol = A("c_sixcol", [C, 1], F32)
    c_trimask = A("c_trimask", [C, 512], BF16)
    c_maskA = A("c_maskA", [C, 512], BF16)
    c_maskB = A("c_maskB", [C, 512], BF16)
    c_mrot = A("c_mrot", [C, C], BF16)
    c_idn = A("c_idn", [C, C], BF16)
    c_two = A("c_two", [C, 1], BF16)
    c_eps = A("c_eps", [C, 1], F32)
    c_cos = A("c_cos", [D, T], BF16)
    c_sin = A("c_sin", [D, T], BF16)
    for t_, src in [(c_triUs, triUs_d), (c_oness, oness_d), (c_sixcol, sixcol_d),
                    (c_eps, epscol_d)]:
        pr.op("sync", lambda e, t_=t_, s=src: e.dma_start(out=t_[:], in_=s[:, :]),
              writes=[t_], dma=True)
    for t_, src in [(c_trimask, trimask_d), (c_maskA, maskA_d), (c_maskB, maskB_d)]:
        pr.op("gpsimd", lambda e, t_=t_, s=src: e.dma_start(out=t_[:], in_=s[:, :]),
              writes=[t_], dma=True)
    for t_, src in [(c_mrot, mrot_d), (c_idn, idn_d), (c_two, twocol_d),
                    (c_cos, cosf), (c_sin, sinf)]:
        pr.op("gpsimd", lambda e, t_=t_, s=src: e.dma_start(out=t_[:], in_=s[:, :]),
              writes=[t_], dma=True)

    x_tm = [A("x_tm0", [C, HID], F32)]
    xn_tm = [A("xn_tm0", [C, HID], BF16)]
    wr_extra = A("wr_extra", [C, 4096], BF16)
    xn_fm = A("xn_fm", [C, DC * SCT], BF16)
    wqb = [A(f"wqb{i}", [C, DC * D], BF16) for i in range(2)]
    wkvb = [A(f"wkvb{i}", [C, KVG * 1024], BF16) for i in range(2)]
    k_tm = A("k_tm", [C, SCH * 512], F32)
    expq = A("expq", [C, NQ * SCT], BF16)
    rq_ar = A("rq_ar", [C, NQ * SCT], BF16)
    q_tmp = [A(f"q_tmp{i}", [C, SCT], BF16) for i in range(2)]
    rt1 = [A("rt1_0", [C, SCT], BF16)]
    NS = SCH + 1
    v_ring = A("v_ring", [C, NKV * NS * C], BF16)
    rk_ring = A("rk_ring", [C, NKV * 3 * C], BF16)
    S_f32 = A("S_f32", [C, NKV * C], F32)
    S_bf = A("S_bf", [C, NKV * C], BF16)
    stats = A("stats", [C, 48], F32)
    mv = A("mv", [C, 2], F32)
    rstd = A("rstd", [C, 1], F32)
    nmean = A("nmean", [C, 1], F32)
    gs0 = A("gs0", [C, 512], F32)
    gs1 = A("gs1", [C, 512], F32)
    gtm = A("gtm", [C, 512], F32)
    expk = A("expk", [C, 512], F32)
    ksum = A("ksum", [C, NKV], F32)
    rk_col = A("rk_col", [C, NKV], F32)
    enG = A("enG", [C, 512], F32)
    eBC = A("eBC", [C, 512], BF16)
    lamc2 = A("lamc2", [C, 2 * NKV], F32)
    eGf2 = A("eGf2", [C, 2 * 512], BF16)
    kt1 = A("kt1", [C, 512], F32)
    ktm = A("ktm", [C, 512], BF16)
    khm2 = A("khm2", [C, 2 * 512], BF16)
    kb16 = A("kb16", [C, 512], BF16)
    kfm = A("kfm", [C, NKV * C], BF16)
    ktfm2 = A("ktfm2", [C, 2 * NKV * C], BF16)
    rk1 = A("rk1", [C, 512], BF16)
    rk2 = A("rk2", [C, 512], BF16)
    qt_ar = A("qt_ar", [C, NQ * C], BF16)
    rq_col = A("rq_col", [C, NQ], F32)
    ptm = A("ptm", [C, 512], BF16)
    wtmp = A("wtmp", [C, 512], F32)
    expsA = A("expsA", [C, 512], BF16)
    expsB = A("expsB", [C, 512], BF16)
    rw_col = A("rw_col", [C, NQ], F32)
    t_y = A("t_y", [C, C], F32)
    o_tm = A("o_tm", [C, NQ * C], BF16)
    ostg = [A("ostg0", [C, NQ * C], BF16)]

    PS = [nc.alloc_psum_tensor(f"bank{i}", [C, 512], F32) for i in range(8)]
    pr.register_psum(PS)

    def mm(out_ap, bank, lhsT, rhs, reads, start, stop, label="", tp=None):
        pr.op("tensor",
              lambda e: e.matmul(out_ap, lhsT, rhs, start=start, stop=stop,
                                 tile_position=tp, skip_group_check=True),
              reads=reads, writes=[bank], label=label)

    pr.op("vector", lambda e: e.memset(o_tm[:], 0.0), writes=[(o_tm, None)])
    pr.op("vector", lambda e: e.memset(S_f32[:], 0.0), writes=[S_f32])
    pr.op("vector", lambda e: e.memset(S_bf[:], 0.0), writes=[S_bf])

    for sc in range(NSC):
        ts0 = sc * SCT
        for cl in range(SCH):
            c = sc * SCH + cl
            xb = x_tm[0]
            nb = xn_tm[0]
            for xh in range(2):
                pr.op("sync", lambda e, xb=xb, c=c, xh=xh: e.dma_start(
                    out=xb[:, xh * 2048:(xh + 1) * 2048],
                    in_=x[c * C:(c + 1) * C, xh * 2048:(xh + 1) * 2048]),
                      writes=[(xb, xh)], dma=True)
                for sl4 in range(4):
                    sl8 = xh * 4 + sl4
                    pr.op("vector", lambda e, xb=xb, sl8=sl8, xh=xh: e.bn_stats(
                        stats[:, sl8 * 6:(sl8 + 1) * 6],
                        xb[:, sl8 * 512:(sl8 + 1) * 512]),
                          reads=[(xb, xh)], writes=[(stats, sl8)])
            pr.op("vector", lambda e: e.bn_aggr(mv[:], stats[:]),
                  reads=[(stats, None)], writes=[mv])
            pr.op("scalar", lambda e: e.activation(rstd[:], mv[:, 1:2], AF.Ln, bias=c_eps[:, 0:1]),
                  reads=[mv, c_eps], writes=[(rstd, "ln")])
            pr.op("scalar", lambda e: e.activation(rstd[:], rstd[:], AF.Exp, scale=-0.5),
                  reads=[(rstd, "ln")], writes=[rstd])
            pr.op("vector", lambda e: e.tensor_scalar(nmean[:], mv[:, 0:1], -1.0, None,
                                                      ALU.mult),
                  reads=[mv], writes=[nmean])
            for xh in range(2):
                pr.op("vector", lambda e, xb=xb, nb=nb, xh=xh: e.tensor_scalar(
                    nb[:, xh * 2048:(xh + 1) * 2048], xb[:, xh * 2048:(xh + 1) * 2048],
                    nmean[:, 0:1], rstd[:, 0:1], ALU.add, ALU.mult),
                      reads=[(xb, xh), nmean, rstd], writes=[(nb, xh)])
            for dc in range(DC):
                if DMAT:
                    q = "sync"
                    pr.op(q, lambda e, nb=nb, dc=dc, cl=cl: e.dma_start(
                        out=xn_fm[:, dc * SCT + cl * C: dc * SCT + (cl + 1) * C],
                        in_=nb[:, dc * C:(dc + 1) * C], transpose=True),
                          reads=[(nb, dc // 16)], writes=[(xn_fm, (dc, cl))],
                          dma=True, label="xnT")
                else:
                    bk = PS[6 + (dc % 2)]
                    pv = bk[:, 0:64].bitcast(BF16)
                    pr.op("tensor", lambda e, nb=nb, dc=dc, pv=pv: e.transpose(
                        pv, nb[:, dc * C:(dc + 1) * C], c_idn[:]),
                          reads=[(nb, dc // 16), c_idn], writes=[bk], label="xnT")
                    pr.op("scalar", lambda e, dc=dc, cl=cl, pv=pv: e.copy(
                        xn_fm[:, dc * SCT + cl * C: dc * SCT + (cl + 1) * C], pv),
                          reads=[bk], writes=[(xn_fm, (dc, cl))])
        if stage < 2:
            continue
        for gg in range(DC // KVG):
            wb = wkvb[gg % 2]
            pr.op("gpsimd", lambda e, wb=wb, gg=gg: e.dma_start(
                out=wb[:, :].rearrange("p (a n) -> p a n", a=KVG),
                in_=wkv[gg * KVG * C:(gg + 1) * KVG * C, :].rearrange(
                    "(a p) n -> p a n", p=C)),
                  writes=[wb], dma=True)
            for di in range(KVG):
                dc = gg * KVG + di
                for t in range(SCH):
                    lhs = xn_fm[:, dc * SCT + t * C: dc * SCT + (t + 1) * C]
                    mm(PS[t][:, :], PS[t], lhs, wb[:, di * 1024: di * 1024 + 512],
                       [(xn_fm, (dc, t)), wb], dc == 0, dc == DC - 1, "kmm")
                    mm(PS[4 + t][:, :], PS[4 + t], lhs,
                       wb[:, di * 1024 + 512: (di + 1) * 1024],
                       [(xn_fm, (dc, t)), wb], dc == 0, dc == DC - 1, "vmm")
        for t in range(SCH):
            c = sc * SCH + t
            pr.op("vector", lambda e, t=t: e.tensor_copy(
                k_tm[:, t * 512:(t + 1) * 512], PS[t][:, :]),
                  reads=[PS[t]], writes=[(k_tm, t)])
            dst = v_ring[:, :].rearrange("p (g s) -> p g s", g=NKV)[
                :, :, (c % NS) * C:(c % NS + 1) * C]
            pr.op("vector", lambda e, t=t, dst=dst: e.tensor_copy(
                dst, PS[4 + t][:, :].rearrange("p (g s) -> p g s", g=NKV)),
                  reads=[PS[4 + t]], writes=[(v_ring, c % NS)])
        for h in range(NQ if stage >= 3 else 0):
            wb = wqb[h % 2]
            for qg in range(8):
                pr.op("gpsimd", lambda e, wb=wb, h=h, qg=qg: e.dma_start(
                    out=wb[:, qg * 4 * D:(qg + 1) * 4 * D].rearrange(
                        "p (a n) -> p a n", a=4),
                    in_=wq[qg * 4 * C:(qg + 1) * 4 * C,
                           h * D:(h + 1) * D].rearrange("(a p) n -> p a n", p=C)),
                      writes=[(wb, qg)], dma=True)
            import os
            qsub = int(os.environ.get("QSUB", "9"))
            psq = PS[4 + h % 2]
            for dc in range(DC if qsub >= 1 else 0):
                mm(psq[:, 0:SCT], psq, wb[:, dc * D:(dc + 1) * D],
                   xn_fm[:, dc * SCT:(dc + 1) * SCT],
                   [wb, (xn_fm, None)], dc == 0, dc == DC - 1, "qmm")
            if qsub < 2:
                continue
            pr.op("scalar", lambda e, h=h, psq=psq: e.activation(
                expq[:, h * SCT:(h + 1) * SCT], psq[:, 0:SCT], AF.Exp),
                  reads=[psq], writes=[(expq, h)])
            qt = q_tmp[h % 2]
            if qsub < 3:
                continue
            pr.op("vector", lambda e, qt=qt, psq=psq: e.tensor_copy(qt[:], psq[:, 0:SCT]),
                  reads=[psq], writes=[qt])
            if qsub < 4:
                continue
            rb = PS[6 + h % 2]
            rbv = rb[:, 0:SCT]
            mm(rbv, rb, c_mrot[:], qt[:], [c_mrot, qt], True, True, "rotq")
            if qsub < 5:
                continue
            t1 = rt1[0]
            pr.op("vector", lambda e, qt=qt, t1=t1, ts0=ts0: e.tensor_tensor(
                t1[:], qt[:], c_cos[:, ts0:ts0 + SCT], ALU.mult),
                  reads=[qt, c_cos], writes=[t1])
            pr.op("vector", lambda e, h=h, rbv=rbv, rb=rb, ts0=ts0: e.tensor_tensor(
                rq_ar[:, h * SCT:(h + 1) * SCT], rbv, c_sin[:, ts0:ts0 + SCT],
                ALU.mult),
                  reads=[rb, c_sin], writes=[(rq_ar, (h, 0))], label="rq_sin")
            pr.op("vector", lambda e, h=h, t1=t1: e.tensor_tensor(
                rq_ar[:, h * SCT:(h + 1) * SCT], rq_ar[:, h * SCT:(h + 1) * SCT],
                t1[:], ALU.add),
                  reads=[(rq_ar, (h, 0)), t1], writes=[(rq_ar, h)], label="rq_add")
        def emit_kside(cl):
            c = sc * SCH + cl
            s = cl % 2
            ks = k_tm[:, cl * 512:(cl + 1) * 512]
            kread = (k_tm, cl)
            kcur = c % 3
            eGf = eGf2[:, s * 512:(s + 1) * 512]
            khm = khm2[:, s * 512:(s + 1) * 512]
            lamc = lamc2[:, s * NKV:(s + 1) * NKV]
            pr.op("scalar", lambda e, ks=ks: e.activation(gs0[:], ks, AF.Abs),
                  reads=[kread], writes=[gs0])
            pr.op("scalar", lambda e: e.activation(gs1[:], gs0[:], AF.Exp, scale=-1.0),
                  reads=[gs0], writes=[gs1])
            pr.op("scalar", lambda e: e.activation(gs1[:], gs1[:], AF.Ln, bias=1.0),
                  reads=[(gs1, None)], writes=[(gs1, "l")])
            pr.op("vector", lambda e, ks=ks: e.tensor_scalar(
                gs0[:], ks, 0.0, None, ALU.min), reads=[kread], writes=[(gs0, "m")])
            pr.op("vector", lambda e: e.tensor_tensor(
                gtm[:], gs0[:], gs1[:], ALU.subtract),
                  reads=[(gs0, "m"), (gs1, "l")], writes=[gtm])
            for g in range(NKV):
                pr.op("scalar", lambda e, g=g, ks=ks: e.activation(
                    expk[:, g * C:(g + 1) * C], ks[:, g * C:(g + 1) * C], AF.Exp,
                    accum_out=ksum[:, g:g + 1]),
                      reads=[kread], writes=[(expk, g), (ksum, g)])
            pr.op("vector", lambda e: e.reciprocal(rk_col[:], ksum[:]),
                  reads=[(ksum, None)], writes=[rk_col])
            mm(PS[0][:, :], PS[0], c_triUs[:], gtm[:], [c_triUs, gtm], True, True, "Gtm")
            mm(PS[1][:, :], PS[1], c_oness[:], gtm[:], [c_oness, gtm], True, True, "GBC")
            pr.op("scalar", lambda e: e.activation(enG[:], PS[0][:, :], AF.Exp, scale=-1.0),
                  reads=[PS[0]], writes=[enG])
            pr.op("scalar", lambda e: e.activation(eBC[:], PS[1][:, :], AF.Exp),
                  reads=[PS[1]], writes=[eBC])
            for g in range(NKV):
                mm(PS[0][:, g:g + 1], PS[0], gtm[:, g * C:(g + 1) * C], c_sixcol[:],
                   [gtm, c_sixcol], True, True, "lam")
            pr.op("scalar", lambda e, lamc=lamc: e.activation(
                lamc[:, 0:NKV], PS[0][:, 0:NKV], AF.Exp),
                  reads=[PS[0]], writes=[(lamc2, s)])
            for g in range(NKV):
                mm(PS[1][:, g * C:(g + 1) * C], PS[1], gtm[:, g * C:(g + 1) * C],
                   c_triUs[:], [gtm, c_triUs], g == 0, g == NKV - 1, "Gfm")
            pr.op("scalar", lambda e, eGf=eGf: e.activation(eGf[:, 0:512], PS[1][:, :],
                                                            AF.Exp),
                  reads=[PS[1]], writes=[(eGf2, s)])
            pr.op("vector", lambda e: e.tensor_tensor(kt1[:], expk[:], enG[:], ALU.mult),
                  reads=[(expk, None), enG], writes=[kt1])
            for g in range(NKV):
                pr.op("vector", lambda e, g=g: e.tensor_scalar(
                    ktm[:, g * C:(g + 1) * C], kt1[:, g * C:(g + 1) * C],
                    rk_col[:, g:g + 1], None, ALU.mult),
                      reads=[kt1, rk_col], writes=[(ktm, g)])
            pr.op("vector", lambda e, khm=khm: e.tensor_tensor(
                khm[:, 0:512], ktm[:], eBC[:], ALU.mult),
                  reads=[(ktm, None), eBC], writes=[(khm2, s)])
            pr.op("vector", lambda e, ks=ks: e.tensor_copy(kb16[:], ks),
                  reads=[kread], writes=[kb16])
            for g in range(NKV):
                bk = PS[g % 2]
                pv = bk[:, 0:64].bitcast(BF16)
                pr.op("tensor", lambda e, g=g, pv=pv: e.transpose(
                    pv, kb16[:, g * C:(g + 1) * C], c_idn[:]),
                      reads=[kb16, c_idn], writes=[bk], label="kT")
                pr.op("scalar", lambda e, g=g, pv=pv: e.copy(
                    kfm[:, g * C:(g + 1) * C], pv), reads=[bk], writes=[(kfm, g)])
                bk2 = PS[(g + 1) % 2]
                pv2 = bk2[:, 64:128].bitcast(BF16)
                pr.op("tensor", lambda e, g=g, pv2=pv2: e.transpose(
                    pv2, ktm[:, g * C:(g + 1) * C], c_idn[:]),
                      reads=[(ktm, g), c_idn], writes=[bk2], label="KtT")
                pr.op("scalar", lambda e, g=g, pv2=pv2: e.copy(
                    ktfm2[:, s * 512 + g * C: s * 512 + (g + 1) * C], pv2),
                      reads=[bk2], writes=[(ktfm2, (s, g))])
            rb = PS[0]
            rbv = rb[:, 0:512]
            mm(rbv, rb, c_mrot[:], kfm[:, :], [c_mrot, (kfm, None)], True, True, "rotk")
            cos3 = c_cos[:, c * C:(c + 1) * C].unsqueeze(1).broadcast_to((C, NKV, C))
            sin3 = c_sin[:, c * C:(c + 1) * C].unsqueeze(1).broadcast_to((C, NKV, C))
            pr.op("vector", lambda e, cos3=cos3: e.tensor_tensor(
                rk1[:, :].rearrange("p (g s) -> p g s", g=NKV),
                kfm[:, :].rearrange("p (g s) -> p g s", g=NKV), cos3, ALU.mult),
                  reads=[(kfm, None), c_cos], writes=[rk1])
            pr.op("vector", lambda e, sin3=sin3, rbv=rbv, rb=rb: e.tensor_tensor(
                rk2[:, :].rearrange("p (g s) -> p g s", g=NKV),
                rbv.rearrange("p (g s) -> p g s", g=NKV), sin3, ALU.mult),
                  reads=[rb, c_sin], writes=[rk2])
            rks = rk_ring[:, :].rearrange("p (g s) -> p g s", g=NKV)[
                :, :, kcur * C:(kcur + 1) * C]
            pr.op("vector", lambda e, rks=rks: e.tensor_tensor(
                rks, rk1[:, :].rearrange("p (g s) -> p g s", g=NKV),
                rk2[:, :].rearrange("p (g s) -> p g s", g=NKV), ALU.add),
                  reads=[rk1, rk2], writes=[(rk_ring, kcur)])

        def emit_mixer(cl):
            c = sc * SCH + cl
            s = cl % 2
            kcur = c % 3
            kprv = (c - 1) % 3
            vcur = c % NS
            vprv = (c - 1) % NS
            eGf = eGf2[:, s * 512:(s + 1) * 512]
            khm = khm2[:, s * 512:(s + 1) * 512]
            lamc = lamc2[:, s * NKV:(s + 1) * NKV]
            for h in range(NQ):
                g = h // GRP
                pr.op("vector", lambda e, h=h, g=g, cl=cl, eGf=eGf: e.tensor_tensor(
                    qt_ar[:, h * C:(h + 1) * C],
                    expq[:, h * SCT + cl * C: h * SCT + (cl + 1) * C],
                    eGf[:, g * C:(g + 1) * C], ALU.mult),
                      reads=[(expq, h), (eGf2, s)], writes=[(qt_ar, h)])
                mm(PS[5][:, h:h + 1], PS[5],
                   expq[:, h * SCT + cl * C: h * SCT + (cl + 1) * C], c_two[:],
                   [(expq, h), c_two], True, True, "qsum")
            pr.op("vector", lambda e: e.reciprocal(rq_col[:], PS[5][:, 0:NQ]),
                  reads=[PS[5]], writes=[rq_col])
            for g in range(NKV):
                vsl = v_ring[:, g * NS * C + vcur * C: g * NS * C + (vcur + 1) * C]
                vrc = (v_ring, vcur)
                mm(PS[2][:, :], PS[2], ktfm2[:, s * 512 + g * C: s * 512 + (g + 1) * C],
                   qt_ar[:, g * GRP * C:(g + 1) * GRP * C],
                   [(ktfm2, (s, g)), (qt_ar, None)], True, True, "PT")
                pr.op("vector", lambda e: e.tensor_tensor(
                    ptm[:], PS[2][:, :], c_trimask[:], ALU.mult),
                      reads=[PS[2], c_trimask], writes=[ptm])
                for j in range(GRP):
                    h = g * GRP + j
                    mm(PS[5][:, j * C:(j + 1) * C], PS[5], qt_ar[:, h * C:(h + 1) * C],
                       S_bf[:, g * C:(g + 1) * C],
                       [(qt_ar, h), (S_bf, g)], True, False, "o_int")
                    mm(PS[5][:, j * C:(j + 1) * C], PS[5], ptm[:, j * C:(j + 1) * C],
                       vsl, [ptm, vrc], False, True, "o_intra")
                mm(PS[3][:, 0:C], PS[3], khm[:, g * C:(g + 1) * C], vsl,
                   [(khm2, s), vrc], True, True, "dS")
                pr.op("vector", lambda e, g=g, lamc=lamc: e.scalar_tensor_tensor(
                    S_f32[:, g * C:(g + 1) * C], S_f32[:, g * C:(g + 1) * C],
                    lamc[:, g:g + 1], PS[3][:, 0:C], ALU.mult, ALU.add),
                      reads=[(S_f32, g), (lamc2, s), PS[3]], writes=[(S_f32, g)])
                pr.op("scalar", lambda e, g=g: e.copy(
                    S_bf[:, g * C:(g + 1) * C], S_f32[:, g * C:(g + 1) * C]),
                      reads=[(S_f32, g)], writes=[(S_bf, g)])
                rqv = rq_ar[:, :].rearrange("p (h s) -> p h s", h=NQ)[
                    :, g * GRP:(g + 1) * GRP, cl * C:(cl + 1) * C]
                rkB = rk_ring[:, g * 3 * C + kcur * C: g * 3 * C + (kcur + 1) * C]
                mm(PS[7][:, :], PS[7], rkB, rqv, [(rk_ring, kcur), (rq_ar, None)],
                   True, True, "scB")
                pr.op("vector", lambda e: e.tensor_tensor(
                    wtmp[:], PS[7][:, :], c_maskB[:], ALU.add),
                      reads=[PS[7], c_maskB], writes=[wtmp])
                pr.op("scalar", lambda e: e.activation(expsB[:], wtmp[:], AF.Exp,
                                                       scale=SCALE),
                      reads=[wtmp], writes=[expsB])
                if c > 0:
                    rkA = rk_ring[:, g * 3 * C + kprv * C + 64:
                                  g * 3 * C + kprv * C + 128]
                    mm(PS[6][64:128, :], PS[6], rkA, rqv,
                       [(rk_ring, kprv), (rq_ar, None)], True, True, "scA", tp=(0, 64))
                    pr.op("vector", lambda e: e.tensor_tensor(
                        wtmp[64:128, :], PS[6][64:128, :], c_maskA[64:128, :], ALU.add),
                          reads=[PS[6], c_maskA], writes=[(wtmp, "A")])
                    pr.op("scalar", lambda e: e.activation(
                        expsA[64:128, :], wtmp[64:128, :], AF.Exp, scale=SCALE),
                          reads=[(wtmp, "A")], writes=[expsA])
                for j in range(GRP):
                    h = g * GRP + j
                    if c > 0:
                        mm(PS[2][:, 16 + h:17 + h], PS[2],
                           expsA[64:128, j * C:(j + 1) * C], c_two[64:128, :],
                           [expsA, c_two], True, False, "wsA", tp=(64, 0))
                    mm(PS[2][:, 16 + h:17 + h], PS[2], expsB[:, j * C:(j + 1) * C],
                       c_two[:], [expsB, c_two], c == 0, True, "wsB")
                    if c > 0:
                        vprev = v_ring[64:128, g * NS * C + vprv * C:
                                       g * NS * C + vprv * C + C]
                        mm(PS[4][:, j * C:(j + 1) * C], PS[4],
                           expsA[64:128, j * C:(j + 1) * C], vprev,
                           [expsA, (v_ring, vprv)], True, False, "yA", tp=(64, 0))
                    mm(PS[4][:, j * C:(j + 1) * C], PS[4],
                       expsB[:, j * C:(j + 1) * C], vsl,
                       [expsB, vrc], c == 0, True, "yB")
                pr.op("vector", lambda e, g=g: e.reciprocal(
                    rw_col[:, g * GRP:(g + 1) * GRP],
                    PS[2][:, 16 + g * GRP:16 + (g + 1) * GRP]),
                      reads=[PS[2]], writes=[(rw_col, g)])
                for j in range(GRP):
                    h = g * GRP + j
                    pr.op("scalar", lambda e, h=h, j=j: e.activation(
                        t_y[:], PS[4][:, j * C:(j + 1) * C], AF.Copy,
                        scale=rw_col[:, h:h + 1]),
                          reads=[PS[4], (rw_col, g)], writes=[t_y])
                    pr.op("vector", lambda e, h=h, j=j: e.scalar_tensor_tensor(
                        o_tm[:, h * C:(h + 1) * C], PS[5][:, j * C:(j + 1) * C],
                        rq_col[:, h:h + 1], t_y[:], ALU.mult, ALU.add),
                          reads=[PS[5], rq_col, t_y], writes=[(o_tm, h)])
            stg = ostg[0]
            for h in range(NQ):
                bk = PS[6 + h % 2]
                pv = bk[:, 0:64].bitcast(BF16)
                pr.op("tensor", lambda e, h=h, pv=pv: e.transpose(
                    pv, o_tm[:, h * C:(h + 1) * C], c_idn[:]),
                      reads=[(o_tm, h), c_idn], writes=[bk], label="oT")
                pr.op("scalar", lambda e, h=h, pv=pv, stg=stg: e.copy(
                    stg[:, h * C:(h + 1) * C], pv), reads=[bk], writes=[(stg, h)])
            if fused:
                tgt, lo = o_lo, c
            else:
                tgt, lo = (o_lo, c) if c < half else (o_hi, c - half)
            pr.op("scalar", lambda e, tgt=tgt, lo=lo, stg=stg: e.dma_start(
                out=tgt[lo * C:(lo + 1) * C, :], in_=stg[:]),
                  reads=[(stg, None)], writes=[(tgt, lo)], dma=True)
            if fused and coll_hook is not None:
                coll_hook(pr, io["o_send"], c)

        if stage >= 4:
            emit_kside(0)
            if SCH > 1:
                emit_kside(1)
            for cl in range(SCH):
                emit_mixer(cl)
                if cl + 2 < SCH:
                    emit_kside(cl + 2)
    io["pool"] = {
        "ofl_blocks": ([(xn_fm, i) for i in range(16)] +
                       [(expq, i) for i in range(8)] +
                       [(rq_ar, i) for i in range(8)]),
        "wring": [xn_tm[0][:, :], wr_extra[:, :], wqb[0][:, :], wqb[1][:, :],
                  wkvb[0][:, :], wkvb[1][:, :],
                  x_tm[0][:, :].bitcast(BF16)[:, 0:4096],
                  x_tm[0][:, :].bitcast(BF16)[:, 4096:8192]],
        "x1r": x_tm[0],
        "f32s": [gs0, gs1, gtm, expk, enG, kt1, wtmp],
        "hsb": qt_ar, "hwf": o_tm,
        "xn2tm": [wqb[0], wqb[1]],
        "c_idn": c_idn, "c_eps": c_eps, "c_two": c_two,
        "small": T < 2048,
        "bcb": k_tm[:, :].bitcast(BF16),
        "wrab_pieces": [c_cos[:, :], c_sin[:, :], ktfm2[:, :]],
        "k_tm": k_tm, "c_cos_t": c_cos, "c_sin_t": c_sin, "ktfm_t": ktfm2,
        "rk1": rk1, "rk2": rk2, "ptm": ptm,
        "router_scratch": eBC[:, :].bitcast(F32),
        "statsAll": wtmp[:, 64:448],
        "eBC_t": eBC,
        "stats": stats, "mv": mv, "rstd": rstd, "nmean": nmean,
        "PS": PS,
    }
    return io


def build_phase_b(nc, pr, NTOK=1024, pool=None, o_parts=None, fstage=9):
    NCHB = NTOK // C
    NH = 32
    fused = pool is not None
    io = {}

    def din(name, shape, dt=F32, kind="ExternalInput"):
        h = nc.dram_tensor(name, shape, dt, kind=kind)
        io[name] = h
        return h.ap()

    x_own = din("x_own", [NTOK, HID])
    wo = din("wo", [NH * C, HID])
    wbT = din("wbT", [HID, HID])
    wra = din("wra", [HID, 136])
    bc = din("bc", [C, HID])
    bbias = din("bbias", [1, HID])
    onerow_d = din("onerow", [1, C])
    y = din("y", [NTOK, HID], kind="ExternalOutput")
    x1 = nc.dram_tensor("x1", [NTOK, HID], F32, kind="Internal").ap()

    A = lambda name, shape, dt: nc.alloc_sbuf_tensor(name, shape, dt)

    if fused:
        PS = pool["PS"]
        c_idn2 = pool["c_idn"]
        c_eps2 = pool["c_eps"]
        blocks = pool["ofl_blocks"]
        wring = pool["wring"]
        wring_units = [ap.tensor for ap in wring]
        x1r0 = pool["x1r"]
        f32s = pool["f32s"]
        xsl = [f32s[0], f32s[1]]
        x1s = [f32s[2], f32s[3]]
        outs = [f32s[4], f32s[5]]
        hsb_t = pool["hsb"]
        hwf_t = pool["hwf"]
        xn2tm = pool["xn2tm"]
        o_send, cc_out = o_parts
        bar = [t for t, _ in blocks] + [x1r0, hsb_t, hwf_t] + list(xn2tm)
        seen = set()
        bar2 = []
        for t in bar:
            if t.name not in seen:
                seen.add(t.name)
                bar2.append(t)
        pr.op("vector", lambda e: e.engine_nop(),
              writes=[(t, None) for t in bar2])
    else:
        PS = [nc.alloc_psum_tensor(f"bankB{i}", [C, 512], F32) for i in range(8)]
        pr.register_psum(PS)
        c_idn2 = A("c_idn2", [C, C], BF16)
        idn2_d = din("idn2", [C, C])
        c_eps2 = A("c_eps2", [C, 1], F32)
        epscol2_d = din("epscol2", [C, 1])
        pr.op("gpsimd", lambda e: e.dma_start(out=c_idn2[:], in_=idn2_d[:, :]),
              writes=[c_idn2], dma=True)
        pr.op("sync", lambda e: e.dma_start(out=c_eps2[:], in_=epscol2_d[:, :]),
              writes=[c_eps2], dma=True)
        o_full = din("o_full", [NCHB * C, NH * C], BF16)
        ofl = A("ofl", [C, NH * NCHB * C], BF16)
        blocks = [(ofl, i) for i in range(NH)]
        wr0 = [A(f"wrg{i}", [C, 4096], BF16) for i in range(8)]
        wring = [t[:, :] for t in wr0]
        wring_units = wr0
        x1r0 = A("x1r0", [C, HID], F32)
        xsl = [A(f"xsl{i}", [C, 512], F32) for i in range(2)]
        x1s = [A(f"x1s{i}", [C, 512], F32) for i in range(2)]
        outs = [A(f"outs{i}", [C, 512], F32) for i in range(2)]
        hsb_t = A("hsb", [C, NCHB * C], BF16)
        hwf_t = A("hwf", [C, NCHB * C], BF16)
        xn2tm = [A(f"xn2tm{i}", [C, HID], BF16) for i in range(2)]

    BST = NCHB * C
    if fused:
        assert NCHB == 8, "fused phase B requires NTOK=1024"

    def blk_ap(b, lo, hi):
        t, i = blocks[b]
        return t[:, i * BST + lo: i * BST + hi]

    if fused:
        c_one2 = pool["rk2"]
        c_one2_u = (pool["rk2"], None)
    else:
        c_one2 = A("c_one2", [1, C], BF16)
        c_one2_u = (c_one2, None)
    pr.op("gpsimd", lambda e: e.dma_start(out=c_one2[0:1, 0:C], in_=onerow_d[:, :]),
          writes=[c_one2_u], dma=True)
    if fused and not pool.get("small"):
        wp = pool["wrab_pieces"]
        wp_units = [pool["c_cos_t"], pool["c_sin_t"], pool["ktfm_t"]]
        def wrab_ap(dc):
            if dc < 15:
                return wp[0][:, dc * 136:(dc + 1) * 136]
            if dc < 30:
                return wp[1][:, (dc - 15) * 136:(dc - 14) * 136]
            return wp[2][:, (dc - 30) * 136:(dc - 29) * 136]
        def wrab_unit(dc):
            return (wp_units[0 if dc < 15 else (1 if dc < 30 else 2)], None)
        pr.op("vector", lambda e: e.engine_nop(),
              writes=[(t, None) for t in wp_units] + [(pool["k_tm"], None)])
        for dcg, (lo, hi) in enumerate([(0, 15), (15, 30), (30, 32)]):
            pr.op("gpsimd", lambda e, lo=lo, hi=hi: e.dma_start(
                out=wp[0 if lo == 0 else (1 if lo == 15 else 2)][
                    :, 0:(hi - lo) * 136].rearrange("p (a n) -> p a n", a=hi - lo),
                in_=wra[lo * C:hi * C, :].rearrange("(a p) n -> p a n", p=C)),
                  writes=[(wp_units[dcg], None)], dma=True)
        bcb = pool["bcb"]
        bcb_unit = (pool["k_tm"], None)
        pr.op("gpsimd", lambda e: e.dma_start(out=bcb[:, 0:HID], in_=bc[:, :]),
              writes=[bcb_unit], dma=True)
        bbr = pool["rk1"]
        bbr_u = (pool["rk1"], None)
        bbr_per_fcw = True
    else:
        wrab = A("wrab", [C, DC * 136], BF16)
        pr.op("gpsimd", lambda e: e.dma_start(
            out=wrab[:, :].rearrange("p (a n) -> p a n", a=DC),
            in_=wra[:, :].rearrange("(a p) n -> p a n", p=C)), writes=[wrab], dma=True)
        def wrab_ap(dc):
            return wrab[:, dc * 136:(dc + 1) * 136]
        def wrab_unit(dc):
            return (wrab, None)
        bcb = A("bcb", [C, HID], BF16)
        bcb_unit = (bcb, None)
        pr.op("gpsimd", lambda e: e.dma_start(out=bcb[:, :], in_=bc[:, :]),
              writes=[bcb_unit], dma=True)
        bbr = A("bbr", [1, HID], BF16)
        bbr_u = (bbr, None)
        bbr_per_fcw = False
        pr.op("gpsimd", lambda e: e.dma_start(out=bbr[:], in_=bbias[:, :]),
              writes=[bbr], dma=True)
    if fused:
        stats = pool["stats"]; mv = pool["mv"]; rstd = pool["rstd"]; nmean = pool["nmean"]
        logits_all = f32s[6]
        statsAll = pool["statsAll"]
        _rt = pool["router_scratch"]
    else:
        stats = A("statsB", [C, 48], F32)
        mv = A("mvB", [C, 2], F32)
        rstd = A("rstdB", [C, 1], F32)
        nmean = A("nmeanB", [C, 1], F32)
        logits_all = A("logits_all", [C, 8 * NCHB], F32)
        statsAll = A("statsAll", [C, 48 * NCHB], F32)
    if fused:
        probs = _rt[:, 0:8]
        oh1 = _rt[:, 8:16]
        probs2 = _rt[:, 16:24]
        oh2 = _rt[:, 24:32]
        cmb = _rt[:, 32:40]
        pmax = _rt[:, 40:41]
        psum_r = _rt[:, 41:42]
        m2c = _rt[:, 42:43]
        den = _rt[:, 43:44]
        rden = _rt[:, 44:45]
        w1c = _rt[:, 45:46]
        w2c = _rt[:, 46:47]
    else:
        pmax = A("pmax", [C, 1], F32)
        psum_r = A("psum_r", [C, 1], F32)
        probs = A("probs", [C, 8], F32)
        oh1 = A("oh1", [C, 8], F32)
        probs2 = A("probs2", [C, 8], F32)
        m2c = A("m2c", [C, 1], F32)
        oh2 = A("oh2", [C, 8], F32)
        den = A("den", [C, 1], F32)
        rden = A("rden", [C, 1], F32)
        w1c = A("w1c", [C, 1], F32)
        w2c = A("w2c", [C, 1], F32)
        cmb = A("cmb", [C, 8], F32)

    def mm(out_ap, bank, lhsT, rhs, reads, start, stop, label="", tp=None):
        pr.op("tensor",
              lambda e: e.matmul(out_ap, lhsT, rhs, start=start, stop=stop,
                                 tile_position=tp, skip_group_check=True),
              reads=reads, writes=[bank], label=label)

    if fused:
        msel_d = din("msel", [C, 2])
        msel = A("mselt", [C, 2], F32)
        pr.op("sync", lambda e: e.dma_start(out=msel[:], in_=msel_d[:, :]),
              writes=[msel], dma=True)
        candA = pool["hwf"]
        candB = pool["hsb"]
        cc_a, cc_b1, cc_b2 = o_parts[1]
        ccA_ap, ccB1_ap, ccB2_ap = cc_a.ap(), cc_b1.ap(), cc_b2.ap()
        for t in range(NCHB):
            for hb in range(2):
                rA = hb * 1024 + t * C
                if t < NCHB // 2:
                    cb_t, cb_ap, rB = cc_b1, ccB1_ap, hb * 512 + t * C
                else:
                    cb_t, cb_ap, rB = cc_b2, ccB2_ap, hb * 512 + (t - NCHB // 2) * C
                pr.op("sync", lambda e, rA=rA: e.dma_start(
                    out=candA[:, :], in_=ccA_ap[rA:rA + C, :]),
                      reads=[(cc_a, None)], writes=[(candA, None)], dma=True)
                pr.op("sync", lambda e, rB=rB, cb_ap=cb_ap: e.dma_start(
                    out=candB[:, :], in_=cb_ap[rB:rB + C, :]),
                      reads=[(cb_t, None)], writes=[(candB, None)], dma=True)
                pr.op("vector", lambda e: e.tensor_scalar(
                    candA[:, :], candA[:, :], msel[:, 0:1], None, ALU.mult),
                      reads=[(candA, None), msel], writes=[(candA, None)])
                hb0 = hb * 16
                b0 = hb0
                while b0 < hb0 + 16:
                    tns, i0 = blocks[b0]
                    b1 = b0
                    while (b1 + 1 < hb0 + 16 and blocks[b1 + 1][0] is tns
                           and blocks[b1 + 1][1] == blocks[b1][1] + 1):
                        b1 += 1
                    nb = b1 - b0 + 1
                    hoff = b0 - hb0
                    dst = tns[:, i0 * BST:(i0 + nb) * BST].rearrange(
                        "p (h n) -> p h n", h=nb)[:, :, t * C:(t + 1) * C]
                    pr.op("vector", lambda e, dst=dst, hoff=hoff, nb=nb: e.scalar_tensor_tensor(
                        dst, candB[:, hoff * C:(hoff + nb) * C].rearrange(
                            "p (h n) -> p h n", h=nb),
                        msel[:, 1:2],
                        candA[:, hoff * C:(hoff + nb) * C].rearrange(
                            "p (h n) -> p h n", h=nb),
                        ALU.mult, ALU.add),
                          reads=[(candA, None), (candB, None), msel],
                          writes=[(tns, ("c", t, hb0 + hoff))], label="oblend")
                    b0 = b1 + 1
    else:
        o_full_ap = io["o_full"].ap()
        for t in range(NCHB):
            dst = blocks[0][0][:, :].rearrange("p (h n) -> p h n", h=NH)[
                :, :, t * C:(t + 1) * C]
            pr.op("sync", lambda e, t=t, dst=dst: e.dma_start(
                out=dst, in_=o_full_ap[t * C:(t + 1) * C, :].rearrange(
                    "p (h n) -> p h n", h=NH)),
                  writes=[(blocks[0][0], ("c", t))], dma=True)

    def wslot(ocw, q):
        return (ocw * 4 + q) % 8

    if fstage < 3:
        for t in range(NCHB):
            ob = x1s[0]
            pr.op("vector", lambda e, t=t, ob=ob: e.tensor_copy(
                ob[:, 0:512], blk_ap(0, t * C, (t + 1) * C).bitcast(F32)[:, 0:64].broadcast_to((C, 512)) if False else ob[:, 0:512]),
                  reads=[], writes=[(ob, None)]) if False else None
            pr.op("sync", lambda e, t=t: e.dma_start(
                out=y[t * C:(t + 1) * C, 0:512],
                in_=blk_ap(0, t * C, (t + 1) * C).bitcast(F32)[:, 0:64]) if False else e.dma_start(
                out=y[t * C:(t + 1) * C, 0:1024].bitcast(BF16)[:, 0:1024] if False else y[t * C:(t + 1) * C, 0:512],
                in_=x_own[t * C:(t + 1) * C, 0:512]),
                  reads=[(blocks[0][0], ("c", t))], writes=[(y, t)], dma=True)
        pr.op("gpsimd", lambda e: e.nop(), reads=[(y, None)])
        return io

    for ocw in range(8):
        for q in range(4):
            sl = wring[wslot(ocw, q)]
            un = wring_units[wslot(ocw, q)]
            pr.op("gpsimd", lambda e, sl=sl, ocw=ocw, q=q: e.dma_start(
                out=sl.rearrange("p (a n) -> p a n", a=8),
                in_=wo[q * 8 * C:(q + 1) * 8 * C,
                       ocw * 512:(ocw + 1) * 512].rearrange("(a p) n -> p a n", p=C)),
                  writes=[(un, None)], dma=True)
        for hd in range(NH):
            sl = wring[wslot(ocw, hd // 8)]
            un = wring_units[wslot(ocw, hd // 8)]
            for t in range(NCHB):
                mm(PS[t][:, :], PS[t], blk_ap(hd, t * C, (t + 1) * C),
                   sl[:, (hd % 8) * 512:(hd % 8 + 1) * 512],
                   [(blocks[hd][0], ("c", t, hd) if fused else ("c", t)),
                    (un, None)], hd == 0, hd == NH - 1, "womm")
        for t in range(NCHB):
            x1b = x1s[t % 2]
            xs = xsl[t % 2]
            pr.op("sync", lambda e, t=t, ocw=ocw, xs=xs: e.dma_start(
                out=xs[:, 0:512], in_=x_own[t * C:(t + 1) * C, ocw * 512:(ocw + 1) * 512]),
                  writes=[(xs, None)], dma=True)
            pr.op("vector", lambda e, t=t, x1b=x1b, xs=xs: e.tensor_tensor(
                x1b[:, 0:512], PS[t][:, :], xs[:, 0:512], ALU.add),
                  reads=[PS[t], (xs, None)], writes=[(x1b, None)])
            pr.op("vector", lambda e, t=t, ocw=ocw, x1b=x1b: e.bn_stats(
                statsAll[:, t * 48 + ocw * 6: t * 48 + (ocw + 1) * 6], x1b[:, 0:512]),
                  reads=[(x1b, None)], writes=[("statsAll", (t, ocw))])
            pr.op("sync", lambda e, t=t, ocw=ocw, x1b=x1b: e.dma_start(
                out=x1[t * C:(t + 1) * C, ocw * 512:(ocw + 1) * 512], in_=x1b[:, 0:512]),
                  reads=[(x1b, None)], writes=[(x1, (t, ocw))], dma=True)

    if fstage < 4:
        pr.op("gpsimd", lambda e: e.nop(), reads=[(x1, None)])
        io["y"] = y
        for t in range(NCHB):
            xs = xsl[0]
            pr.op("sync", lambda e, t=t, xs=xs: e.dma_start(
                out=xs[:, 0:512], in_=x1[t * C:(t + 1) * C, 0:512]),
                  reads=[(x1, None)], writes=[(xs, None)], dma=True)
            pr.op("sync", lambda e, t=t, xs=xs: e.dma_start(
                out=y[t * C:(t + 1) * C, 0:512], in_=xs[:, 0:512]),
                  reads=[(xs, None)], writes=[(y, t)], dma=True)
        pr.op("gpsimd", lambda e: e.nop(), reads=[(y, None)])
        return io

    pr.op("vector", lambda e: e.engine_nop(),
          writes=[(blocks[b][0], None) for b in (0, 16, 24)] +
                 [(t, None) for t in xn2tm] + [(x1r0, None)])

    for t in range(NCHB):
        xr = x1r0
        nb = xn2tm[t % 2]
        pr.op("sync", lambda e, t=t, xr=xr: e.dma_start(
            out=xr[:, :], in_=x1[t * C:(t + 1) * C, :]),
              reads=[(x1, None)], writes=[(xr, None)], dma=True)
        pr.op("vector", lambda e, t=t: e.bn_aggr(
            mv[:], statsAll[:, t * 48:(t + 1) * 48]),
              reads=[("statsAll", None)], writes=[mv])
        pr.op("scalar", lambda e: e.activation(rstd[:], mv[:, 1:2], AF.Ln,
                                               bias=c_eps2[:, 0:1]),
              reads=[mv, c_eps2], writes=[(rstd, "ln")])
        pr.op("scalar", lambda e: e.activation(rstd[:], rstd[:], AF.Exp, scale=-0.5),
              reads=[(rstd, "ln")], writes=[rstd])
        pr.op("vector", lambda e: e.tensor_scalar(nmean[:], mv[:, 0:1], -1.0, None,
                                                  ALU.mult),
              reads=[mv], writes=[nmean])
        pr.op("vector", lambda e, xr=xr, nb=nb: e.tensor_scalar(
            nb[:, :], xr[:, :], nmean[:, 0:1], rstd[:, 0:1], ALU.add, ALU.mult),
              reads=[(xr, None), nmean, rstd], writes=[(nb, None)])
        for dc in range(DC):
            if DMAT:
                q = "sync"
                pr.op(q, lambda e, nb=nb, dc=dc, t=t: e.dma_start(
                    out=blk_ap(dc, t * C, (t + 1) * C), in_=nb[:, dc * C:(dc + 1) * C],
                    transpose=True),
                      reads=[(nb, None)], writes=[(blocks[dc][0], ("x", dc, t))],
                      dma=True, label="xn2T")
            else:
                bk = PS[6 + (dc % 2)]
                pv = bk[:, 0:64].bitcast(BF16)
                pr.op("tensor", lambda e, nb=nb, dc=dc, pv=pv: e.transpose(
                    pv, nb[:, dc * C:(dc + 1) * C], c_idn2[:]),
                      reads=[(nb, None), c_idn2], writes=[bk], label="xn2T")
                pr.op("scalar", lambda e, dc=dc, t=t, pv=pv: e.copy(
                    blk_ap(dc, t * C, (t + 1) * C), pv),
                      reads=[bk], writes=[(blocks[dc][0], ("x", dc, t))])

    def xn2read(dc, t):
        return (blocks[dc][0], ("x", dc, t))

    for t in range(NCHB):
        for dc in range(DC):
            mm(PS[t][:, 0:136], PS[t], blk_ap(dc, t * C, (t + 1) * C),
               wrab_ap(dc),
               [xn2read(dc, t), wrab_unit(dc)], dc == 0, dc == DC - 1, "ramm")
    pr.op("vector", lambda e: e.engine_nop(),
          writes=[(hsb_t, None), (hwf_t, None)])
    for t in range(NCHB):
        pr.op("vector", lambda e, t=t: e.tensor_copy(
            logits_all[:, t * 8:(t + 1) * 8], PS[t][:, 0:8]),
              reads=[PS[t]], writes=[(logits_all, t)])
        pr.op("vector", lambda e, t=t: e.tensor_copy(
            hsb_t[:, t * C:(t + 1) * C], PS[t][:, 8:136]),
              reads=[PS[t]], writes=[(hsb_t, ("h", t))])
    for t in range(NCHB):
        pr.op("vector", lambda e, t=t: e.tensor_reduce(
            pmax[:], logits_all[:, t * 8:(t + 1) * 8], mybir.AxisListType.X, ALU.max),
              reads=[(logits_all, t)], writes=[pmax])
        pr.op("vector", lambda e, t=t: e.tensor_scalar(
            probs[:], logits_all[:, t * 8:(t + 1) * 8], pmax[:, 0:1],
            None, ALU.subtract),
              reads=[(logits_all, t), pmax], writes=[(probs, "s")])
        pr.op("scalar", lambda e: e.activation(probs[:], probs[:], AF.Exp,
                                               accum_out=psum_r[:, 0:1]),
              reads=[(probs, "s")], writes=[(probs, "e"), psum_r])
        pr.op("vector", lambda e: e.reciprocal(pmax[:], psum_r[:]),
              reads=[psum_r], writes=[(pmax, "r")])
        pr.op("vector", lambda e: e.tensor_scalar(probs[:], probs[:], pmax[:, 0:1],
                                                  None, ALU.mult),
              reads=[(probs, "e"), (pmax, "r")], writes=[(probs, None)])
        pr.op("vector", lambda e: e.tensor_reduce(pmax[:], probs[:],
                                                  mybir.AxisListType.X, ALU.max),
              reads=[(probs, None)], writes=[(pmax, "m1")])
        pr.op("vector", lambda e: e.tensor_scalar(oh1[:], probs[:], pmax[:, 0:1], None,
                                                  ALU.is_ge),
              reads=[(probs, None), (pmax, "m1")], writes=[oh1])
        pr.op("vector", lambda e: e.scalar_tensor_tensor(probs2[:], oh1[:], -2.0,
                                                         probs[:], ALU.mult, ALU.add),
              reads=[oh1, (probs, None)], writes=[probs2])
        pr.op("vector", lambda e: e.tensor_reduce(m2c[:], probs2[:],
                                                  mybir.AxisListType.X, ALU.max),
              reads=[probs2], writes=[m2c])
        pr.op("vector", lambda e: e.tensor_scalar(oh2[:], probs2[:], m2c[:, 0:1], None,
                                                  ALU.is_ge),
              reads=[probs2, m2c], writes=[oh2])
        pr.op("vector", lambda e: e.tensor_tensor(den[:], pmax[:], m2c[:], ALU.add),
              reads=[(pmax, "m1"), m2c], writes=[den])
        pr.op("vector", lambda e: e.reciprocal(rden[:], den[:]),
              reads=[den], writes=[rden])
        pr.op("vector", lambda e: e.tensor_tensor(w1c[:], pmax[:], rden[:], ALU.mult),
              reads=[(pmax, "m1"), rden], writes=[w1c])
        pr.op("vector", lambda e: e.tensor_tensor(w2c[:], m2c[:], rden[:], ALU.mult),
              reads=[m2c, rden], writes=[w2c])
        pr.op("vector", lambda e: e.tensor_scalar(oh1[:], oh1[:], w1c[:, 0:1], None,
                                                  ALU.mult),
              reads=[(oh1, None), w1c], writes=[(oh1, "w")])
        pr.op("vector", lambda e: e.scalar_tensor_tensor(cmb[:], oh2[:], w2c[:, 0:1],
                                                         oh1[:], ALU.mult, ALU.add),
              reads=[(oh2, None), w2c, (oh1, "w")], writes=[cmb])
        cmb3 = cmb[:, :].unsqueeze(2).broadcast_to((C, 8, 16))
        hwb_sl = hsb_t[:, 1024 + t * C: 1024 + (t + 1) * C]
        pr.op("vector", lambda e, t=t, cmb3=cmb3, hwb_sl=hwb_sl: e.scalar_tensor_tensor(
            hwb_sl.rearrange("p (a b) -> p a b", a=8),
            hsb_t[:, t * C:(t + 1) * C].rearrange("p (a b) -> p a b", a=8), 2.0,
            cmb3, ALU.mult, ALU.mult),
              reads=[(hsb_t, ("h", t)), cmb], writes=[(hsb_t, ("hw", t))])

    for t in range(NCHB):
        hwb_sl = hsb_t[:, 1024 + t * C: 1024 + (t + 1) * C]
        bk = PS[6 + t % 2]
        pv = bk[:, 64:128].bitcast(BF16)
        pr.op("tensor", lambda e, pv=pv, hwb_sl=hwb_sl: e.transpose(
            pv, hwb_sl, c_idn2[:]),
              reads=[(hsb_t, ("hw", t)), c_idn2], writes=[bk], label="hwT")
        pr.op("scalar", lambda e, t=t, pv=pv: e.copy(
            hwf_t[:, t * C:(t + 1) * C], pv), reads=[bk], writes=[(hwf_t, ("w", t))])

    for fcw in range(8):
        for q in range(4):
            sl = wring[wslot(fcw, q)]
            un = wring_units[wslot(fcw, q)]
            pr.op("gpsimd", lambda e, sl=sl, fcw=fcw, q=q: e.dma_start(
                out=sl.rearrange("p (a n) -> p a n", a=8),
                in_=wbT[q * 8 * C:(q + 1) * 8 * C,
                        fcw * 512:(fcw + 1) * 512].rearrange("(a p) n -> p a n", p=C)),
                  writes=[(un, None)], dma=True)
        for dc in range(DC):
            sl = wring[wslot(fcw, dc // 8)]
            un = wring_units[wslot(fcw, dc // 8)]
            for t in range(NCHB):
                mm(PS[t][:, :], PS[t], blk_ap(dc, t * C, (t + 1) * C),
                   sl[:, (dc % 8) * 512:(dc % 8 + 1) * 512],
                   [xn2read(dc, t), (un, None)], dc == 0, False, "bmm")
        if bbr_per_fcw:
            pr.op("gpsimd", lambda e, fcw=fcw: e.dma_start(
                out=bbr[0:1, 0:512], in_=bbias[0:1, fcw * 512:(fcw + 1) * 512]),
                  writes=[bbr_u], dma=True)
        for t in range(NCHB):
            mm(PS[t][:, :], PS[t], hwf_t[:, t * C:(t + 1) * C],
               bcb[:, fcw * 512:(fcw + 1) * 512],
               [(hwf_t, ("w", t)), bcb_unit], False, False, "emm")
            bslice = bbr[0:1, 0:512] if bbr_per_fcw else bbr[0:1, fcw * 512:(fcw + 1) * 512]
            mm(PS[t][:, :], PS[t], c_one2[0:1, 0:C], bslice,
               [c_one2_u, bbr_u], False, True, "biasmm")
        for t in range(NCHB):
            xs = xsl[t % 2]
            pr.op("sync", lambda e, t=t, fcw=fcw, xs=xs: e.dma_start(
                out=xs[:, 0:512], in_=x1[t * C:(t + 1) * C, fcw * 512:(fcw + 1) * 512]),
                  reads=[(x1, None)], writes=[(xs, None)], dma=True)
            ob = outs[t % 2]
            pr.op("vector", lambda e, t=t, ob=ob, xs=xs: e.tensor_tensor(
                ob[:, 0:512], PS[t][:, :], xs[:, 0:512], ALU.add),
                  reads=[PS[t], (xs, None)], writes=[(ob, None)])
            pr.op("scalar", lambda e, t=t, fcw=fcw, ob=ob: e.dma_start(
                out=y[t * C:(t + 1) * C, fcw * 512:(fcw + 1) * 512], in_=ob[:, 0:512]),
                  reads=[(ob, None)], writes=[(y, (t, fcw))], dma=True)
    return io


def build_fused(nc, pr, T=2048, SCH=4):
    import os
    fstage = int(os.environ.get("FSTAGE", "9"))
    NCH = T // C
    half_rows = (NCH // 2) * C
    qr = (NCH // 4) * C
    cc_outA = nc.dram_tensor("cc_outA", [2 * half_rows, NQ * C], BF16, kind="Internal")
    cc_outB1 = nc.dram_tensor("cc_outB1", [2 * qr, NQ * C], BF16, kind="Internal")
    cc_outB2 = nc.dram_tensor("cc_outB2", [2 * qr, NQ * C], BF16, kind="Internal")

    def emit_coll(osap, rows, out_t):
        def coll(e):
            return e.collective_compute(
                "AllGather", mybir.AluOpType.bypass,
                replica_groups=[[0, 1], [2, 3], [4, 5], [6, 7]],
                ins=[osap[rows[0]:rows[1], :].opt()],
                outs=[out_t.ap()[:, :].opt()])
        return coll

    def coll_hook(pr_, o_send_h, after_chunk):
        osap = o_send_h.ap()
        if after_chunk == NCH // 2 - 1:
            pr_.op("gpsimd", emit_coll(osap, (0, half_rows), cc_outA),
                   reads=[(o_send_h, None)], writes=[(cc_outA, None)],
                   private=True, label="agA")
        elif after_chunk == 3 * NCH // 4 - 1:
            pr_.op("gpsimd", emit_coll(osap, (half_rows, half_rows + qr), cc_outB1),
                   reads=[(o_send_h, None)], writes=[(cc_outB1, None)],
                   private=True, label="agB1")
        elif after_chunk == NCH - 1:
            pr_.op("gpsimd", emit_coll(osap, (half_rows + qr, 2 * half_rows), cc_outB2),
                   reads=[(o_send_h, None)], writes=[(cc_outB2, None)],
                   private=True, label="agB2")

    io_a = build_phase_a(nc, pr, T=T, SCH=SCH, fused=True, coll_hook=coll_hook)
    pool = io_a["pool"]
    o_send = io_a["o_send"]
    cc_out = (cc_outA, cc_outB1, cc_outB2)

    if fstage < 2:
        dbg = nc.dram_tensor("y", [T // 2, HID], F32, kind="ExternalOutput").ap()
        sb = nc.alloc_sbuf_tensor("dbgsb", [C, 512], F32)
        pr.op("sync", lambda e: e.dma_start(out=sb[:, 0:512],
                                            in_=cc_out.ap()[0:C, 0:512]),
              reads=[(cc_out, None)], writes=[sb], dma=True)
        pr.op("sync", lambda e: e.dma_start(out=dbg[0:C, 0:512], in_=sb[:, 0:512]),
              reads=[sb], writes=[("yo", None)], dma=True)
        pr.op("gpsimd", lambda e: e.nop(), reads=[("yo", None)])
        return io_a, {"y": dbg}
    io_b = build_phase_b(nc, pr, NTOK=T // 2, pool=pool, o_parts=(o_send, cc_out),
                         fstage=fstage)
    pr.op("gpsimd", lambda e: e.nop(), reads=[io_b["y"]])
    return io_a, io_b


import concourse.bass as _bass
from concourse.bass_utils import run_bass_kernel_spmd as _run_spmd

_CACHE = {}


def _build_fused():
    if "f" not in _CACHE:
        nc = _bass.Bass("TRN2", target_bir_lowering=False, debug=False)
        pr = Prog(nc)
        build_fused(nc, pr, T=2048, SCH=4)
        pr.finalize_and_emit()
        _CACHE["f"] = nc
    return _CACHE["f"]


def kernel(L0_attn_hidden_states, L0_attn_Wq, L0_attn_Wk, L0_attn_Wv, L0_attn_Wo,
           L0_ffn_W_base, L0_ffn_b_base, L0_ffn_W_router, L0_ffn_A, L0_ffn_B):
    x = np.ascontiguousarray(np.asarray(L0_attn_hidden_states, dtype=np.float32))
    Wq = np.asarray(L0_attn_Wq, dtype=np.float32)
    Wk = np.asarray(L0_attn_Wk, dtype=np.float32)
    Wv = np.asarray(L0_attn_Wv, dtype=np.float32)
    Wo = np.ascontiguousarray(np.asarray(L0_attn_Wo, dtype=np.float32))
    Wb = np.asarray(L0_ffn_W_base, dtype=np.float32)
    bb = np.asarray(L0_ffn_b_base, dtype=np.float32)
    Wr = np.asarray(L0_ffn_W_router, dtype=np.float32)
    Af = np.asarray(L0_ffn_A, dtype=np.float32)
    Bf = np.asarray(L0_ffn_B, dtype=np.float32)

    cons = host_constants()
    cosf, sinf = rope_tables_fm(2048)
    wra = np.ascontiguousarray(np.concatenate(
        [Wr.T, Af.reshape(128, HID).T], axis=1).astype(np.float32))
    bcc = np.ascontiguousarray(Bf.transpose(0, 2, 1).reshape(128, HID).astype(np.float32))
    shared = {"cosf": cosf, "sinf": sinf, "wo": Wo,
              "wbT": np.ascontiguousarray(Wb.T), "wra": wra, "bc": bcc,
              "bbias": np.ascontiguousarray(bb[None, :]),
              "onerow": np.ones((1, 128), np.float32)}
    for k in ["triUs", "oness", "sixcol", "trimask", "maskA", "maskB", "mrot",
              "idn", "twocol", "epscol"]:
        shared[k] = cons[k]

    in_maps = []
    for c in range(8):
        bi, hh = c >> 1, c & 1
        m = dict(shared)
        m["x"] = np.ascontiguousarray(x[bi])
        m["wq"] = np.ascontiguousarray(Wq[:, hh * 2048:(hh + 1) * 2048])
        m["wkv"] = np.ascontiguousarray(
            np.concatenate([Wk[:, hh * 512:(hh + 1) * 512],
                            Wv[:, hh * 512:(hh + 1) * 512]], axis=1))
        m["x_own"] = np.ascontiguousarray(x[bi, hh * 1024:(hh + 1) * 1024])
        ms = np.zeros((128, 2), np.float32)
        ms[:, 0] = 1.0 if hh == 0 else 0.0
        ms[:, 1] = 1.0 if hh == 1 else 0.0
        m["msel"] = ms
        in_maps.append(m)

    res = _run_spmd(_build_fused(), in_maps, core_ids=list(range(8)))

    out = np.zeros((4, 2048, 4096), np.float32)
    for c in range(8):
        bi, hh = c >> 1, c & 1
        out[bi, hh * 1024:(hh + 1) * 1024] = np.asarray(res.results[c]["y"])
    return out
